# version 06
# baseline (speedup 1.0000x reference)
import numpy as np
from contextlib import ExitStack
from types import SimpleNamespace

import concourse.bass as bass
import concourse.mybir as mybir
from concourse.bass_utils import run_bass_kernel_spmd

F32 = mybir.dt.float32
BF16 = mybir.dt.bfloat16
AF = mybir.ActivationFunctionType
ALU = mybir.AluOpType
AX = mybir.AxisListType

B = 4
TF = 4096
TO = 2048
D = 1024
NHL = 8
FH = NHL * 64
DK = 64
SEC = 512
CHUNK = 128
NCH = SEC // CHUNK
NB32 = SEC // 32
NB8 = SEC // 8
EPS_RMS = 1e-6
EPS_GN = 1e-5
EPS_LN = 1e-5
DCLAMP = 40.0
MOE_NE = 16


class Ctx:
    def __init__(self, nc, es):
        self.nc = nc
        self.sem, self.cnt = {}, {}
        for e in ("t", "v", "s", "g"):
            self.sem[e] = es.enter_context(nc.semaphore("tick_" + e))
            self.cnt[e] = 0
        self.dsem, self.dcnt = {}, {}
        self.pool, self.rr = {}, {}
        for ch, n in (("lq", 4), ("lg", 4), ("sq", 2), ("cc", 1)):
            names = []
            for i in range(n):
                nm = f"dma_{ch}{i}"
                self.dsem[nm] = es.enter_context(nc.semaphore(nm))
                self.dcnt[nm] = 0
                names.append(nm)
            self.pool[ch] = names
            self.rr[ch] = 0
        self.waited = {}

    def inc(self, e, inst):
        self.cnt[e] += 1
        inst.then_inc(self.sem[e], 1)
        return self.cnt[e]

    def dgroup(self, eng, consumer, pairs, ch):
        nm = self.pool[ch][self.rr[ch] % len(self.pool[ch])]
        self.rr[ch] += 1
        self.wait(eng, consumer, nm, self.dcnt[nm])
        for out, in_ in pairs:
            self.dcnt[nm] += 16
            eng.dma_start(out=out, in_=in_).then_inc(self.dsem[nm], 16)
        return (nm, self.dcnt[nm])

    def ccg(self, g, kind, ins, outs, groups):
        nm = self.pool["cc"][0]
        self.dcnt[nm] += 1
        g.collective_compute(kind, ALU.bypass, replica_groups=groups,
                             ins=ins, outs=outs).then_inc(self.dsem[nm], 1)
        return (nm, self.dcnt[nm])

    def dwait(self, eng, consumer, key):
        if key is None:
            return
        self.wait(eng, consumer, key[0], key[1])

    def ch(self, ename, eng):
        return ChainEng(self, ename, eng)

    def wait(self, eng, consumer, sname, val):
        if val is None or val <= 0:
            return
        key = (consumer, sname)
        if self.waited.get(key, 0) >= val:
            return
        sem = self.sem[sname] if sname in self.sem else self.dsem[sname]
        eng.wait_ge(sem, val)
        self.waited[key] = val


_CHAIN_OPS = {
    "tensor_copy", "tensor_tensor", "tensor_add", "tensor_sub", "tensor_mul",
    "tensor_max", "tensor_scalar", "tensor_single_scalar", "tensor_scalar_add",
    "tensor_scalar_sub", "tensor_scalar_mul", "tensor_scalar_max",
    "tensor_scalar_min", "tensor_relu", "tensor_reduce", "reduce_max",
    "reduce_sum", "scalar_tensor_tensor", "tensor_tensor_scan", "engine_nop",
    "memset", "memzero", "reciprocal", "select", "copy_predicated",
    "transpose", "bn_stats", "bn_aggr", "max", "max_index", "stream_shuffle",
    "activation", "sign", "square", "sqrt", "mul", "copy", "add", "iota",
    "affine_select", "partition_broadcast", "partition_all_reduce",
}


class ChainEng:

    def __init__(self, ctx, ename, eng):
        self._ctx, self._e, self._raw = ctx, ename, eng

    def __getattr__(self, name):
        f = getattr(self._raw, name)
        if name not in _CHAIN_OPS:
            return f
        ctx, e, raw = self._ctx, self._e, self._raw

        def g(*a, **k):
            ctx.wait(raw, e, e, ctx.cnt[e])
            inst = f(*a, **k)
            ctx.inc(e, inst)
            return inst
        return g


class PBank:

    def __init__(self, ctx, es, name):
        self.ctx = ctx
        self.h = es.enter_context(ctx.nc.psum_tensor(name, [128, 512], F32))
        self.last_w = 0
        self.reads = {}

    def pre(self, t):
        self.ctx.wait(t, "t", "t", self.last_w)
        for e, v in self.reads.items():
            self.ctx.wait(t, "t", e, v)
        self.reads = {}

    def mm(self, t, out_ap, lhsT, rhs, start, stop, tile_position=None):
        self.pre(t)
        kw = {}
        if tile_position is not None:
            kw["tile_position"] = tile_position
        inst = t.matmul(out_ap, lhsT, rhs, start=start, stop=stop, **kw)
        self.last_w = self.ctx.inc("t", inst)
        return inst

    def tr(self, t, out_ap, in_ap, ident):
        self.pre(t)
        inst = t.transpose(out_ap, in_ap, ident)
        self.last_w = self.ctx.inc("t", inst)
        return inst

    def rd(self, e):
        self.reads[e] = self.ctx.cnt[e]


def bc_blocks(ap2d, base, bs, nblk, rep=None):
    rep = bs if rep is None else rep
    full = ap2d[:, :]
    pitch = full.ap[0][0]
    return bass.AP(full.tensor, full.offset + base,
                   [[pitch, 128], [bs, nblk], [0, rep]])


def cols(ap2d, base, stride, n):
    full = ap2d[:, :]
    pitch = full.ap[0][0]
    return bass.AP(full.tensor, full.offset + base, [[pitch, 128], [stride, n]])


def grp(ap2d, off, bs, nblk):
    return ap2d[:, off:off + bs * nblk].rearrange("p (b i) -> p b i", i=bs)


def build_program(taps=(), n_sec=TF // SEC, full=True, stop_after=None,
                  dbg_rope=False):
    nc = bass.Bass("TRN2", target_bir_lowering=False, debug=False)
    es = ExitStack()
    ctx = Ctx(nc, es)
    ns = SimpleNamespace(nc=nc, es=es, ctx=ctx, n_sec=n_sec, full=full,
                         stop_after=stop_after)
    ns.dbg_rope = dbg_rope

    def din(name, shape, dtype=F32):
        ap = nc.dram_tensor(name, list(shape), dtype, kind="ExternalInput").ap()
        setattr(ns, "I_" + name, ap)

    din("xf", (D, TF))
    din("xo", (D, TO))
    for nm in ("wr", "wk", "wv", "wg"):
        din(nm, (D, FH))
    din("wo", (D, D))
    din("w1", (D, 4096))
    din("w2", (4096, D))
    din("wqt", (D, D))
    din("l1wq", (D, FH))
    din("l1wk", (D, FH))
    din("l1wv", (D, FH))
    din("l1wo", (D, D))
    din("l1bq", (128, 4))
    din("l1bk", (128, 4))
    din("l1bv_row", (1, FH))
    din("l1bo", (128, 8))
    din("rope_perm", (128, 128))
    din("ropesin", (128, TF))
    din("ropecos", (128, TF))
    din("cmask", (128, 4 * 512), BF16)
    din("gate_wt", (D, 16))
    din("bias_e", (1, 16))
    din("moe_up", (MOE_NE * D, 512))
    din("moe_dn", (MOE_NE * 512, D))
    din("sh_upt", (D, 2048))
    din("sh_dnt", (2048, D))
    din("wla", (D, 64))
    din("wlb", (64, FH))
    for nm in ("mu_r", "mu_k", "mu_v", "mu_w", "mu_g", "ln1w", "ln2w"):
        din(nm, (128, 8))
    for nm in ("w0", "uu", "gnw", "gnb"):
        din(nm, (128, 4))
    din("ident", (128, 128), BF16)
    din("identf", (128, 128))
    din("inde16", (16, 16 * 128), BF16)
    din("indh", (128, 2))
    din("indhb", (128, 2), BF16)
    din("indb", (8, 512), BF16)
    din("indbf", (8, 512))
    din("indbt", (128, 32), BF16)
    din("indbtf", (128, 32))
    din("md8", (128, 128), BF16)
    din("m8", (128, 128), BF16)
    din("m16", (128, 128), BF16)
    din("mstr", (128, 128), BF16)
    din("bmask", (8, 3 * SEC), BF16)
    din("onesc", (1, 128))
    din("onesr", (128, 1))
    din("mab", (128, 2))

    ns.OUT = nc.dram_tensor("out", [D, TO], F32, kind="ExternalOutput").ap()
    ns.taps = {}
    for name, shape, dtype in taps:
        ns.taps[name] = nc.dram_tensor(name, list(shape), dtype,
                                       kind="ExternalOutput").ap()

    ns.ORW_B = nc.dram_tensor("orw_b", [FH, TF], BF16).ap()
    ns.ORW_F = nc.dram_tensor("orw_f", [2 * FH, TF], BF16).ap()
    ns.X2B = nc.dram_tensor("x2b", [D, TO], BF16).ap()
    ns.X2F = nc.dram_tensor("x2f", [2 * D, TO], BF16).ap()
    ns.OAT_B = nc.dram_tensor("oat_b", [FH, TF], BF16).ap()
    ns.OAT_F = nc.dram_tensor("oat_f", [2 * FH, TF], BF16).ap()
    if dbg_rope:
        ns.DBG_SC = nc.dram_tensor("dbg_sc", [128, 2 * 512], F32,
                                   kind="ExternalOutput").ap()
        ns.DBG_QK = nc.dram_tensor("dbg_qk", [128, 8 * TF], BF16,
                                   kind="ExternalOutput").ap()

    def sb(name, shape, dtype=F32, stack=None):
        setattr(ns, name, (stack or es).enter_context(
            nc.sbuf_tensor("sb_" + name, shape, dtype)))
    ns.sb = sb

    sb("ident", (128, 128), BF16)
    sb("indh", (128, 2))
    sb("indhb", (128, 2), BF16)
    sb("indb", (8, 512), BF16)
    sb("indbf", (8, 512))
    sb("indbt", (128, 32), BF16)
    sb("indbtf", (128, 32))
    sb("md8", (128, 128), BF16)
    sb("m8", (128, 128), BF16)
    sb("m16", (128, 128), BF16)
    sb("mstr", (128, 128), BF16)
    sb("bmask", (8, 3 * SEC), BF16)
    sb("onesc", (1, 128))
    sb("onesr", (128, 1))
    sb("mab", (128, 2))
    sb("zeros64", (64, 128), BF16)
    for nm in ("mu_r", "mu_k", "mu_v", "mu_w", "mu_g", "ln1w", "ln2w"):
        sb("v_" + nm, (128, 8))
    for nm in ("w0", "uu", "gnw", "gnb"):
        sb("v_" + nm, (128, 4))
    for m in ("r", "k", "v", "w", "g"):
        sb("omu_" + m, (128, 8))

    mixer_es = ExitStack()
    ns.mixer_es = mixer_es
    def sbm(name, shape, dtype=F32):
        ns.sb(name, shape, dtype, stack=mixer_es)
    sb = sbm
    sb("xs", (128, 8 * SEC))
    sb("hb", (128, 8 * (SEC + 1)), BF16)
    sb("mixb", (128, 8 * SEC), BF16)
    sb("sqb", (128, SEC))
    sb("wbA", (128, 8 * FH), BF16)
    sb("wbB", (128, 8 * FH), BF16)
    sb("wlab", (128, 8 * 64), BF16)
    sb("wlbb", (64, FH), BF16)
    sb("rb", (128, 4 * SEC), BF16)
    sb("kb", (128, 4 * (SEC + 3)), BF16)
    sb("vb", (128, 4 * (SEC + 3)), BF16)
    sb("gb", (128, 4 * SEC), BF16)
    sb("wfp", (128, 4 * SEC))
    sb("ewb", (128, 4 * SEC))
    sb("csb", (128, 4 * (SEC + 3)))
    sb("cscar", (128, 4))
    sb("vtm", (128, NCH * FH), BF16)
    sb("khtm", (128, NCH * FH), BF16)
    FAMS = ("r32", "kr32_1", "kr32_2", "kr32_3", "r16", "kr16_1", "r8",
            "kr8_1", "kd8", "rd8", "khat", "rchunk")
    ns.FAMS = FAMS
    for f in FAMS:
        sb("fam_" + f, (128, 4 * SEC), BF16)
    sb("b2f", (128, 4 * SEC), BF16)
    sb("b3f", (128, 4 * SEC), BF16)
    sb("arg1", (128, SEC))
    sb("arg2", (128, SEC))
    sb("ef32", (128, 4 * 17), BF16)
    sb("efh8", (128, 4 * NB8), BF16)
    sb("efh8b", (128, 4 * NB8), BF16)
    sb("f16k", (128, 4 * NB8), BF16)
    sb("f16r", (128, 4 * NB8), BF16)
    sb("fchk", (128, 4 * NB32), BF16)
    sb("fchkr", (128, 4 * NB32), BF16)
    sb("etot", (128, 4 * NCH))
    sb("prodb", (128, 4 * SEC), BF16)
    for d_ in range(4):
        sb("browm%d" % d_, (8, SEC), BF16)
    sb("mean8", (8, SEC))
    sb("rstd8", (8, SEC))
    sb("psb", (128, 4 * 128), BF16)
    sb("osec", (128, 4 * SEC))
    sb("omix", (128, 4 * SEC), BF16)
    sb("Ssb", (128, 4 * 64))
    sb("Sbf", (128, 4 * 64), BF16)
    sb("srow", (8, SEC))
    sb("svar", (8, SEC))

    sb = ns.sb
    ns.bank = [PBank(ctx, es, f"bank{i}") for i in range(8)]
    ns.groups = [[0, 1], [2, 3], [4, 5], [6, 7]]
    ns.mix_free = 0
    ns.w_free = {"wr": 0, "wk": 0, "wv": 0, "wg": 0}
    ns.bfree = {}
    ns.omix_keys = []
    ns.w_load = {}
    ns.fam_free = 0
    ns.sec_free_v = 0
    ns.carry_v = 0

    with nc.Block() as block:
        ns.block = block
        emit_preamble(ns)
        for s in range(n_sec):
            emit_mixer_section(ns, s)
        emit_mixer_taps(ns)
        if getattr(ns, "full", True):
            emit_phase_join(ns)
            mixer_es.close()
            emit_l0post(ns)
            if ns.stop_after not in ("wo", "ffn", "x1", "x2"):
                emit_l1attn(ns)
            if ns.stop_after not in ("wo", "ffn", "x1", "x2", "attn"):
                emit_moe(ns)
        emit_finish(ns)

    es.close()
    return nc, ns


def emit_preamble(ns):
    ctx, block = ns.ctx, ns.block

    @block.sync
    def _(q):
        pairs = [(ns.ident, ns.I_ident), (ns.indh, ns.I_indh),
                 (ns.indhb, ns.I_indhb), (ns.indb, ns.I_indb),
                 (ns.md8, ns.I_md8), (ns.m8, ns.I_m8), (ns.m16, ns.I_m16),
                 (ns.mstr, ns.I_mstr),
                 (ns.bmask, ns.I_bmask), (ns.onesc, ns.I_onesc),
                 (ns.indbf, ns.I_indbf),
                 (ns.indbt, ns.I_indbt), (ns.indbtf, ns.I_indbtf),
                 (ns.onesr, ns.I_onesr), (ns.mab, ns.I_mab)]
        for nm in ("mu_r", "mu_k", "mu_v", "mu_w", "mu_g", "ln1w", "ln2w",
                   "w0", "uu", "gnw", "gnb"):
            pairs.append((getattr(ns, "v_" + nm), getattr(ns, "I_" + nm)))
        ns.c_pre = ctx.dgroup(q, "q", [(d[:, :], s_) for d, s_ in pairs], "lq")

    @block.gpsimd
    def _(g):
        g = ctx.ch("g", g)
        ns.c_lora = ctx.dgroup(g, "g", [
            (ns.wlab[:, :].rearrange("p (a c) -> p a c", c=64),
             ns.I_wla.rearrange("(a p) c -> p a c", p=128)),
            (ns.wlbb[:, :], ns.I_wlb)], "lg")

    @block.vector
    def _(v):
        v = ctx.ch("v", v)
        ctx.dwait(v, "v", ns.c_pre)
        for m, nm in (("r", "mu_r"), ("k", "mu_k"), ("v", "mu_v"),
                      ("w", "mu_w"), ("g", "mu_g")):
            v.tensor_scalar(out=getattr(ns, "omu_" + m)[:, :],
                            in0=getattr(ns, "v_" + nm)[:, :],
                            scalar1=-1.0, scalar2=1.0,
                            op0=ALU.mult, op1=ALU.add)
        v.memset(ns.csb[:, :], 0.0)
        v.memset(ns.cscar[:, :], 0.0)
        v.memset(ns.kb[:, :], 0.0)
        v.memset(ns.vb[:, :], 0.0)
        v.memset(ns.hb[:, :], 0.0)
        v.memset(ns.Ssb[:, :], 0.0)
        v.memset(ns.zeros64[:, :], 0.0)
        v.memset(ns.bank[0].h[:, :], 0.0)
        v.memset(ns.bank[1].h[:, :], 0.0)
        ns.bank[0].rd("v")
        ns.bank[1].rd("v")
        v.engine_nop()
    ns.v_pre = ctx.cnt["v"]


def emit_mixer_section(ns, s):
    ctx, block, bank = ns.ctx, ns.block, ns.bank
    xoff = 0
    P3 = SEC + 3
    c_xs = [None]

    def csp(mt):
        return ns.csb[:, mt * P3:(mt + 1) * P3]

    @block.sync
    def _(q):
        ctx.wait(q, "q", "v", getattr(ns, "xs_free", 0))
        c_xs[0] = ctx.dgroup(q, "q", [
            (ns.xs[:, xoff + p * SEC:xoff + (p + 1) * SEC],
             ns.I_xf[128 * p:128 * (p + 1), s * SEC:(s + 1) * SEC])
            for p in range(8)], "lq")

    WMAP = [("wr", ns.I_wr, ns.wbA), ("wk", ns.I_wk, ns.wbB),
            ("wv", ns.I_wv, ns.wbA), ("wg", ns.I_wg, ns.wbB)]

    @block.gpsimd
    def _(g):
        g = ctx.ch("g", g)
        if s == 0:
            ctx.dwait(g, "g", ns.c_lora)
        for nm, src, buf in WMAP[:2]:
            ctx.wait(g, "g", "t", ns.bfree.get(id(buf), 0))
            ns.w_load[nm] = ctx.dgroup(g, "g", [
                (buf[:, p * FH:(p + 1) * FH], src[128 * p:128 * (p + 1), :])
                for p in range(8)], "lg")

    for p in range(8):
        @block.scalar
        def _(sc, p=p):
            sc = ctx.ch("s", sc)
            ctx.dwait(sc, "s", ns.c_pre)
            ctx.dwait(sc, "s", c_xs[0])
            ctx.wait(sc, "s", "t", getattr(ns, "sq_guard", 0))
            if p > 0:
                ctx.wait(sc, "s", "t", ns.sq_mm)
            inst = sc.activation(out=ns.sqb[:, :],
                                 in_=ns.xs[:, xoff + p * SEC:xoff + (p + 1) * SEC],
                                 func=AF.Square)

        @block.tensor
        def _(t, p=p):
            ctx.dwait(t, "t", ns.c_pre)
            ctx.wait(t, "t", "s", ctx.cnt["s"])
            bank[4].mm(t, bank[4].h[0:1, 0:SEC], ns.onesr[:, :], ns.sqb[:, :],
                       start=(p == 0), stop=(p == 7))
            ns.sq_mm = ctx.cnt["t"]

    @block.vector
    def _(v):
        v = ctx.ch("v", v)
        ctx.wait(v, "v", "t", ns.sq_mm)
        v.tensor_scalar(out=ns.arg1[0:1, 0:SEC], in0=bank[4].h[0:1, 0:SEC],
                        scalar1=1.0 / D, scalar2=EPS_RMS,
                        op0=ALU.mult, op1=ALU.add)
        bank[4].rd("v")
        inst = v.reciprocal(out=ns.arg2[0:1, 0:SEC], in_=ns.arg1[0:1, 0:SEC])

    @block.scalar
    def _(sc):
        sc = ctx.ch("s", sc)
        ctx.wait(sc, "s", "v", ctx.cnt["v"])
        inst = sc.activation(out=ns.sqb[0:1, 0:SEC], in_=ns.arg2[0:1, 0:SEC], func=AF.Sqrt)
    s_rstd = ctx.cnt["s"]

    @block.tensor
    def _(t):
        ctx.wait(t, "t", "s", s_rstd)
        bank[5].mm(t, bank[5].h[:, 0:SEC], ns.onesc[:, :], ns.sqb[0:1, 0:SEC],
                   start=True, stop=True)
    t_rstdbc = ctx.cnt["t"]
    ns.sq_guard = t_rstdbc

    @block.vector
    def _(v):
        v = ctx.ch("v", v)
        ctx.wait(v, "v", "t", t_rstdbc)
        ctx.wait(v, "v", "v", ns.carry_v)
        for p in range(8):
            hp = ns.hb[:, p * (SEC + 1):(p + 1) * (SEC + 1)]
            inst = v.scalar_tensor_tensor(
                out=hp[:, 1:SEC + 1],
                in0=ns.xs[:, xoff + p * SEC:xoff + (p + 1) * SEC],
                scalar=ns.v_ln1w[:, p:p + 1], in1=bank[5].h[:, 0:SEC],
                op0=ALU.mult, op1=ALU.mult)
        bank[5].rd("v")
    v_h = ctx.cnt["v"]
    ns.xs_free = v_h

    def emit_mix(m):
        @block.vector
        def _(v):
            v = ctx.ch("v", v)
            ctx.wait(v, "v", "v", v_h)
            ctx.wait(v, "v", "t", ns.mix_free)
            for p in range(8):
                hp = ns.hb[:, p * (SEC + 1):(p + 1) * (SEC + 1)]
                mp = ns.mixb[:, p * SEC:(p + 1) * SEC]
                v.tensor_scalar(out=mp, in0=hp[:, 1:SEC + 1],
                                scalar1=getattr(ns, "omu_" + m)[:, p:p + 1],
                                scalar2=None, op0=ALU.mult)
            for p in range(8):
                hp = ns.hb[:, p * (SEC + 1):(p + 1) * (SEC + 1)]
                mp = ns.mixb[:, p * SEC:(p + 1) * SEC]
                inst = v.scalar_tensor_tensor(
                    out=mp, in0=hp[:, 0:SEC],
                    scalar=getattr(ns, "v_mu_" + m)[:, p:p + 1],
                    in1=mp, op0=ALU.mult, op1=ALU.add)
        return ctx.cnt["v"]

    def emit_proj(wsb, nm, vwait):
        @block.tensor
        def _(t):
            ctx.wait(t, "t", "v", vwait)
            ctx.dwait(t, "t", ns.w_load[nm])
            for k in range(8):
                for mt in range(4):
                    bank[mt].mm(
                        t, bank[mt].h[:, 0:SEC],
                        wsb[:, k * FH + mt * 128:k * FH + (mt + 1) * 128],
                        ns.mixb[:, k * SEC:(k + 1) * SEC],
                        start=(k == 0), stop=(k == 7))
            ns.mix_free = ctx.cnt["t"]
            ns.w_free[nm] = ctx.cnt["t"]
            ns.bfree[id(wsb)] = ctx.cnt["t"]

    vw = emit_mix("w")

    @block.tensor
    def _(t):
        ctx.wait(t, "t", "v", vw)
        ctx.dwait(t, "t", ns.c_lora)
        for k in range(8):
            bank[4].mm(t, bank[4].h[0:64, 0:SEC],
                       ns.wlab[:, k * 64:(k + 1) * 64],
                       ns.mixb[:, k * SEC:(k + 1) * SEC],
                       start=(k == 0), stop=(k == 7))

    @block.scalar
    def _(sc):
        sc = ctx.ch("s", sc)
        ctx.wait(sc, "s", "t", ctx.cnt["t"])
        inst = sc.activation(out=ns.prodb[0:64, 0:SEC],
                             in_=ns.bank[4].h[0:64, 0:SEC], func=AF.Tanh)
        bank[4].rd("s")
    s_tanh = ctx.cnt["s"]

    @block.tensor
    def _(t):
        ctx.wait(t, "t", "s", s_tanh)
        for mt in range(4):
            bank[4 + mt].mm(t, bank[4 + mt].h[:, 0:SEC],
                            ns.wlbb[:, mt * 128:(mt + 1) * 128],
                            ns.prodb[0:64, 0:SEC], start=True, stop=True)
        ns.mix_free = max(ns.mix_free, ctx.cnt["t"])
    @block.vector
    def _(v):
        v = ctx.ch("v", v)
        ctx.wait(v, "v", "t", ctx.cnt["t"])
        for mt in range(4):
            bk = bank[4 + mt]
            inst = v.tensor_scalar(out=ns.wfp[:, mt * SEC:(mt + 1) * SEC],
                                   in0=bk.h[:, 0:SEC],
                                   scalar1=ns.v_w0[:, mt:mt + 1], scalar2=None,
                                   op0=ALU.add)
            bk.rd("v")
    v_w = ctx.cnt["v"]

    @block.scalar
    def _(sc):
        sc = ctx.ch("s", sc)
        ctx.wait(sc, "s", "v", v_w)
        for mt in range(4):
            inst = sc.activation(out=ns.ewb[:, mt * SEC:(mt + 1) * SEC],
                                 in_=ns.wfp[:, mt * SEC:(mt + 1) * SEC],
                                 func=AF.Exp)
    s_ew = ctx.cnt["s"]

    @block.vector
    def _(v):
        v = ctx.ch("v", v)
        ctx.wait(v, "v", "s", s_ew)
        ctx.wait(v, "v", "v", ns.carry_v)
        for mt in range(4):
            cp = csp(mt)
            inst = v.tensor_tensor_scan(
                out=cp[:, 3:SEC + 3],
                data0=ns.ewb[:, mt * SEC:(mt + 1) * SEC],
                data1=ns.ewb[:, mt * SEC:(mt + 1) * SEC],
                initial=ns.cscar[:, mt:mt + 1], op0=ALU.add, op1=ALU.bypass)

    @block.vector
    def _(v):
        v = ctx.ch("v", v)
        ctx.wait(v, "v", "v", ctx.cnt["v"])
        for mt in range(4):
            inst = v.tensor_copy(out=ns.cscar[:, mt:mt + 1],
                                 in_=csp(mt)[:, SEC + 2:SEC + 3])
    v_cs = ctx.cnt["v"]

    def evac_plain(dst):
        @block.vector
        def _(v):
            v = ctx.ch("v", v)
            ctx.wait(v, "v", "t", ctx.cnt["t"])
            for mt in range(4):
                inst = v.tensor_copy(out=dst[:, mt * SEC:(mt + 1) * SEC],
                                     in_=bank[mt].h[:, 0:SEC])
                bank[mt].rd("v")

    def evac_shift(dst):
        @block.vector
        def _(v):
            v = ctx.ch("v", v)
            ctx.wait(v, "v", "t", ctx.cnt["t"])
            ctx.wait(v, "v", "v", ns.carry_v)
            for mt in range(4):
                dp = dst[:, mt * P3:(mt + 1) * P3]
                inst = v.tensor_copy(out=dp[:, 3:SEC + 3],
                                     in_=bank[mt].h[:, 0:SEC])
                bank[mt].rd("v")

    def evac_g():
        @block.scalar
        def _(sc):
            sc = ctx.ch("s", sc)
            ctx.wait(sc, "s", "t", ctx.cnt["t"])
            for mt in range(4):
                inst = sc.activation(out=ns.gb[:, mt * SEC:(mt + 1) * SEC],
                                     in_=bank[mt].h[:, 0:SEC], func=AF.Silu)
                bank[mt].rd("s")

    vr = emit_mix("r")
    emit_proj(ns.wbA, "wr", vr)
    evac_plain(ns.rb)

    @block.gpsimd
    def _(g):
        g = ctx.ch("g", g)
        nm, src, buf = WMAP[2]
        ctx.wait(g, "g", "t", ns.bfree.get(id(buf), 0))
        ns.w_load[nm] = ctx.dgroup(g, "g", [
            (buf[:, p * FH:(p + 1) * FH], src[128 * p:128 * (p + 1), :])
            for p in range(8)], "lg")

    vk = emit_mix("k")
    emit_proj(ns.wbB, "wk", vk)
    evac_shift(ns.kb)

    @block.gpsimd
    def _(g):
        g = ctx.ch("g", g)
        nm, src, buf = WMAP[3]
        ctx.wait(g, "g", "t", ns.bfree.get(id(buf), 0))
        ns.w_load[nm] = ctx.dgroup(g, "g", [
            (buf[:, p * FH:(p + 1) * FH], src[128 * p:128 * (p + 1), :])
            for p in range(8)], "lg")
    vv = emit_mix("v")
    emit_proj(ns.wbA, "wv", vv)
    evac_shift(ns.vb)
    vg = emit_mix("g")
    emit_proj(ns.wbB, "wg", vg)
    evac_g()
    v_rkvg = ctx.cnt["v"]

    @block.tensor
    def _(t):
        ctx.wait(t, "t", "v", v_rkvg)
        for c in range(NCH):
            for mt in range(4):
                bk = bank[6 + (c * 4 + mt) % 2]
                src = ns.vb[:, mt * P3 + 3 + c * 128:mt * P3 + 3 + (c + 1) * 128]
                bk.tr(t, bk.h[:, 0:64].bitcast(BF16)[:, 0:128], src,
                      ns.ident[:, :])
                @block.scalar
                def _(sc, bk=bk, c=c, mt=mt):
                    sc = ctx.ch("s", sc)
                    ctx.wait(sc, "s", "t", ctx.cnt["t"])
                    inst = sc.copy(
                        out=ns.vtm[:, c * FH + mt * 128:c * FH + (mt + 1) * 128],
                        in_=bk.h[:, 0:64].bitcast(BF16)[:, 0:128])
                    bk.rd("s")
    s_vtm = ctx.cnt["s"]

    emit_families(ns, s, csp)

    emit_chunks(ns, s, csp)

    emit_bands_gn(ns, s, csp)

    @block.vector
    def _(v):
        v = ctx.ch("v", v)
        ctx.wait(v, "v", "t", ctx.cnt["t"])
        ctx.wait(v, "v", "v", ctx.cnt["v"])
        for mt in range(4):
            v.tensor_copy(out=ns.kb[:, mt * P3:mt * P3 + 3],
                          in_=ns.kb[:, mt * P3 + SEC:mt * P3 + SEC + 3])
            v.tensor_copy(out=ns.vb[:, mt * P3:mt * P3 + 3],
                          in_=ns.vb[:, mt * P3 + SEC:mt * P3 + SEC + 3])
            inst = v.tensor_copy(out=ns.csb[:, mt * P3:mt * P3 + 3],
                                 in_=ns.csb[:, mt * P3 + SEC:mt * P3 + SEC + 3])
        for p in range(8):
            hp = ns.hb[:, p * (SEC + 1):(p + 1) * (SEC + 1)]
            inst = v.tensor_copy(out=hp[:, 0:1], in_=hp[:, SEC:SEC + 1])
    ns.carry_v = ctx.cnt["v"]


def emit_families(ns, s, csp):
    ctx, block, bank = ns.ctx, ns.block, ns.bank
    P3 = SEC + 3

    def fam(f):
        return getattr(ns, "fam_" + f)

    for mt in range(4):
        cp = csp(mt)
        co = mt * SEC
        c3 = mt * P3

        def g32(base):
            return bc_blocks(cp, base, 32, 16)

        def g8(base):
            return bc_blocks(cp, base, 8, 64)

        def cs_in(off):
            return grp(cp, off, 32, 16)

        def cs_in8(off):
            return grp(cp, off, 8, 64)

        @block.vector
        def _(v):
            v = ctx.ch("v", v)
            ctx.wait(v, "v", "s", ctx.cnt["s"])
            ctx.wait(v, "v", "t", ns.fam_free)
            v.tensor_tensor(out=grp(ns.arg1, 0, 32, 16), in0=cs_in(3),
                            in1=g32(34), op=ALU.subtract)
            inst = v.tensor_tensor(out=grp(ns.arg2, 0, 32, 16), in0=g32(2),
                                   in1=cs_in(2), op=ALU.subtract)

        @block.scalar
        def _(sc):
            sc = ctx.ch("s", sc)
            ctx.wait(sc, "s", "v", ctx.cnt["v"])
            sc.activation(out=fam("kr32_1")[:, co:co + SEC], in_=ns.arg1[:, :],
                          func=AF.Exp)
            inst = sc.activation(out=fam("r32")[:, co:co + SEC],
                                 in_=ns.arg2[:, :], func=AF.Exp)

        @block.vector
        def _(v):
            v = ctx.ch("v", v)
            ctx.wait(v, "v", "s", ctx.cnt["s"])
            v.tensor_tensor(out=grp(ns.arg1, 0, 8, 64), in0=cs_in8(3),
                            in1=g8(10), op=ALU.subtract)
            inst = v.tensor_tensor(out=grp(ns.arg2, 0, 8, 64), in0=g8(2),
                                   in1=cs_in8(2), op=ALU.subtract)

        @block.scalar
        def _(sc):
            sc = ctx.ch("s", sc)
            ctx.wait(sc, "s", "v", ctx.cnt["v"])
            sc.activation(out=fam("kr8_1")[:, co:co + SEC], in_=ns.arg1[:, :],
                          func=AF.Exp)
            inst = sc.activation(out=fam("r8")[:, co:co + SEC],
                                 in_=ns.arg2[:, :], func=AF.Exp)

        @block.vector
        def _(v):
            v = ctx.ch("v", v)
            ctx.wait(v, "v", "s", ctx.cnt["s"])
            v.tensor_tensor(out=ns.arg1[:, :], in0=cp[:, 1:SEC + 1],
                            in1=cp[:, 2:SEC + 2], op=ALU.subtract)
            inst = v.tensor_tensor(out=ns.arg2[:, :], in0=cp[:, 0:SEC],
                                   in1=cp[:, 2:SEC + 2], op=ALU.subtract)

        @block.scalar
        def _(sc):
            sc = ctx.ch("s", sc)
            ctx.wait(sc, "s", "v", ctx.cnt["v"])
            sc.activation(out=ns.b2f[:, co:co + SEC], in_=ns.arg1[:, :],
                          func=AF.Exp)
            inst = sc.activation(out=ns.b3f[:, co:co + SEC], in_=ns.arg2[:, :],
                                 func=AF.Exp)
        s_bases = ctx.cnt["s"]

        @block.vector
        def _(v):
            v = ctx.ch("v", v)
            ctx.wait(v, "v", "s", ctx.cnt["s"])
            v.memset(ns.arg1[:, 0:17], 0.0)
            v.tensor_tensor(out=ns.arg1[:, 0:15],
                            in0=cols(cp, 34, 32, 15),
                            in1=cols(cp, 66, 32, 15),
                            op=ALU.subtract)
            v.tensor_tensor(out=ns.arg1[:, 32:32 + NB8],
                            in0=cols(cp, 10, 8, NB8),
                            in1=cols(cp, 6, 8, NB8),
                            op=ALU.subtract)
            v.tensor_tensor(out=ns.arg1[:, 96:96 + NB8],
                            in0=cols(cp, 6, 8, NB8),
                            in1=cols(cp, 2, 8, NB8),
                            op=ALU.subtract)
            v.tensor_scalar_min(out=ns.arg1[:, 32:32 + 2 * NB8],
                                in0=ns.arg1[:, 32:32 + 2 * NB8], scalar1=DCLAMP)
            v.tensor_tensor(out=grp(ns.arg1, 160, 2, 32),
                            in0=cols(cp, 10, 8, NB8).rearrange("p (b i) -> p b i", i=2),
                            in1=bc_blocks(cp, 18, 16, 32, rep=2),
                            op=ALU.subtract)
            inst = v.tensor_tensor(out=grp(ns.arg1, 224, 2, 32),
                                   in0=bc_blocks(cp, 2, 16, 32, rep=2),
                                   in1=cols(cp, 2, 8, NB8).rearrange("p (b i) -> p b i", i=2),
                                   op=ALU.subtract)
            v.tensor_tensor(out=grp(ns.arg1, 288, 4, 4),
                            in0=cols(cp, 34, 32, 16).rearrange("p (b i) -> p b i", i=4),
                            in1=bc_blocks(cp, 130, 128, 4, rep=4),
                            op=ALU.subtract)
            inst2 = v.tensor_tensor(out=grp(ns.arg1, 304, 4, 4),
                                    in0=bc_blocks(cp, 2, 128, 4, rep=4),
                                    in1=cols(cp, 2, 32, 16).rearrange("p (b i) -> p b i", i=4),
                                    op=ALU.subtract)
            inst3 = v.tensor_tensor(out=ns.arg1[:, 320:320 + NCH],
                                    in0=cols(cp, 2, 128, NCH),
                                    in1=cols(cp, 130, 128, NCH),
                                    op=ALU.subtract)

        @block.scalar
        def _(sc):
            sc = ctx.ch("s", sc)
            ctx.wait(sc, "s", "v", ctx.cnt["v"])
            sc.activation(out=ns.ef32[:, mt * 17:mt * 17 + 17],
                          in_=ns.arg1[:, 0:17], func=AF.Exp)
            sc.activation(out=ns.efh8[:, mt * NB8:(mt + 1) * NB8],
                          in_=ns.arg1[:, 32:32 + NB8], func=AF.Exp)
            sc.activation(out=ns.efh8b[:, mt * NB8:(mt + 1) * NB8],
                          in_=ns.arg1[:, 96:96 + NB8], func=AF.Exp)
            sc.activation(out=ns.f16k[:, mt * NB8:(mt + 1) * NB8],
                          in_=ns.arg1[:, 160:160 + NB8], func=AF.Exp)
            sc.activation(out=ns.f16r[:, mt * NB8:(mt + 1) * NB8],
                          in_=ns.arg1[:, 224:224 + NB8], func=AF.Exp)
            sc.activation(out=ns.fchk[:, mt * NB32:(mt + 1) * NB32],
                          in_=ns.arg1[:, 288:288 + NB32], func=AF.Exp)
            sc.activation(out=ns.fchkr[:, mt * NB32:(mt + 1) * NB32],
                          in_=ns.arg1[:, 304:304 + NB32], func=AF.Exp)
            inst = sc.activation(out=ns.etot[:, mt * NCH:(mt + 1) * NCH],
                                 in_=ns.arg1[:, 320:320 + NCH], func=AF.Exp)
        s_fac = ctx.cnt["s"]

        @block.vector
        def _(v):
            v = ctx.ch("v", v)
            ctx.wait(v, "v", "s", s_fac)
            kv = ns.kb[:, c3 + 3:c3 + 3 + SEC]
            rv = ns.rb[:, co:co + SEC]
            F = lambda f: fam(f)[:, co:co + SEC]
            v.tensor_tensor(out=F("kr32_2"), in0=F("kr32_1"),
                            in1=bc_blocks(ns.ef32, mt * 17, 1, 16, rep=32),
                            op=ALU.mult)
            v.tensor_tensor(out=F("kr32_3"), in0=F("kr32_2"),
                            in1=bc_blocks(ns.ef32, mt * 17 + 1, 1, 16, rep=32),
                            op=ALU.mult)
            v.tensor_tensor(out=F("kr16_1"), in0=F("kr8_1"),
                            in1=bc_blocks(ns.f16k, mt * NB8, 1, NB8, rep=8),
                            op=ALU.mult)
            v.tensor_tensor(out=F("r16"), in0=F("r8"),
                            in1=bc_blocks(ns.f16r, mt * NB8, 1, NB8, rep=8),
                            op=ALU.mult)
            v.tensor_tensor(out=F("kd8"), in0=F("kr8_1"),
                            in1=bc_blocks(ns.efh8, mt * NB8, 1, NB8, rep=8),
                            op=ALU.mult)
            v.tensor_tensor(out=F("rd8"), in0=F("r8"),
                            in1=bc_blocks(ns.efh8b, mt * NB8, 1, NB8, rep=8),
                            op=ALU.mult)
            v.tensor_tensor(out=F("khat"), in0=F("kr32_1"),
                            in1=bc_blocks(ns.fchk, mt * NB32, 1, NB32, rep=32),
                            op=ALU.mult)
            v.tensor_tensor(out=F("rchunk"), in0=F("r32"),
                            in1=bc_blocks(ns.fchkr, mt * NB32, 1, NB32, rep=32),
                            op=ALU.mult)
            for f in ("kr32_1", "kr32_2", "kr32_3", "kr16_1", "kr8_1", "kd8",
                      "khat"):
                v.tensor_tensor(out=F(f), in0=F(f), in1=kv, op=ALU.mult)
            for f in ("r32", "r16", "r8", "rd8", "rchunk"):
                inst = v.tensor_tensor(out=F(f), in0=F(f), in1=rv, op=ALU.mult)
    ns.v_fam = ctx.cnt["v"]

    @block.tensor
    def _(t):
        ctx.wait(t, "t", "v", ns.v_fam)
        for c in range(NCH):
            for mt in range(4):
                bk = bank[6 + (c * 4 + mt) % 2]
                src = ns.fam_khat[:, mt * SEC + c * 128:mt * SEC + (c + 1) * 128]
                bk.tr(t, bk.h[:, 0:64].bitcast(BF16)[:, 0:128], src,
                      ns.ident[:, :])

                @block.scalar
                def _(sc, bk=bk, c=c, mt=mt):
                    sc = ctx.ch("s", sc)
                    ctx.wait(sc, "s", "t", ctx.cnt["t"])
                    inst = sc.copy(
                        out=ns.khtm[:, c * FH + mt * 128:c * FH + (mt + 1) * 128],
                        in_=bk.h[:, 0:64].bitcast(BF16)[:, 0:128])
                    bk.rd("s")
    ns.s_khtm = ctx.cnt["s"]


def emit_chunks(ns, s, csp):
    ctx, block, bank = ns.ctx, ns.block, ns.bank

    def fsl(f, h, cols):
        mt, hr = h // 2, 64 * (h % 2)
        return getattr(ns, "fam_" + f)[hr:hr + 64,
                                       mt * SEC + cols[0]:mt * SEC + cols[1]]

    for c in range(NCH):
        @block.vector
        def _(v):
            v = ctx.ch("v", v)
            ctx.wait(v, "v", "v", ctx.cnt["v"])
            ctx.wait(v, "v", "t", getattr(ns, "S_read", 0))
            inst = v.tensor_copy(out=ns.Sbf[:, :], in_=ns.Ssb[:, :])
        v_sbf = ctx.cnt["v"]

        for h in range(NHL):
            mt, hr = h // 2, 64 * (h % 2)
            cc = c * 128
            pm = bank[h % 2]
            a1, a2, a3 = bank[2], bank[3], bank[6]
            ob = bank[4 + h % 2]

            @block.tensor
            def _(t):
                ctx.wait(t, "t", "v", max(ns.v_fam, v_sbf))
                if s > 0 or c > 0 or h > 1:
                    pass
                for j in range(1, 4):
                    for i in range(j):
                        pm.mm(t, pm.h[32 * i:32 * i + 32, 32 * j:32 * j + 32],
                              fsl("kr32_%d" % (j - i), h,
                                  (cc + 32 * i, cc + 32 * i + 32)),
                              fsl("r32", h, (cc + 32 * j, cc + 32 * j + 32)),
                              start=True, stop=True,
                              tile_position=(hr, 32 * i))
                a1.mm(t, a1.h[:, 0:128], fsl("kd8", h, (cc, cc + 128)),
                      fsl("rd8", h, (cc, cc + 128)), start=True, stop=True,
                      tile_position=(hr, 0))
                a2.mm(t, a2.h[:, 0:128], fsl("kr8_1", h, (cc, cc + 128)),
                      fsl("r8", h, (cc, cc + 128)), start=True, stop=True,
                      tile_position=(hr, 0))
                a3.mm(t, a3.h[:, 0:128], fsl("kr16_1", h, (cc, cc + 128)),
                      fsl("r16", h, (cc, cc + 128)), start=True, stop=True,
                      tile_position=(hr, 0))
            t_pieces = ctx.cnt["t"]

            @block.vector
            def _(v):
                v = ctx.ch("v", v)
                ctx.wait(v, "v", "t", t_pieces)
                v.tensor_tensor(out=ns.psb[:, 0:128], in0=pm.h[:, 0:128],
                                in1=ns.mstr[:, :], op=ALU.mult)
                pm.rd("v")
                v.tensor_tensor(out=ns.psb[:, 128:256], in0=a1.h[:, 0:128],
                                in1=ns.md8[:, :], op=ALU.mult)
                a1.rd("v")
                v.tensor_tensor(out=ns.psb[:, 256:384], in0=a2.h[:, 0:128],
                                in1=ns.m8[:, :], op=ALU.mult)
                a2.rd("v")
                inst = v.tensor_tensor(out=ns.psb[:, 384:512],
                                       in0=a3.h[:, 0:128],
                                       in1=ns.m16[:, :], op=ALU.mult)
                a3.rd("v")
            v_psb = ctx.cnt["v"]
            s_psb = ctx.cnt["s"]

            @block.tensor
            def _(t):
                ctx.wait(t, "t", "v", v_psb)
                ctx.wait(t, "t", "s", max(s_psb, ns.s_khtm,
                                          getattr(ns, "s_vtm", 0)))
                vsl = ns.vtm[:, c * FH + 64 * h:c * FH + 64 * h + 64]
                for pc in range(4):
                    ob.mm(t, ob.h[0:64, 0:128], vsl,
                          ns.psb[:, 128 * pc:128 * pc + 128],
                          start=(pc == 0), stop=False)
                ob.mm(t, ob.h[0:64, 0:128], ns.Sbf[hr:hr + 64, 64 * mt:64 * mt + 64],
                      fsl("rchunk", h, (cc, cc + 128)), start=False, stop=True)
                ksl = ns.khtm[:, c * FH + 64 * h:c * FH + 64 * h + 64]
                a1.mm(t, a1.h[hr:hr + 64, 0:64], ksl, vsl, start=True,
                      stop=True, tile_position=(0, hr))
                ns.S_read = ctx.cnt["t"]

            @block.scalar
            def _(sc):
                sc = ctx.ch("s", sc)
                ctx.wait(sc, "s", "t", ctx.cnt["t"])
                inst = sc.copy(out=ns.osec[hr:hr + 64,
                                           mt * SEC + cc:mt * SEC + cc + 128],
                               in_=ob.h[0:64, 0:128])
                ob.rd("s")

            @block.vector
            def _(v):
                v = ctx.ch("v", v)
                ctx.wait(v, "v", "t", ctx.cnt["t"])
                inst = v.scalar_tensor_tensor(
                    out=ns.Ssb[hr:hr + 64, 64 * mt:64 * mt + 64],
                    in0=ns.Ssb[hr:hr + 64, 64 * mt:64 * mt + 64],
                    scalar=ns.etot[hr:hr + 64, mt * NCH + c:mt * NCH + c + 1],
                    in1=a1.h[hr:hr + 64, 0:64],
                    op0=ALU.mult, op1=ALU.add)
                a1.rd("v")
    ns.fam_free = ctx.cnt["t"]


def emit_bands_gn(ns, s, csp):
    ctx, block, bank = ns.ctx, ns.block, ns.bank
    P3 = SEC + 3

    for d in range(4):
        @block.vector
        def _(v, d=d):
            v = ctx.ch("v", v)
            ctx.wait(v, "v", "t", getattr(ns, "band_mm", 0))
            for mt in range(4):
                rv = ns.rb[:, mt * SEC:(mt + 1) * SEC]
                pb = ns.prodb[:, mt * SEC:(mt + 1) * SEC]
                if d == 0:
                    v.tensor_scalar(out=pb, in0=rv,
                                    scalar1=ns.v_uu[:, mt:mt + 1],
                                    scalar2=None, op0=ALU.mult)
                    v.tensor_tensor(
                        out=pb, in0=pb,
                        in1=ns.kb[:, mt * P3 + 3:mt * P3 + 3 + SEC],
                        op=ALU.mult)
                else:
                    ksh = ns.kb[:, mt * P3 + 3 - d:mt * P3 + 3 - d + SEC]
                    v.tensor_tensor(out=pb, in0=rv, in1=ksh, op=ALU.mult)
                    if d == 2:
                        v.tensor_tensor(
                            out=pb, in0=pb,
                            in1=ns.b2f[:, mt * SEC:(mt + 1) * SEC],
                            op=ALU.mult)
                    elif d == 3:
                        v.tensor_tensor(
                            out=pb, in0=pb,
                            in1=ns.b3f[:, mt * SEC:(mt + 1) * SEC],
                            op=ALU.mult)
        v_prod = ctx.cnt["v"]

        @block.tensor
        def _(t, d=d):
            ctx.wait(t, "t", "v", v_prod)
            bk = bank[6 + d % 2]
            for mt in range(4):
                bk.mm(t, bk.h[0:8, 0:SEC], ns.indbt[:, 8 * mt:8 * mt + 8],
                      ns.prodb[:, mt * SEC:(mt + 1) * SEC],
                      start=(mt == 0), stop=(mt == 3))
            ns.band_mm = ctx.cnt["t"]

        @block.vector
        def _(v, d=d):
            v = ctx.ch("v", v)
            ctx.wait(v, "v", "t", ns.band_mm)
            bk = bank[6 + d % 2]
            bm = getattr(ns, "browm%d" % d)
            if d == 0:
                v.tensor_copy(out=bm[:, 0:SEC], in_=bk.h[0:8, 0:SEC])
            else:
                v.tensor_tensor(out=bm[:, 0:SEC], in0=bk.h[0:8, 0:SEC],
                                in1=ns.bmask[:, (d - 1) * SEC:d * SEC],
                                op=ALU.mult)
            bk.rd("v")
    v_brow = ctx.cnt["v"]

    for d in range(4):
        for mt in range(4):
            @block.tensor
            def _(t, d=d, mt=mt):
                ctx.wait(t, "t", "v", max(v_brow, getattr(ns, "band_ap", 0)))
                bk = bank[4 + mt % 2]
                bk.mm(t, bk.h[:, 0:SEC], ns.indb[:, 128 * mt:128 * mt + 128],
                      getattr(ns, "browm%d" % d)[:, 0:SEC], start=True,
                      stop=True)

                @block.vector
                def _(v, bk=bk, d=d, mt=mt):
                    v = ctx.ch("v", v)
                    ctx.wait(v, "v", "t", ctx.cnt["t"])
                    vsh = ns.vb[:, mt * P3 + 3 - d:mt * P3 + 3 - d + SEC]
                    v.tensor_tensor(out=ns.ewb[:, mt * SEC:(mt + 1) * SEC],
                                    in0=bk.h[:, 0:SEC], in1=vsh, op=ALU.mult)
                    bk.rd("v")
                    v.tensor_tensor(
                        out=ns.osec[:, mt * SEC:(mt + 1) * SEC],
                        in0=ns.osec[:, mt * SEC:(mt + 1) * SEC],
                        in1=ns.ewb[:, mt * SEC:(mt + 1) * SEC],
                        op=ALU.add)
                    ns.band_ap = ctx.cnt["v"]

    for mt in range(4):
        @block.scalar
        def _(sc, mt=mt):
            sc = ctx.ch("s", sc)
            ctx.wait(sc, "s", "v", ctx.cnt["v"])
            sc.activation(out=ns.ewb[:, mt * SEC:(mt + 1) * SEC],
                          in_=ns.osec[:, mt * SEC:(mt + 1) * SEC],
                          func=AF.Square)
    s_sq = ctx.cnt["s"]

    @block.tensor
    def _(t):
        ctx.wait(t, "t", "s", s_sq)
        ctx.wait(t, "t", "v", ctx.cnt["v"])
        for mt in range(4):
            bank[6].mm(t, bank[6].h[0:8, 0:SEC],
                       ns.indbtf[:, 8 * mt:8 * mt + 8],
                       ns.osec[:, mt * SEC:(mt + 1) * SEC],
                       start=(mt == 0), stop=(mt == 3))
        for mt in range(4):
            bank[7].mm(t, bank[7].h[0:8, 0:SEC],
                       ns.indbtf[:, 8 * mt:8 * mt + 8],
                       ns.ewb[:, mt * SEC:(mt + 1) * SEC],
                       start=(mt == 0), stop=(mt == 3))
    t_gsum = ctx.cnt["t"]

    @block.vector
    def _(v):
        v = ctx.ch("v", v)
        ctx.wait(v, "v", "t", t_gsum)
        sr = ns.srow
        v.tensor_scalar(out=ns.mean8[:, 0:SEC], in0=bank[6].h[0:8, 0:SEC],
                        scalar1=1.0 / 64, scalar2=None, op0=ALU.mult)
        v.tensor_scalar(out=sr[0:8, 0:SEC], in0=bank[7].h[0:8, 0:SEC],
                        scalar1=1.0 / 64, scalar2=None, op0=ALU.mult)
        bank[6].rd("v")
        bank[7].rd("v")
        v.tensor_tensor(out=ns.svar[:, 0:SEC], in0=ns.mean8[:, 0:SEC],
                        in1=ns.mean8[:, 0:SEC], op=ALU.mult)
        v.tensor_tensor(out=ns.svar[:, 0:SEC], in0=sr[0:8, 0:SEC],
                        in1=ns.svar[:, 0:SEC], op=ALU.subtract)
        v.tensor_scalar(out=ns.svar[:, 0:SEC], in0=ns.svar[:, 0:SEC],
                        scalar1=EPS_GN, scalar2=None, op0=ALU.add)
        v.reciprocal(out=ns.rstd8[:, 0:SEC], in_=ns.svar[:, 0:SEC])

    @block.scalar
    def _(sc):
        sc = ctx.ch("s", sc)
        ctx.wait(sc, "s", "v", ctx.cnt["v"])
        sc.activation(out=ns.rstd8[:, 0:SEC], in_=ns.rstd8[:, 0:SEC],
                      func=AF.Sqrt)
    s_rstd8 = ctx.cnt["s"]

    for mt in range(4):
        @block.tensor
        def _(t, mt=mt):
            ctx.wait(t, "t", "s", s_rstd8)
            bk2 = bank[mt % 2]
            bk3 = bank[2 + mt % 2]
            bk2.mm(t, bk2.h[:, 0:SEC], ns.indbf[:, 128 * mt:128 * mt + 128],
                   ns.mean8[:, 0:SEC], start=True, stop=True)
            bk3.mm(t, bk3.h[:, 0:SEC], ns.indbf[:, 128 * mt:128 * mt + 128],
                   ns.rstd8[:, 0:SEC], start=True, stop=True)

        @block.vector
        def _(v, mt=mt):
            v = ctx.ch("v", v)
            ctx.wait(v, "v", "t", ctx.cnt["t"])
            ctx.dwait(v, "v", getattr(ns, "omix_st", None))
            bk2 = bank[mt % 2]
            bk3 = bank[2 + mt % 2]
            ob = ns.osec[:, mt * SEC:(mt + 1) * SEC]
            v.tensor_tensor(out=ob, in0=ob, in1=bk2.h[:, 0:SEC],
                            op=ALU.subtract)
            bk2.rd("v")
            v.tensor_tensor(out=ob, in0=ob, in1=bk3.h[:, 0:SEC], op=ALU.mult)
            bk3.rd("v")
            v.tensor_scalar(out=ob, in0=ob, scalar1=ns.v_gnw[:, mt:mt + 1],
                            scalar2=ns.v_gnb[:, mt:mt + 1], op0=ALU.mult,
                            op1=ALU.add)
            v.tensor_tensor(out=ns.omix[:, mt * SEC:(mt + 1) * SEC], in0=ob,
                            in1=ns.gb[:, mt * SEC:(mt + 1) * SEC],
                            op=ALU.mult)
    v_omix = ctx.cnt["v"]

    @block.sync
    def _(q):
        ctx.wait(q, "q", "v", v_omix)
        ns.omix_st = ctx.dgroup(q, "q", [
            (ns.ORW_B[128 * mt:128 * (mt + 1), s * SEC:(s + 1) * SEC],
             ns.omix[:, mt * SEC:(mt + 1) * SEC]) for mt in range(4)], "sq")
        ns.omix_keys.append(ns.omix_st)


def emit_mixer_taps(ns):
    ctx, block = ns.ctx, ns.block
    if not ns.taps:
        return

    @block.sync
    def _(q):
        ctx.wait(q, "q", "v", ctx.cnt["v"])
        ctx.wait(q, "q", "s", ctx.cnt["s"])
        prs = []
        for nm, ap in ns.taps.items():
            src = getattr(ns, nm, None)
            if src is None:
                continue
            prs.append((ap, src[:, 0:ap.shape[1]]))
        ns.tap_st = ctx.dgroup(q, "q", prs, "sq")


def emit_finish(ns):
    ctx, block = ns.ctx, ns.block

    @block.sync
    def _(q):
        ctx.wait(q, "q", "v", ctx.cnt["v"])
        srcbuf = getattr(ns, "x2buf", None)
        if srcbuf is not None:
            key = ctx.dgroup(q, "q", [
                (ns.OUT[128 * p:128 * (p + 1), 0:TO],
                 srcbuf[:, p * TO:(p + 1) * TO]) for p in range(8)], "sq")
        else:
            key = ctx.dgroup(q, "q", [
                (ns.OUT[128 * p:128 * (p + 1), 0:TO], ns.xs[:, 0:TO])
                for p in range(8)], "sq")
        for nm, val in ctx.dcnt.items():
            if val > 0:
                q.wait_ge(ctx.dsem[nm], val)


def _build_consts():
    c = {}
    i = np.arange(128)
    c["ident"] = np.eye(128, dtype=np.float32)
    c["indh"] = np.stack([(i < 64).astype(np.float32),
                          (i >= 64).astype(np.float32)], 1)
    c["indhb"] = c["indh"]
    indb = np.zeros((8, 512), np.float32)
    for mt in range(4):
        indb[2 * mt, 128 * mt:128 * mt + 64] = 1.0
        indb[2 * mt + 1, 128 * mt + 64:128 * mt + 128] = 1.0
    c["indb"] = indb
    c["indbf"] = indb
    indbt = np.zeros((128, 32), np.float32)
    for mt in range(4):
        indbt[0:64, 8 * mt + 2 * mt] = 1.0
        indbt[64:128, 8 * mt + 2 * mt + 1] = 1.0
    c["indbt"] = indbt
    c["indbtf"] = indbt
    ss, tt = np.meshgrid(i, i, indexing="ij")
    same8 = (ss // 8) == (tt // 8)
    c["md8"] = (same8 & (tt - ss >= 4)).astype(np.float32)
    same32 = (tt // 32) == (ss // 32)
    c["m8"] = (((tt // 8) == (ss // 8) + 1) & same32).astype(np.float32)
    c["m16"] = (((tt // 16) == (ss // 16) + 1) &
                ((tt // 8) >= (ss // 8) + 2) & same32).astype(np.float32)
    c["mstr"] = ((tt // 32) > (ss // 32)).astype(np.float32)
    tmod = np.arange(SEC) % 8
    bm = np.zeros((8, 3 * SEC), np.float32)
    for d in (1, 2, 3):
        bm[:, (d - 1) * SEC:d * SEC] = (tmod >= d)[None, :]
    c["bmask"] = bm
    c["onesc"] = np.ones((1, 128), np.float32)
    c["onesr"] = np.ones((128, 1), np.float32)
    c["identf"] = np.eye(128, dtype=np.float32)
    perm = np.zeros((128, 128), np.float32)
    for mm_ in range(128):
        s_ = mm_ + 32 if (mm_ % 64) < 32 else mm_ - 32
        perm[s_, mm_] = 1.0
    c["rope_perm"] = perm
    inde = np.zeros((16, 16 * 128), np.float32)
    for e in range(16):
        inde[e, e * 128:(e + 1) * 128] = 1.0
    c["inde16"] = inde
    pos = np.arange(TF, dtype=np.float64)
    dim = np.arange(32, dtype=np.float64)
    ang = pos[None, :] / (10000.0 ** (dim / 32.0))[:, None]
    sn, cs_ = np.sin(ang), np.cos(ang)
    rs = np.zeros((128, TF)); rc = np.zeros((128, TF))
    for half in range(2):
        b = 64 * half
        rs[b:b + 32] = -sn
        rs[b + 32:b + 64] = sn
        rc[b:b + 32] = cs_
        rc[b + 32:b + 64] = cs_
    c["ropesin"] = rs
    c["ropecos"] = rc
    ii, qq = np.meshgrid(np.arange(128), np.arange(512), indexing="ij")
    cm = np.zeros((128, 4 * 512), np.float32)
    for j in range(4):
        cm[:, j * 512:(j + 1) * 512] = (qq - ii >= 128 * j)
    c["cmask"] = cm
    return c


def _pack_vec(x, ncols):
    return np.ascontiguousarray(x.reshape(ncols, 128).T)


def make_in_maps(inputs, n_sec=TF // SEC):
    consts = _build_consts()
    bf = np.dtype("bfloat16") if False else None
    import ml_dtypes
    BFD = ml_dtypes.bfloat16
    in_maps = []
    x = np.asarray(inputs["L0_attn_x"], np.float32)
    for core in range(8):
        b, rank = core // 2, core % 2
        hs = slice(rank * FH, (rank + 1) * FH)
        m = {}
        xf = np.ascontiguousarray(x[b].T)
        m["xf"] = xf
        m["xo"] = np.ascontiguousarray(xf[:, rank * TO:(rank + 1) * TO])
        m["wr"] = np.ascontiguousarray(inputs["L0_attn_Wr"][:, hs], np.float32)
        m["wk"] = np.ascontiguousarray(inputs["L0_attn_Wk"][:, hs], np.float32)
        m["wv"] = np.ascontiguousarray(inputs["L0_attn_Wv"][:, hs], np.float32)
        m["wg"] = np.ascontiguousarray(inputs["L0_attn_Wg"][:, hs], np.float32)
        m["wo"] = np.ascontiguousarray(inputs["L0_attn_Wo"], np.float32)
        m["w1"] = np.ascontiguousarray(inputs["L0_attn_ffn_w1"], np.float32)
        m["w2"] = np.ascontiguousarray(inputs["L0_attn_ffn_w2"], np.float32)
        m["wqt"] = np.ascontiguousarray(np.asarray(inputs["L0_ffn_weight"], np.float32).T)
        m["wla"] = np.ascontiguousarray(inputs["L0_attn_wA"], np.float32)
        m["wlb"] = np.ascontiguousarray(inputs["L0_attn_wB"][:, hs], np.float32)
        for nm, key in (("mu_r", "L0_attn_mu_r"), ("mu_k", "L0_attn_mu_k"),
                        ("mu_v", "L0_attn_mu_v"), ("mu_w", "L0_attn_mu_w"),
                        ("mu_g", "L0_attn_mu_g"), ("ln1w", "L0_attn_ln1_w"),
                        ("ln2w", "L0_attn_ln2_w")):
            m[nm] = _pack_vec(np.asarray(inputs[key], np.float32), 8)
        m["w0"] = _pack_vec(np.asarray(inputs["L0_attn_w0"], np.float32)[rank * FH:(rank + 1) * FH], 4)
        m["uu"] = _pack_vec(np.asarray(inputs["L0_attn_u"], np.float32).reshape(-1)[rank * FH:(rank + 1) * FH], 4)
        m["gnw"] = _pack_vec(np.asarray(inputs["L0_attn_gn_w"], np.float32).reshape(-1)[rank * FH:(rank + 1) * FH], 4)
        m["gnb"] = _pack_vec(np.asarray(inputs["L0_attn_gn_b"], np.float32).reshape(-1)[rank * FH:(rank + 1) * FH], 4)
        for k, v in consts.items():
            if k in ("ident", "indhb", "indb", "indbt", "md8", "m8", "m16", "mstr", "bmask", "inde16", "cmask"):
                m[k] = v.astype(BFD)
            else:
                m[k] = v.astype(np.float32)
        hsel = slice(rank * 8, (rank + 1) * 8)
        for nm, key in (("l1wq", "L1_attn_W_Q"), ("l1wk", "L1_attn_W_K"),
                        ("l1wv", "L1_attn_W_V")):
            w3 = np.asarray(inputs[key], np.float32)[hsel]
            m[nm] = np.ascontiguousarray(w3.transpose(1, 0, 2).reshape(D, FH))
        m["l1wo"] = np.ascontiguousarray(
            np.asarray(inputs["L1_attn_W_O"], np.float32).reshape(D, D))
        m["l1bq"] = _pack_vec(np.asarray(inputs["L1_attn_b_Q"],
                                         np.float32).reshape(-1)[hs], 4)
        m["l1bk"] = _pack_vec(np.asarray(inputs["L1_attn_b_K"],
                                         np.float32).reshape(-1)[hs], 4)
        m["l1bv_row"] = np.ascontiguousarray(
            np.asarray(inputs["L1_attn_b_V"], np.float32).reshape(1, -1)[:, hs])
        m["l1bo"] = _pack_vec(np.asarray(inputs["L1_attn_b_O"], np.float32), 8)
        m["gate_wt"] = np.ascontiguousarray(
            np.asarray(inputs["L1_ffn_gate_w"], np.float32).T)
        m["bias_e"] = np.ascontiguousarray(
            np.asarray(inputs["L1_ffn_bias_e"], np.float32).reshape(1, 16))
        m["moe_up"] = np.ascontiguousarray(
            np.asarray(inputs["L1_ffn_up_w"], np.float32).transpose(0, 2, 1)
            .reshape(MOE_NE * D, 512))
        m["moe_dn"] = np.ascontiguousarray(
            np.asarray(inputs["L1_ffn_down_w"], np.float32).transpose(0, 2, 1)
            .reshape(MOE_NE * 512, D))
        m["sh_upt"] = np.ascontiguousarray(
            np.asarray(inputs["L1_ffn_shared_up"], np.float32).T)
        m["sh_dnt"] = np.ascontiguousarray(
            np.asarray(inputs["L1_ffn_shared_down"], np.float32).T)
        mab = np.zeros((128, 2), np.float32)
        mab[:, 0] = 1.0 if rank == 0 else 0.0
        mab[:, 1] = 1.0 if rank == 1 else 0.0
        m["mab"] = mab
        in_maps.append(m)
    return in_maps


def kernel(**inputs):
    nc, ns = build_program()
    in_maps = make_in_maps(inputs)
    res = run_bass_kernel_spmd(nc, in_maps, core_ids=list(range(8)))
    out = np.zeros((B, TF, D), np.float32)
    for core in range(8):
        b, rank = core // 2, core % 2
        out[b, rank * TO:(rank + 1) * TO] = res.results[core]["out"].T
    return out


def emit_phase_join(ns):
    ctx, block = ns.ctx, ns.block
    tv, ts_, tt, tg = ctx.cnt["v"], ctx.cnt["s"], ctx.cnt["t"], ctx.cnt["g"]
    dkeys = [(nm, val) for nm, val in ctx.dcnt.items() if val > 0]

    @block.vector
    def _(v):
        v = ctx.ch("v", v)
        for e, c in (("s", ts_), ("t", tt), ("g", tg)):
            ctx.wait(v, "v", e, c)
        for nm, val in dkeys:
            ctx.wait(v, "v", nm, val)
        v.engine_nop()
    vj = ctx.cnt["v"]

    @block.scalar
    def _(sc):
        ctx.wait(sc, "s", "v", vj)

    @block.tensor
    def _(t):
        ctx.wait(t, "t", "v", vj)

    @block.gpsimd
    def _(g):
        ctx.wait(g, "g", "v", vj)

    @block.sync
    def _(q):
        ctx.wait(q, "q", "v", vj)
    ns.join_v = vj


def emit_colstats(ns, buf, coff, ncols, mode, eps, scale=None, out=None,
                  bufW=None, outW=None, ooff=0):
    ctx, block, bank = ns.ctx, ns.block, ns.bank
    W = bufW if bufW is not None else ns.postW
    out = out if out is not None else buf
    outW = outW if outW is not None else W
    if out is buf:
        ooff = coff

    for p in range(8):
        @block.scalar
        def _(sc, p=p):
            sc = ctx.ch("s", sc)
            if p == 0:
                ctx.wait(sc, "s", "v", ctx.cnt["v"])
            else:
                ctx.wait(sc, "s", "t", ctx.cnt["t"])
            sc.activation(out=ns.psq[:, 0:ncols],
                          in_=buf[:, p * W + coff:p * W + coff + ncols],
                          func=AF.Square)

        @block.tensor
        def _(t, p=p):
            ctx.wait(t, "t", "s", ctx.cnt["s"])
            bank[6].mm(t, bank[6].h[0:1, 0:ncols], ns.onesr[:, :],
                       ns.psq[:, 0:ncols], start=(p == 0), stop=(p == 7))
            if mode == "ln":
                bank[7].mm(t, bank[7].h[0:1, 0:ncols], ns.onesr[:, :],
                           buf[:, p * W + coff:p * W + coff + ncols],
                           start=(p == 0), stop=(p == 7))
    t_stats = ctx.cnt["t"]

    @block.vector
    def _(v):
        v = ctx.ch("v", v)
        ctx.wait(v, "v", "t", t_stats)
        r1 = ns.prow1[0:1, 0:ncols]
        r2 = ns.prow2[0:1, 0:ncols]
        if mode == "rms":
            v.tensor_scalar(out=r1, in0=bank[6].h[0:1, 0:ncols],
                            scalar1=1.0 / D, scalar2=eps, op0=ALU.mult,
                            op1=ALU.add)
            bank[6].rd("v")
            v.reciprocal(out=r2, in_=r1)
        else:
            v.tensor_scalar(out=ns.prow3[0:1, 0:ncols],
                            in0=bank[7].h[0:1, 0:ncols],
                            scalar1=1.0 / D, scalar2=None, op0=ALU.mult)
            bank[7].rd("v")
            v.tensor_scalar(out=r1, in0=bank[6].h[0:1, 0:ncols],
                            scalar1=1.0 / D, scalar2=None, op0=ALU.mult)
            bank[6].rd("v")
            v.tensor_tensor(out=r2, in0=ns.prow3[0:1, 0:ncols],
                            in1=ns.prow3[0:1, 0:ncols], op=ALU.mult)
            v.tensor_tensor(out=r1, in0=r1, in1=r2, op=ALU.subtract)
            v.tensor_scalar(out=r1, in0=r1, scalar1=eps, scalar2=None,
                            op0=ALU.add)
            v.reciprocal(out=r2, in_=r1)

    @block.scalar
    def _(sc):
        sc = ctx.ch("s", sc)
        ctx.wait(sc, "s", "v", ctx.cnt["v"])
        sc.activation(out=ns.prow2[0:1, 0:ncols], in_=ns.prow2[0:1, 0:ncols],
                      func=AF.Sqrt)
    s_r = ctx.cnt["s"]

    @block.tensor
    def _(t):
        ctx.wait(t, "t", "s", s_r)
        bank[6].mm(t, bank[6].h[:, 0:ncols], ns.onesc[:, :],
                   ns.prow2[0:1, 0:ncols], start=True, stop=True)
        if mode == "ln":
            bank[7].mm(t, bank[7].h[:, 0:ncols], ns.onesc[:, :],
                       ns.prow3[0:1, 0:ncols], start=True, stop=True)
    t_bc = ctx.cnt["t"]

    @block.vector
    def _(v):
        v = ctx.ch("v", v)
        ctx.wait(v, "v", "t", max(t_bc, ctx.cnt["t"]))
        for p in range(8):
            src = buf[:, p * W + coff:p * W + coff + ncols]
            dst = out[:, p * outW + ooff:p * outW + ooff + ncols]
            if mode == "rms":
                if scale is not None:
                    v.scalar_tensor_tensor(out=dst, in0=src,
                                           scalar=scale[:, p:p + 1],
                                           in1=bank[6].h[:, 0:ncols],
                                           op0=ALU.mult, op1=ALU.mult)
                else:
                    v.tensor_tensor(out=dst, in0=src,
                                    in1=bank[6].h[:, 0:ncols], op=ALU.mult)
            else:
                v.tensor_tensor(out=ns.ptt[:, 0:ncols], in0=src,
                                in1=bank[7].h[:, 0:ncols], op=ALU.subtract)
                v.tensor_tensor(out=dst, in0=ns.ptt[:, 0:ncols],
                                in1=bank[6].h[:, 0:ncols], op=ALU.mult)
        bank[6].rd("v")
        if mode == "ln":
            bank[7].rd("v")
    return ctx.cnt["v"]


def emit_l0post(ns):
    ctx, block, bank, nc = ns.ctx, ns.block, ns.bank, ns.nc
    sb = ns.sb
    es2 = ns.es.enter_context(ExitStack())
    sb("xa", (128, 8 * TO), F32, stack=es2)
    sb("psq", (128, 512), F32, stack=es2)
    sb("prow1", (1, 512), F32, stack=es2)
    sb("prow2", (1, 512), F32, stack=es2)
    sb("prow3", (1, 512), F32, stack=es2)
    sb("ptt", (128, 512), F32, stack=es2)
    ns.es2x = ExitStack()
    sb("x0w", (128, 8 * 512), F32, stack=ns.es2x)
    ns.postW = TO
    ns.postWout = TO
    NCK = TO // 512

    @block.gpsimd
    def _(g):
        for key in ns.omix_keys:
            ctx.dwait(g, "g", key)
        ns.cc_orw = ctx.ccg(g, "AllGather", [ns.ORW_B], [ns.ORW_F], ns.groups)

    es3 = ExitStack()
    sb("wob", (128, 8 * D), BF16, stack=es3)
    sb("ogh0", (128, 8 * 512), BF16, stack=es3)
    sb("ogh1", (128, 8 * 512), BF16, stack=es3)
    sb("ogg", (128, 8 * 512), BF16, stack=es3)

    @block.gpsimd
    def _(g):
        ns.wo_ld = ctx.dgroup(g, "g", [
            (ns.wob[:, p * D:(p + 1) * D], ns.I_wo[128 * p:128 * (p + 1), :])
            for p in range(8)], "lg")

    for ck in range(NCK):
        @block.sync
        def _(q, ck=ck):
            ctx.dwait(q, "q", ns.cc_orw)
            ctx.wait(q, "q", "v", getattr(ns, "og_free_v", 0))
            prs = []
            for p in range(8):
                prs.append((ns.ogh0[:, p * 512:(p + 1) * 512],
                            ns.ORW_F[128 * p:128 * (p + 1),
                                     ck * 512:ck * 512 + 512]))
                prs.append((ns.ogh1[:, p * 512:(p + 1) * 512],
                            ns.ORW_F[128 * p:128 * (p + 1),
                                     TO + ck * 512:TO + ck * 512 + 512]))
            ns.og_ld = ctx.dgroup(q, "q", prs, "lq")

        @block.vector
        def _(v, ck=ck):
            v = ctx.ch("v", v)
            ctx.dwait(v, "v", ns.og_ld)
            ctx.wait(v, "v", "t", getattr(ns, "og_free", 0))
            for p in range(8):
                v.tensor_scalar(out=ns.ogg[:, p * 512:(p + 1) * 512],
                                in0=ns.ogh0[:, p * 512:(p + 1) * 512],
                                scalar1=ns.mab[:, 0:1], scalar2=None,
                                op0=ALU.mult)
                v.scalar_tensor_tensor(out=ns.ogg[:, p * 512:(p + 1) * 512],
                                       in0=ns.ogh1[:, p * 512:(p + 1) * 512],
                                       scalar=ns.mab[:, 1:2],
                                       in1=ns.ogg[:, p * 512:(p + 1) * 512],
                                       op0=ALU.mult, op1=ALU.add)
            ns.og_free_v = ctx.cnt["v"]
        v_og = ctx.cnt["v"]

        @block.sync
        def _(q, ck=ck):
            ctx.wait(q, "q", "v", getattr(ns, "x0w_free", 0))
            ns.x0w_ld = ctx.dgroup(q, "q", [
                (ns.x0w[:, p * 512:(p + 1) * 512],
                 ns.I_xo[128 * p:128 * (p + 1), ck * 512:ck * 512 + 512])
                for p in range(8)], "lq")

        for half in range(2):
            @block.tensor
            def _(t, ck=ck, half=half):
                ctx.wait(t, "t", "v", v_og)
                ctx.dwait(t, "t", ns.wo_ld)
                for k in range(8):
                    for mi in range(4):
                        mt = half * 4 + mi
                        bk = bank[mi]
                        bk.mm(t, bk.h[:, 0:512],
                              ns.wob[:, k * D + mt * 128:k * D + (mt + 1) * 128],
                              ns.ogg[:, k * 512:(k + 1) * 512],
                              start=(k == 0), stop=(k == 7))
                ns.og_free = ctx.cnt["t"]

            @block.vector
            def _(v, ck=ck, half=half):
                v = ctx.ch("v", v)
                ctx.wait(v, "v", "t", ctx.cnt["t"])
                ctx.dwait(v, "v", ns.x0w_ld)
                for mi in range(4):
                    mt = half * 4 + mi
                    bk = bank[mi]
                    v.tensor_tensor(
                        out=ns.xa[:, mt * TO + ck * 512:mt * TO + ck * 512 + 512],
                        in0=bk.h[:, 0:512],
                        in1=ns.x0w[:, mt * 512:(mt + 1) * 512],
                        op=ALU.add)
                    bk.rd("v")
                ns.x0w_free = ctx.cnt["v"]
    emit_phase_join(ns)
    es3.close()
    ns.og_ld = None
    if ns.stop_after == "wo":
        ns.x2buf = ns.xa
        emit_phase_join(ns)
        ns.es2x.close()
        return

    es4 = ExitStack()
    sb("h2c", (128, 8 * TO), BF16, stack=es4)
    sb("wf1h", (128, 8 * 2048), BF16, stack=es4)
    sb("wf2h", (128, 16 * D), BF16, stack=es4)
    sb("hallh", (128, 16 * 256), BF16, stack=es4)

    ns.wf_ld = None
    ns.wf_free = 0
    for ck in range(NCK):
        emit_colstats(ns, ns.xa, ck * 512, 512, "rms", EPS_RMS,
                      scale=ns.v_ln2w, out=ns.h2c, bufW=TO, outW=TO,
                      ooff=ck * 512)
    v_h2 = ctx.cnt["v"]
    for hib2 in range(2):
        @block.gpsimd
        def _(g, hib2=hib2):
            g = ctx.ch("g", g)
            ctx.wait(g, "g", "t", ns.wf_free)
            prs = []
            for p in range(8):
                prs.append((ns.wf1h[:, p * 2048:(p + 1) * 2048],
                            ns.I_w1[128 * p:128 * (p + 1),
                                    hib2 * 2048:(hib2 + 1) * 2048]))
            for kk in range(16):
                prs.append((ns.wf2h[:, kk * D:(kk + 1) * D],
                            ns.I_w2[hib2 * 2048 + 128 * kk:
                                    hib2 * 2048 + 128 * (kk + 1), :]))
            ns.wf_ld = ctx.dgroup(g, "g", prs, "lg")

        for ck in range(NCK):
            for tc2 in range(2):
                tco = ck * 512 + tc2 * 256
                for ib4 in range(4):
                    @block.tensor
                    def _(t, ib4=ib4, tco=tco):
                        ctx.wait(t, "t", "v", v_h2)
                        ctx.dwait(t, "t", ns.wf_ld)
                        for k in range(8):
                            for mi in range(4):
                                bk = bank[4 + mi]
                                woff = (ib4 * 4 + mi) * 128
                                bk.mm(t, bk.h[:, 0:256],
                                      ns.wf1h[:, k * 2048 + woff:
                                              k * 2048 + woff + 128],
                                      ns.h2c[:, k * TO + tco:
                                             k * TO + tco + 256],
                                      start=(k == 0), stop=(k == 7))
                    t_up = ctx.cnt["t"]

                    @block.scalar
                    def _(sc, ib4=ib4):
                        sc = ctx.ch("s", sc)
                        ctx.wait(sc, "s", "t", t_up)
                        for mi in range(4):
                            bk = bank[4 + mi]
                            sc.activation(
                                out=ns.hallh[:, (ib4 * 4 + mi) * 256:
                                             (ib4 * 4 + mi + 1) * 256],
                                in_=bk.h[:, 0:256], func=AF.Gelu)
                            bk.rd("s")
                s_hall = ctx.cnt["s"]

                for half in range(2):
                    @block.tensor
                    def _(t, ck=ck, tc2=tc2, half=half, s_hall=s_hall):
                        ctx.wait(t, "t", "s", s_hall)
                        for kk in range(16):
                            for mi in range(4):
                                mt = half * 4 + mi
                                bk = bank[mi]
                                bk.mm(t, bk.h[:, 0:256],
                                      ns.wf2h[:, kk * D + mt * 128:
                                              kk * D + (mt + 1) * 128],
                                      ns.hallh[:, kk * 256:(kk + 1) * 256],
                                      start=(kk == 0), stop=(kk == 15))
                        ns.wf_free = ctx.cnt["t"]

                    @block.vector
                    def _(v, ck=ck, tc2=tc2, half=half):
                        v = ctx.ch("v", v)
                        ctx.wait(v, "v", "t", ctx.cnt["t"])
                        for mi in range(4):
                            mt = half * 4 + mi
                            bk = bank[mi]
                            co = mt * TO + ck * 512 + tc2 * 256
                            v.tensor_tensor(out=ns.xa[:, co:co + 256],
                                            in0=bk.h[:, 0:256],
                                            in1=ns.xa[:, co:co + 256],
                                            op=ALU.add)
                            bk.rd("v")
    emit_phase_join(ns)
    es4.close()
    if ns.stop_after == "ffn":
        ns.x2buf = ns.xa
        emit_phase_join(ns)
        ns.es2x.close()
        return

    es5a = ExitStack()
    ns.x0c = ns.x0w
    for ck in range(NCK):
        @block.sync
        def _(q, ck=ck):
            ctx.wait(q, "q", "v", getattr(ns, "x0c_free", 0))
            ns.x0c_ld = ctx.dgroup(q, "q", [
                (ns.x0c[:, p * 512:(p + 1) * 512],
                 ns.I_xo[128 * p:128 * (p + 1), ck * 512:ck * 512 + 512])
                for p in range(8)], "lq")

        @block.vector
        def _(v, ck=ck):
            v = ctx.ch("v", v)
            ctx.dwait(v, "v", ns.x0c_ld)
            for p in range(8):
                co = p * TO + ck * 512
                v.tensor_tensor(out=ns.xa[:, co:co + 512],
                                in0=ns.x0c[:, p * 512:(p + 1) * 512],
                                in1=ns.xa[:, co:co + 512], op=ALU.add)
            ns.x0c_free = ctx.cnt["v"]
        emit_colstats(ns, ns.xa, ck * 512, 512, "ln", EPS_LN, bufW=TO)
    emit_phase_join(ns)
    es5a.close()
    if ns.stop_after == "x1":
        ns.x2buf = ns.xa
        emit_phase_join(ns)
        ns.es2x.close()
        return

    es5 = ExitStack()
    sb("wqf", (128, 8 * D), F32, stack=es5)
    sb("wqb", (128, 8 * D), BF16, stack=es5)
    sb("qsc", (128, 1), F32, stack=es5)
    sb("x1b", (128, 8 * 512), BF16, stack=es5)

    @block.sync
    def _(q):
        ns.wq_ld = ctx.dgroup(q, "q", [
            (ns.wqf[:, p * D:(p + 1) * D], ns.I_wqt[128 * p:128 * (p + 1), :])
            for p in range(8)], "lq")

    @block.vector
    def _(v):
        v = ctx.ch("v", v)
        ctx.dwait(v, "v", ns.wq_ld)
        v.tensor_reduce(out=ns.psq[:, 0:1], in_=ns.wqf[:, :], axis=AX.X,
                        op=ALU.add, apply_absolute_value=True)
    v_abs = ctx.cnt["v"]

    @block.tensor
    def _(t):
        ctx.wait(t, "t", "v", v_abs)
        bank[6].mm(t, bank[6].h[0:1, 0:1], ns.onesr[:, :], ns.psq[:, 0:1],
                   start=True, stop=True)
    t_abs = ctx.cnt["t"]

    @block.vector
    def _(v):
        v = ctx.ch("v", v)
        ctx.wait(v, "v", "t", t_abs)
        v.tensor_scalar(out=ns.prow1[0:1, 0:1], in0=bank[6].h[0:1, 0:1],
                        scalar1=1.0 / (D * D), scalar2=None, op0=ALU.mult)
        bank[6].rd("v")
        v.tensor_scalar_max(out=ns.prow1[0:1, 0:1], in0=ns.prow1[0:1, 0:1],
                            scalar1=1e-5)
    v_sc = ctx.cnt["v"]

    @block.tensor
    def _(t):
        ctx.wait(t, "t", "v", v_sc)
        bank[6].mm(t, bank[6].h[:, 0:1], ns.onesc[:, :], ns.prow1[0:1, 0:1],
                   start=True, stop=True)
    t_scbc = ctx.cnt["t"]

    @block.vector
    def _(v):
        v = ctx.ch("v", v)
        ctx.wait(v, "v", "t", t_scbc)
        v.tensor_copy(out=ns.qsc[:, :], in_=bank[6].h[:, 0:1])
        bank[6].rd("v")
        v.reciprocal(out=ns.qsc[:, :], in_=ns.qsc[:, :])
        v.tensor_single_scalar(out=ns.wqf[:, :], in_=ns.wqf[:, :],
                               scalar=ns.qsc[:, :], op=ALU.mult)
        v.tensor_single_scalar(out=ns.wqb[:, :], in_=ns.wqf[:, :], scalar=0.5,
                               op=ALU.is_gt)
        v.tensor_single_scalar(out=ns.wqf[:, :], in_=ns.wqf[:, :], scalar=-0.5,
                               op=ALU.is_lt)
        v.tensor_tensor(out=ns.wqf[:, :], in0=ns.wqb[:, :], in1=ns.wqf[:, :],
                        op=ALU.subtract)
        v.reciprocal(out=ns.qsc[:, :], in_=ns.qsc[:, :])
        v.tensor_single_scalar(out=ns.wqb[:, :], in_=ns.wqf[:, :],
                               scalar=ns.qsc[:, :], op=ALU.mult)
    v_wq = ctx.cnt["v"]

    for ck in range(NCK):
        @block.vector
        def _(v, ck=ck):
            v = ctx.ch("v", v)
            ctx.wait(v, "v", "t", ctx.cnt["t"])
            for p in range(8):
                v.tensor_copy(out=ns.x1b[:, p * 512:(p + 1) * 512],
                              in_=ns.xa[:, p * TO + ck * 512:p * TO + ck * 512 + 512])
        v_x1b = ctx.cnt["v"]
        for half in range(2):
            @block.tensor
            def _(t, ck=ck, half=half, v_x1b=v_x1b):
                ctx.wait(t, "t", "v", max(v_wq, v_x1b))
                for k in range(8):
                    for mi in range(4):
                        mt = half * 4 + mi
                        bk = bank[mi]
                        bk.mm(t, bk.h[:, 0:512],
                              ns.wqb[:, k * D + mt * 128:k * D + (mt + 1) * 128],
                              ns.x1b[:, k * 512:(k + 1) * 512],
                              start=(k == 0), stop=(k == 7))

            @block.vector
            def _(v, ck=ck, half=half):
                v = ctx.ch("v", v)
                ctx.wait(v, "v", "t", ctx.cnt["t"])
                for mi in range(4):
                    mt = half * 4 + mi
                    bk = bank[mi]
                    co = mt * TO + ck * 512
                    v.tensor_tensor(out=ns.xa[:, co:co + 512],
                                    in0=bk.h[:, 0:512],
                                    in1=ns.xa[:, co:co + 512], op=ALU.add)
                    bk.rd("v")
    emit_phase_join(ns)
    es5.close()

    for ck in range(NCK):
        emit_colstats(ns, ns.xa, ck * 512, 512, "ln", EPS_LN, bufW=TO)
    ns.x2buf = ns.xa
    ns.x2_done = ctx.cnt["v"]
    emit_phase_join(ns)
    ns.es2x.close()


def emit_l1attn(ns):
    ctx, block, bank, nc, sb = ns.ctx, ns.block, ns.bank, ns.nc, ns.sb
    NCK = TO // 512

    esx = ExitStack()
    sb("x2c", (128, 8 * 512), BF16, stack=esx)
    for ck in range(NCK):
        @block.vector
        def _(v, ck=ck):
            v = ctx.ch("v", v)
            ctx.dwait(v, "v", getattr(ns, "x2c_st", None))
            for p in range(8):
                v.tensor_copy(out=ns.x2c[:, p * 512:(p + 1) * 512],
                              in_=ns.xa[:, p * TO + ck * 512:p * TO + ck * 512 + 512])
        vv = ctx.cnt["v"]

        @block.sync
        def _(q, ck=ck, vv=vv):
            ctx.wait(q, "q", "v", vv)
            ns.x2c_st = ctx.dgroup(q, "q", [
                (ns.X2B[128 * p:128 * (p + 1), ck * 512:ck * 512 + 512],
                 ns.x2c[:, p * 512:(p + 1) * 512]) for p in range(8)], "sq")
            ns.x2_sts = getattr(ns, "x2_sts", [])
            ns.x2_sts.append(ns.x2c_st)
    emit_phase_join(ns)
    esx.close()

    @block.gpsimd
    def _(g):
        for key in ns.x2_sts:
            ctx.dwait(g, "g", key)
        ns.cc_x2 = ctx.ccg(g, "AllGather", [ns.X2B], [ns.X2F], ns.groups)

    es6 = ExitStack()
    sb("qb2", (128, 4 * TF), BF16, stack=es6)
    sb("kb2", (128, 4 * TF), BF16, stack=es6)
    sb("xst", (128, 8 * 512), BF16, stack=es6)

    es6q = ExitStack()
    sb("sinc", (128, 512), F32, stack=es6q)
    sb("cosc", (128, 512), F32, stack=es6q)
    sb("wl1", (128, 8 * FH), BF16, stack=es6q)
    sb("wl2", (128, 8 * FH), BF16, stack=es6q)
    sb("qraw", (128, 4 * 512), F32, stack=es6q)
    sb("rope_perm", (128, 128), F32, stack=es6q)
    sb("v_l1bq", (128, 4), F32, stack=es6q)
    sb("v_l1bk", (128, 4), F32, stack=es6q)

    @block.gpsimd
    def _(g):
        g = ctx.ch("g", g)
        prs = []
        for nm, buf in (("l1wq", ns.wl1), ("l1wk", ns.wl2)):
            srcd = getattr(ns, "I_" + nm)
            for p in range(8):
                prs.append((buf[:, p * FH:(p + 1) * FH],
                            srcd[128 * p:128 * (p + 1), :]))
        prs.append((ns.rope_perm[:, :], ns.I_rope_perm))
        prs.append((ns.v_l1bq[:, :], ns.I_l1bq))
        prs.append((ns.v_l1bk[:, :], ns.I_l1bk))
        ns.l1w_ld = ctx.dgroup(g, "g", prs, "lg")

    for ck8 in range(TF // 512):
        blk = ck8 // 4
        co = (ck8 % 4) * 512

        @block.sync
        def _(q, ck8=ck8, blk=blk, co=co):
            ctx.dwait(q, "q", ns.cc_x2)
            ctx.wait(q, "q", "t", getattr(ns, "xst_free", 0))
            ns.xst_ld = ctx.dgroup(q, "q", [
                (ns.xst[:, p * 512:(p + 1) * 512],
                 ns.X2F[blk * D + 128 * p:blk * D + 128 * (p + 1),
                        co:co + 512]) for p in range(8)], "lq")

        @block.sync
        def _(q, ck8=ck8):
            ctx.wait(q, "q", "v", getattr(ns, "tab_free", 0))
            ns.tab_ld = ctx.dgroup(q, "q", [
                (ns.sinc[:, :], ns.I_ropesin[:, ck8 * 512:(ck8 + 1) * 512]),
                (ns.cosc[:, :], ns.I_ropecos[:, ck8 * 512:(ck8 + 1) * 512])],
                "lq")
        if ns.dbg_rope and ck8 == 7:
            @block.sync
            def _(q):
                ctx.dwait(q, "q", ns.tab_ld)
                ns.dbg1 = ctx.dgroup(q, "q", [
                    (ns.DBG_SC[:, 0:512], ns.sinc[:, :]),
                    (ns.DBG_SC[:, 512:1024], ns.cosc[:, :])], "sq")

        for which, wbuf, bvec in (("q", ns.wl1, ns.v_l1bq),
                                  ("k", ns.wl2, ns.v_l1bk)):
            @block.tensor
            def _(t, ck8=ck8, wbuf=wbuf):
                ctx.dwait(t, "t", ns.xst_ld)
                ctx.dwait(t, "t", ns.l1w_ld)
                ctx.wait(t, "t", "v", ctx.cnt["v"])
                for k in range(8):
                    for mt in range(4):
                        bk = bank[mt]
                        bk.mm(t, bk.h[:, 0:512],
                              wbuf[:, k * FH + mt * 128:k * FH + (mt + 1) * 128],
                              ns.xst[:, k * 512:(k + 1) * 512],
                              start=(k == 0), stop=(k == 7))
                ns.xst_free = ctx.cnt["t"]
            t_p = ctx.cnt["t"]

            @block.vector
            def _(v, ck8=ck8, which=which, bvec=bvec, t_p=t_p):
                v = ctx.ch("v", v)
                ctx.wait(v, "v", "t", t_p)
                for mt in range(4):
                    bk = bank[mt]
                    v.tensor_scalar(out=ns.qraw[:, mt * 512:(mt + 1) * 512],
                                    in0=bk.h[:, 0:512],
                                    scalar1=bvec[:, mt:mt + 1], scalar2=None,
                                    op0=ALU.add)
                    bk.rd("v")
            v_raw = ctx.cnt["v"]

            @block.tensor
            def _(t, ck8=ck8, which=which, v_raw=v_raw):
                ctx.wait(t, "t", "v", v_raw)
                for mt in range(4):
                    bk = bank[4 + mt % 2]
                    bk.mm(t, bk.h[:, 0:512], ns.rope_perm[:, :],
                          ns.qraw[:, mt * 512:(mt + 1) * 512],
                          start=True, stop=True)

                    @block.vector
                    def _(v, bk=bk, mt=mt, ck8=ck8, which=which):
                        v = ctx.ch("v", v)
                        ctx.wait(v, "v", "t", ctx.cnt["t"])
                        dst = (ns.qb2 if which == "q" else ns.kb2)[
                            :, mt * TF + ck8 * 512:mt * TF + ck8 * 512 + 512]
                        ctx.dwait(v, "v", ns.tab_ld)
                        v.tensor_tensor(out=ns.ptt[:, 0:512],
                                        in0=bk.h[:, 0:512], in1=ns.sinc[:, :],
                                        op=ALU.mult)
                        bk.rd("v")
                        v.tensor_tensor(out=ns.psq[:, 0:512],
                                        in0=ns.qraw[:, mt * 512:(mt + 1) * 512],
                                        in1=ns.cosc[:, :], op=ALU.mult)
                        v.tensor_tensor(out=dst, in0=ns.ptt[:, 0:512],
                                        in1=ns.psq[:, 0:512], op=ALU.add)
        ns.tab_free = ctx.cnt["v"]
    v_qkv = ctx.cnt["v"]
    if ns.dbg_rope:
        @block.sync
        def _(q):
            ctx.wait(q, "q", "v", ctx.cnt["v"])
            ns.dbg2 = ctx.dgroup(q, "q", [
                (ns.DBG_QK[:, 0:4 * TF], ns.qb2[:, :]),
                (ns.DBG_QK[:, 4 * TF:8 * TF], ns.kb2[:, :])], "sq")
    emit_phase_join(ns)
    es6q.close()


    es6z = ExitStack()
    sb("vtm2", (128, (TF // 128) * 520), BF16, stack=es6z)

    es6v = ExitStack()
    sb("wl3", (128, 8 * FH), BF16, stack=es6v)
    sb("onesbf", (1, 128), BF16, stack=es6v)
    sb("bvrow", (1, FH), BF16, stack=es6v)

    @block.gpsimd
    def _(g):
        prs = [(ns.wl3[:, p * FH:(p + 1) * FH],
                ns.I_l1wv[128 * p:128 * (p + 1), :]) for p in range(8)]
        prs.append((ns.bvrow[:, :], ns.I_l1bv_row))
        ns.l1v_ld = ctx.dgroup(g, "g", prs, "lg")

    @block.vector
    def _(v):
        v = ctx.ch("v", v)
        v.memset(ns.onesbf[:, :], 1.0)
        full = ns.vtm2[:, :]
        pitch = full.ap[0][0]
        for h in range(NHL):
            onescols = bass.AP(full.tensor, full.offset + 65 * h + 64,
                               [[pitch, 128], [520, TF // 128], [1, 1]])
            v.memset(onescols, 1.0)
    v_ones = ctx.cnt["v"]

    ns.xst_free = 0
    for ck8 in range(TF // 512):
        blk = ck8 // 4
        co = (ck8 % 4) * 512

        @block.sync
        def _(q, ck8=ck8, blk=blk, co=co):
            ctx.dwait(q, "q", ns.cc_x2)
            ctx.wait(q, "q", "t", ns.xst_free)
            ns.xst_ld = ctx.dgroup(q, "q", [
                (ns.xst[:, p * 512:(p + 1) * 512],
                 ns.X2F[blk * D + 128 * p:blk * D + 128 * (p + 1),
                        co:co + 512]) for p in range(8)], "lq")

        for sub in range(4):
            @block.tensor
            def _(t, ck8=ck8, sub=sub):
                ctx.dwait(t, "t", ns.xst_ld)
                ctx.dwait(t, "t", ns.l1v_ld)
                ctx.wait(t, "t", "v", max(v_ones, ctx.cnt["v"]))
                bk = bank[6 + sub % 2]
                for k in range(8):
                    bk.mm(t, bk.h[:, 0:512],
                          ns.xst[:, k * 512 + sub * 128:k * 512 + sub * 128 + 128],
                          ns.wl3[:, k * FH:(k + 1) * FH],
                          start=(k == 0), stop=False)
                bk.mm(t, bk.h[:, 0:512], ns.onesbf[:, 0:128],
                      ns.bvrow[:, :], start=False, stop=True)
                if sub == 3:
                    ns.xst_free = ctx.cnt["t"]

                @block.vector
                def _(v, bk=bk, ck8=ck8, sub=sub):
                    v = ctx.ch("v", v)
                    ctx.wait(v, "v", "t", ctx.cnt["t"])
                    tokb = ck8 * 4 + sub
                    full = ns.vtm2[:, :]
                    pitch = full.ap[0][0]
                    dst = bass.AP(full.tensor, full.offset + tokb * 520,
                                  [[pitch, 128], [65, 8], [1, 64]])
                    v.tensor_copy(out=dst,
                                  in_=bk.h[:, 0:512].rearrange(
                                      "p (h c) -> p h c", c=64))
                    bk.rd("v")
    emit_phase_join(ns)
    es6v.close()

    es6d = ExitStack()
    sb("pt", (128, 512), BF16, stack=es6d)
    sb("pt2", (128, 512), BF16, stack=es6d)
    sb("pt3", (128, 512), BF16, stack=es6d)
    sb("pt4", (128, 512), BF16, stack=es6d)
    sb("orow", (1, 512), F32, stack=es6d)
    sb("ost", (128, 512), BF16, stack=es6d)
    sb("expb", (128, 1), F32, stack=es6d)
    sb("cmask", (128, 4 * 512), BF16, stack=es6d)

    @block.vector
    def _(v):
        v = ctx.ch("v", v)
        v.memset(ns.expb[:, :], -30.0)
    v_expb = ctx.cnt["v"]

    @block.sync
    def _(q):
        ns.cm_ld = ctx.dgroup(q, "q", [(ns.cmask[:, :], ns.I_cmask)], "lq")
    ns.ptfree = {0: 0, 1: 0, 2: 0, 3: 0}

    PTS = (None,)
    for h in range(NHL):
        mt, hr = h // 2, 64 * (h % 2)
        for qt in range(TF // 512):
            nkc = 4 * (qt + 1)
            ob = bank[0] if (h * 8 + qt) % 2 == 0 else bank[4]
            gate = {}
            for kc in range(nkc + 3):
                if kc < nkc:
                    sbnk = bank[1 + kc % 3]
                    ptb = (ns.pt, ns.pt2, ns.pt3, ns.pt4)[kc % 4]

                    @block.tensor
                    def _(t, h=h, qt=qt, kc=kc, sbnk=sbnk, mt=mt, hr=hr):
                        ctx.wait(t, "t", "v", v_qkv)
                        sbnk.mm(t, sbnk.h[:, 0:512],
                                ns.kb2[hr:hr + 64,
                                       mt * TF + kc * 128:
                                       mt * TF + (kc + 1) * 128],
                                ns.qb2[hr:hr + 64,
                                       mt * TF + qt * 512:
                                       mt * TF + (qt + 1) * 512],
                                start=True, stop=True)
                    t_sc = ctx.cnt["t"]

                    @block.scalar
                    def _(sc, kc=kc, sbnk=sbnk, t_sc=t_sc, ptb=ptb):
                        sc = ctx.ch("s", sc)
                        ctx.wait(sc, "s", "t", t_sc)
                        ctx.wait(sc, "s", "t", ns.ptfree[kc % 4])
                        ctx.wait(sc, "s", "v", v_expb)
                        sc.activation(out=ptb[:, :], in_=sbnk.h[:, 0:512],
                                      func=AF.Exp, scale=0.125,
                                      bias=ns.expb[:, 0:1])
                        sbnk.rd("s")
                    s_exp = ctx.cnt["s"]

                    if kc >= 4 * qt:
                        @block.vector
                        def _(v, qt=qt, kc=kc, s_exp=s_exp, ptb=ptb):
                            v = ctx.ch("v", v)
                            ctx.wait(v, "v", "s", s_exp)
                            ctx.dwait(v, "v", ns.cm_ld)
                            j = kc - 4 * qt
                            v.tensor_tensor(
                                out=ptb[:, :], in0=ptb[:, :],
                                in1=ns.cmask[:, j * 512:(j + 1) * 512],
                                op=ALU.mult)
                        gate[kc] = ("v", ctx.cnt["v"])
                    else:
                        gate[kc] = ("s", s_exp)

                pv = kc - 3
                if pv >= 0:
                    ptb_pv = (ns.pt, ns.pt2, ns.pt3, ns.pt4)[pv % 4]

                    @block.tensor
                    def _(t, h=h, pv=pv, nkc=nkc, ob=ob, ptb_pv=ptb_pv,
                          g8=gate[pv]):
                        ctx.wait(t, "t", g8[0], g8[1])
                        ob.mm(t, ob.h[0:65, 0:512],
                              ns.vtm2[:, pv * 520 + 65 * h:
                                      pv * 520 + 65 * h + 65],
                              ptb_pv[:, :], start=(pv == 0),
                              stop=(pv == nkc - 1))
                        ns.ptfree[pv % 4] = ctx.cnt["t"]

            @block.vector
            def _(v, h=h, qt=qt, ob=ob):
                v = ctx.ch("v", v)
                ctx.wait(v, "v", "t", ctx.cnt["t"])
                v.reciprocal(out=ns.orow[:, :], in_=ob.h[64:65, 0:512])
            v_r = ctx.cnt["v"]

            @block.tensor
            def _(t, h=h, qt=qt, v_r=v_r):
                ctx.wait(t, "t", "v", v_r)
                bank[5].mm(t, bank[5].h[0:64, 0:512], ns.onesc[:, 0:64],
                           ns.orow[:, :], start=True, stop=True)
            t_bc = ctx.cnt["t"]

            @block.scalar
            def _(sc, h=h, qt=qt, t_bc=t_bc):
                sc = ctx.ch("s", sc)
                ctx.wait(sc, "s", "t", t_bc)
                sc.copy(out=ns.ptt[0:64, 0:512], in_=bank[5].h[0:64, 0:512])
                bank[5].rd("s")
            s_bc = ctx.cnt["s"]

            @block.vector
            def _(v, h=h, qt=qt, ob=ob, s_bc=s_bc):
                v = ctx.ch("v", v)
                ctx.wait(v, "v", "s", s_bc)
                ctx.dwait(v, "v", getattr(ns, "oat_st", None))
                v.tensor_tensor(out=ns.ost[0:64, :], in0=ob.h[0:64, 0:512],
                                in1=ns.ptt[0:64, 0:512], op=ALU.mult)
                ob.rd("v")
            v_o = ctx.cnt["v"]

            @block.sync
            def _(q, h=h, qt=qt, v_o=v_o):
                ctx.wait(q, "q", "v", v_o)
                ns.oat_st = ctx.dgroup(q, "q", [
                    (ns.OAT_B[64 * h:64 * h + 64, qt * 512:(qt + 1) * 512],
                     ns.ost[0:64, :])], "sq")

    emit_phase_join(ns)
    es6d.close()
    es6z.close()
    es6.close()

    @block.gpsimd
    def _(g):
        for nm, val in ctx.dcnt.items():
            ctx.wait(g, "g", nm, val)
        ns.cc_oat = ctx.ccg(g, "AllGather", [ns.OAT_B], [ns.OAT_F], ns.groups)

    es7 = ExitStack()
    sb("wob2", (128, 8 * D), BF16, stack=es7)
    sb("v_l1bo", (128, 8), F32, stack=es7)
    sb("ogh0b", (128, 8 * 512), BF16, stack=es7)
    sb("ogh1b", (128, 8 * 512), BF16, stack=es7)
    sb("oggb", (128, 8 * 512), BF16, stack=es7)

    @block.gpsimd
    def _(g):
        ns.wo2_ld = ctx.dgroup(g, "g", [
            (ns.wob2[:, p * D:(p + 1) * D],
             ns.I_l1wo[128 * p:128 * (p + 1), :]) for p in range(8)] +
            [(ns.v_l1bo[:, :], ns.I_l1bo)], "lg")

    for ck in range(NCK):
        @block.sync
        def _(q, ck=ck):
            ctx.dwait(q, "q", ns.cc_oat)
            ctx.wait(q, "q", "v", getattr(ns, "og2_free_v", 0))
            prs = []
            for p in range(8):
                prs.append((ns.ogh0b[:, p * 512:(p + 1) * 512],
                            ns.OAT_F[128 * p:128 * (p + 1),
                                     ck * 512:ck * 512 + 512]))
                prs.append((ns.ogh1b[:, p * 512:(p + 1) * 512],
                            ns.OAT_F[128 * p:128 * (p + 1),
                                     TO + ck * 512:TO + ck * 512 + 512]))
            ns.og2_ld = ctx.dgroup(q, "q", prs, "lq")

        @block.vector
        def _(v, ck=ck):
            v = ctx.ch("v", v)
            ctx.dwait(v, "v", ns.og2_ld)
            ctx.wait(v, "v", "t", getattr(ns, "og2_free", 0))
            for p in range(8):
                v.tensor_scalar(out=ns.oggb[:, p * 512:(p + 1) * 512],
                                in0=ns.ogh0b[:, p * 512:(p + 1) * 512],
                                scalar1=ns.mab[:, 0:1], scalar2=None,
                                op0=ALU.mult)
                v.scalar_tensor_tensor(out=ns.oggb[:, p * 512:(p + 1) * 512],
                                       in0=ns.ogh1b[:, p * 512:(p + 1) * 512],
                                       scalar=ns.mab[:, 1:2],
                                       in1=ns.oggb[:, p * 512:(p + 1) * 512],
                                       op0=ALU.mult, op1=ALU.add)
            ns.og2_free_v = ctx.cnt["v"]
        v_og2 = ctx.cnt["v"]

        for half in range(2):
            @block.tensor
            def _(t, ck=ck, half=half, v_og2=v_og2):
                ctx.wait(t, "t", "v", v_og2)
                ctx.dwait(t, "t", ns.wo2_ld)
                for k in range(8):
                    for mi in range(4):
                        mt2 = half * 4 + mi
                        bk = bank[mi]
                        bk.mm(t, bk.h[:, 0:512],
                              ns.wob2[:, k * D + mt2 * 128:k * D + (mt2 + 1) * 128],
                              ns.oggb[:, k * 512:(k + 1) * 512],
                              start=(k == 0), stop=(k == 7))
                ns.og2_free = ctx.cnt["t"]

            @block.vector
            def _(v, ck=ck, half=half):
                v = ctx.ch("v", v)
                ctx.wait(v, "v", "t", ctx.cnt["t"])
                for mi in range(4):
                    mt2 = half * 4 + mi
                    bk = bank[mi]
                    co = mt2 * TO + ck * 512
                    v.tensor_scalar(out=ns.ptt[:, 0:512], in0=bk.h[:, 0:512],
                                    scalar1=ns.v_l1bo[:, mt2:mt2 + 1],
                                    scalar2=None, op0=ALU.add)
                    bk.rd("v")
                    v.tensor_tensor(out=ns.xa[:, co:co + 512],
                                    in0=ns.ptt[:, 0:512],
                                    in1=ns.xa[:, co:co + 512], op=ALU.add)
        emit_colstats(ns, ns.xa, ck * 512, 512, "ln", EPS_LN, bufW=TO)
    emit_phase_join(ns)
    es7.close()
    ns.x2buf = ns.xa



def emit_moe(ns):
    ctx, block, bank, nc, sb = ns.ctx, ns.block, ns.bank, ns.nc, ns.sb
    NCK = TO // 512
    es8 = ns.es.enter_context(ExitStack())
    sb("acc", (128, 8 * TO), F32, stack=es8)
    sb("x3b", (128, 8 * TO), BF16, stack=es8)
    sb("cfs", (16, TO), BF16, stack=es8)
    sb("sqeps", (16, 1), F32, stack=es8)
    sb("inde16", (16, 16 * 128), BF16, stack=es8)
    es8r = ExitStack()
    sb("gwt", (128, 8 * 16), F32, stack=es8r)
    sb("cmb", (128, 16 * 16), F32, stack=es8r)
    sb("bebc", (128, 16), F32, stack=es8r)
    sb("mrow", (128, 2), F32, stack=es8r)
    sb("rsc1", (128, 16), F32, stack=es8r)
    sb("rsc2", (128, 16), F32, stack=es8r)
    sb("identf", (128, 128), F32, stack=es8r)
    sb("berow", (1, 16), F32, stack=es8r)

    @block.vector
    def _(v):
        v = ctx.ch("v", v)
        v.memset(ns.sqeps[:, :], 1e-30)
        for p in range(8):
            v.tensor_copy(out=ns.x3b[:, p * TO:(p + 1) * TO],
                          in_=ns.xa[:, p * TO:(p + 1) * TO])
    v_x3b = ctx.cnt["v"]

    @block.sync
    def _(q):
        ns.gw_ld = ctx.dgroup(q, "q", [
            (ns.gwt[:, p * 16:(p + 1) * 16],
             ns.I_gate_wt[128 * p:128 * (p + 1), :]) for p in range(8)] +
            [(ns.identf[:, :], ns.I_identf), (ns.inde16[:, :], ns.I_inde16),
             (ns.berow[:, :], ns.I_bias_e)], "lq")

    @block.tensor
    def _(t):
        ctx.dwait(t, "t", ns.gw_ld)
        bank[6].mm(t, bank[6].h[:, 0:16], ns.onesc[:, :], ns.berow[:, :],
                   start=True, stop=True)
    t_be = ctx.cnt["t"]

    @block.vector
    def _(v):
        v = ctx.ch("v", v)
        ctx.wait(v, "v", "t", t_be)
        v.tensor_copy(out=ns.bebc[:, :], in_=bank[6].h[:, 0:16])
        bank[6].rd("v")
    v_be = ctx.cnt["v"]

    for ck in range(NCK):
        @block.tensor
        def _(t, ck=ck):
            ctx.dwait(t, "t", ns.gw_ld)
            ctx.wait(t, "t", "v", ctx.cnt["v"])
            for k in range(8):
                bank[7].mm(t, bank[7].h[0:16, 0:512],
                           ns.gwt[:, k * 16:(k + 1) * 16],
                           ns.xa[:, k * TO + ck * 512:k * TO + ck * 512 + 512],
                           start=(k == 0), stop=(k == 7))
        t_lg = ctx.cnt["t"]

        @block.vector
        def _(v, ck=ck):
            v = ctx.ch("v", v)
            ctx.wait(v, "v", "t", t_lg)
            v.tensor_copy(out=ns.psq[0:16, 0:512], in_=bank[7].h[0:16, 0:512])
            bank[7].rd("v")
        v_lg = ctx.cnt["v"]

        for sub in range(4):
            ti = ck * 4 + sub

            @block.tensor
            def _(t, ck=ck, sub=sub):
                ctx.wait(t, "t", "v", v_lg)
                ctx.dwait(t, "t", ns.gw_ld)
                bk = bank[4 + sub % 2]
                bk.tr(t, bk.h[:, 0:16], ns.psq[0:16, sub * 128:(sub + 1) * 128],
                      ns.identf[0:16, 0:16])
            t_tr = ctx.cnt["t"]

            @block.scalar
            def _(sc, ti=ti):
                sc = ctx.ch("s", sc)
                ctx.wait(sc, "s", "t", t_tr)
                bk = bank[4 + ti % 2]
                sc.activation(out=ns.cmb[:, ti * 16:(ti + 1) * 16],
                              in_=bk.h[:, 0:16], func=AF.Sigmoid)
                bk.rd("s")
            s_sig = ctx.cnt["s"]

            @block.vector
            def _(v, ti=ti):
                v = ctx.ch("v", v)
                ctx.wait(v, "v", "s", s_sig)
                ctx.wait(v, "v", "v", v_be)
                sco = ns.cmb[:, ti * 16:(ti + 1) * 16]
                sc_ = ns.rsc1[:, :]
                v.tensor_tensor(out=sc_, in0=sco, in1=ns.bebc[:, :], op=ALU.add)
                g4 = lambda a, i: bass.AP(a[:, :].tensor, a[:, :].offset + i,
                                          [[a[:, :].ap[0][0], 128], [4, 4]])
                pairs = [(0, 1), (0, 2), (0, 3), (1, 2), (1, 3), (2, 3)]
                first = True
                for (i, j) in pairs:
                    v.tensor_tensor(out=ns.rsc2[:, 0:4], in0=g4(ns.rsc1, i),
                                    in1=g4(ns.rsc1, j), op=ALU.min)
                    v.tensor_tensor(out=ns.rsc2[:, 4:8], in0=g4(ns.rsc1, i),
                                    in1=g4(ns.rsc1, j), op=ALU.add)
                    if first:
                        v.tensor_copy(out=ns.rsc2[:, 8:12], in_=ns.rsc2[:, 0:4])
                        v.tensor_copy(out=ns.rsc2[:, 12:16], in_=ns.rsc2[:, 4:8])
                        first = False
                    else:
                        v.tensor_tensor(out=ns.rsc2[:, 8:12],
                                        in0=ns.rsc2[:, 8:12],
                                        in1=ns.rsc2[:, 0:4], op=ALU.max)
                        v.tensor_tensor(out=ns.rsc2[:, 12:16],
                                        in0=ns.rsc2[:, 12:16],
                                        in1=ns.rsc2[:, 4:8], op=ALU.max)
                v.tensor_reduce(out=ns.rsc2[:, 0:4],
                                in_=ns.rsc1[:, :].rearrange("p (g i) -> p g i", i=4),
                                axis=AX.X, op=ALU.max)
                gs = ns.rsc2[:, 12:16]
                first = True
                for (i, j) in pairs:
                    v.tensor_tensor(out=ns.rsc2[:, 0:1], in0=gs[:, i:i + 1],
                                    in1=gs[:, j:j + 1], op=ALU.min)
                    if first:
                        v.tensor_copy(out=ns.rsc2[:, 4:5], in_=ns.rsc2[:, 0:1])
                        first = False
                    else:
                        v.tensor_tensor(out=ns.rsc2[:, 4:5],
                                        in0=ns.rsc2[:, 4:5],
                                        in1=ns.rsc2[:, 0:1], op=ALU.max)
                v.tensor_scalar(out=ns.rsc2[:, 6:7], in0=ns.rsc2[:, 4:5],
                                scalar1=-1.0, scalar2=None, op0=ALU.mult)
                v.tensor_single_scalar(out=ns.rsc2[:, 8:12], in_=gs,
                                       scalar=ns.rsc2[:, 6:7], op=ALU.add)
                v.tensor_single_scalar(out=ns.rsc2[:, 8:12],
                                       in_=ns.rsc2[:, 8:12], scalar=0.0,
                                       op=ALU.is_ge)
                gselb = bass.AP(ns.rsc2[:, :].tensor,
                                ns.rsc2[:, :].offset + 8,
                                [[ns.rsc2[:, :].ap[0][0], 128], [1, 4], [0, 4]])
                v.tensor_tensor(out=ns.rsc1[:, :].rearrange("p (g i) -> p g i", i=4),
                                in0=ns.rsc1[:, :].rearrange("p (g i) -> p g i", i=4),
                                in1=gselb, op=ALU.mult)
                for _r in range(4):
                    v.tensor_reduce(out=ns.mrow[:, 0:1], in_=ns.rsc1[:, :],
                                    axis=AX.X, op=ALU.max)
                    v.tensor_scalar(out=ns.mrow[:, 1:2], in0=ns.mrow[:, 0:1],
                                    scalar1=-1.0, scalar2=None, op0=ALU.mult)
                    v.tensor_single_scalar(out=ns.rsc2[:, 0:16],
                                           in_=ns.rsc1[:, :],
                                           scalar=ns.mrow[:, 1:2],
                                           op=ALU.add)
                    v.tensor_single_scalar(out=ns.rsc2[:, 0:16],
                                           in_=ns.rsc2[:, 0:16], scalar=0.0,
                                           op=ALU.is_equal)
                    v.scalar_tensor_tensor(out=ns.rsc1[:, :],
                                           in0=ns.rsc2[:, 0:16],
                                           scalar=-1e30, in1=ns.rsc1[:, :],
                                           op0=ALU.mult, op1=ALU.add)
                v.tensor_single_scalar(out=ns.rsc2[:, 0:16], in_=ns.rsc1[:, :],
                                       scalar=-1e29, op=ALU.is_lt)
                v.tensor_tensor(out=sco, in0=sco, in1=ns.rsc2[:, 0:16],
                                op=ALU.mult)
                v.tensor_reduce(out=ns.mrow[:, 0:1], in_=sco, axis=AX.X,
                                op=ALU.add)
                v.tensor_scalar(out=ns.mrow[:, 0:1], in0=ns.mrow[:, 0:1],
                                scalar1=1e-20, scalar2=None, op0=ALU.add)
                v.reciprocal(out=ns.mrow[:, 0:1], in_=ns.mrow[:, 0:1])
                v.tensor_scalar(out=ns.mrow[:, 0:1], in0=ns.mrow[:, 0:1],
                                scalar1=2.5, scalar2=None, op0=ALU.mult)
                v.tensor_single_scalar(out=sco, in_=sco,
                                       scalar=ns.mrow[:, 0:1], op=ALU.mult)
    v_cmb = ctx.cnt["v"]

    for ti in range(16):
        @block.tensor
        def _(t, ti=ti):
            ctx.wait(t, "t", "v", v_cmb)
            ctx.dwait(t, "t", ns.gw_ld)
            bk = bank[4 + ti % 2]
            bk.tr(t, bk.h[0:16, 0:128], ns.cmb[:, ti * 16:(ti + 1) * 16],
                  ns.identf[:, :])

            @block.scalar
            def _(sc, bk=bk, ti=ti):
                sc = ctx.ch("s", sc)
                ctx.wait(sc, "s", "t", ctx.cnt["t"])
                sc.activation(out=ns.cfs[:, ti * 128:(ti + 1) * 128],
                              in_=bk.h[0:16, 0:128], func=AF.Sqrt,
                              bias=ns.sqeps[:, 0:1])
                bk.rd("s")
    s_cfs = ctx.cnt["s"]

    emit_phase_join(ns)
    es8r.close()

    es9 = ExitStack()
    sb("wsh1", (128, 8 * 512), BF16, stack=es9)
    sb("wsh2", (128, 4 * 1024), BF16, stack=es9)
    sb("hsh", (128, 4 * 512), BF16, stack=es9)
    ns.wsh_free = 0
    for qsh in range(4):
        @block.gpsimd
        def _(g, qsh=qsh):
            g = ctx.ch("g", g)
            ctx.wait(g, "g", "t", ns.wsh_free)
            prs = []
            for p in range(8):
                prs.append((ns.wsh1[:, p * 512:(p + 1) * 512],
                            ns.I_sh_upt[128 * p:128 * (p + 1),
                                        qsh * 512:(qsh + 1) * 512]))
            for kk in range(4):
                prs.append((ns.wsh2[:, kk * 1024:(kk + 1) * 1024],
                            ns.I_sh_dnt[qsh * 512 + 128 * kk:
                                        qsh * 512 + 128 * (kk + 1), :]))
            ns.wsh_ld = ctx.dgroup(g, "g", prs, "lg")

        for ck in range(NCK):
            tco = ck * 512

            @block.tensor
            def _(t, qsh=qsh, tco=tco):
                ctx.wait(t, "t", "v", v_x3b)
                ctx.dwait(t, "t", ns.wsh_ld)
                for k in range(8):
                    for mi in range(4):
                        bk = bank[4 + mi]
                        bk.mm(t, bk.h[:, 0:512],
                              ns.wsh1[:, k * 512 + mi * 128:
                                      k * 512 + (mi + 1) * 128],
                              ns.x3b[:, k * TO + tco:k * TO + tco + 512],
                              start=(k == 0), stop=(k == 7))
            t_up = ctx.cnt["t"]

            v_hsh_mi = []

            @block.vector
            def _(v, t_up=t_up):
                v = ctx.ch("v", v)
                ctx.wait(v, "v", "t", t_up)
                for mi in range(4):
                    bk = bank[4 + mi]
                    v.tensor_scalar_max(out=ns.psq[:, 0:512],
                                        in0=bk.h[:, 0:512], scalar1=0.0)
                    bk.rd("v")
                    v.tensor_tensor(out=ns.hsh[:, mi * 512:(mi + 1) * 512],
                                    in0=ns.psq[:, 0:512], in1=ns.psq[:, 0:512],
                                    op=ALU.mult)
                    v_hsh_mi.append(ctx.cnt["v"])

            for half in range(2):
                @block.tensor
                def _(t, qsh=qsh, tco=tco, half=half, v_hsh_mi=v_hsh_mi):
                    for kk in range(4):
                        ctx.wait(t, "t", "v", v_hsh_mi[kk])
                        for mi in range(4):
                            mt = half * 4 + mi
                            bk = bank[mi]
                            bk.mm(t, bk.h[:, 0:512],
                                  ns.wsh2[:, kk * 1024 + mt * 128:
                                          kk * 1024 + (mt + 1) * 128],
                                  ns.hsh[:, kk * 512:(kk + 1) * 512],
                                  start=(kk == 0), stop=(kk == 3))
                    ns.wsh_free = ctx.cnt["t"]

                @block.vector
                def _(v, qsh=qsh, tco=tco, half=half):
                    v = ctx.ch("v", v)
                    ctx.wait(v, "v", "t", ctx.cnt["t"])
                    for mi in range(4):
                        mt = half * 4 + mi
                        bk = bank[mi]
                        co = mt * TO + tco
                        if qsh == 0:
                            v.tensor_copy(out=ns.acc[:, co:co + 512],
                                          in_=bk.h[:, 0:512])
                        else:
                            v.tensor_tensor(out=ns.acc[:, co:co + 512],
                                            in0=bk.h[:, 0:512],
                                            in1=ns.acc[:, co:co + 512],
                                            op=ALU.add)
                        bk.rd("v")
    emit_phase_join(ns)
    es9.close()

    es10 = ExitStack()
    sb("wme1", (128, 8 * 512), BF16, stack=es10)
    sb("wme2", (128, 4 * 1024), BF16, stack=es10)
    sb("hme", (128, 4 * 512), BF16, stack=es10)
    ns.wme_free = 0
    for e in range(MOE_NE):
        @block.gpsimd
        def _(g, e=e):
            g = ctx.ch("g", g)
            ctx.wait(g, "g", "t", ns.wme_free)
            prs = []
            for p in range(8):
                prs.append(
                    (ns.wme1[:, p * 512:(p + 1) * 512],
                     ns.I_moe_up[e * D + 128 * p:e * D + 128 * (p + 1), :]))
            for kk in range(4):
                prs.append(
                    (ns.wme2[:, kk * D:(kk + 1) * D],
                     ns.I_moe_dn[e * 512 + 128 * kk:
                                 e * 512 + 128 * (kk + 1), :]))
            ns.wme_ld = ctx.dgroup(g, "g", prs, "lg")

        for ck in range(NCK):
            tco = ck * 512

            @block.tensor
            def _(t, e=e, tco=tco):
                ctx.wait(t, "t", "s", s_cfs)
                ctx.wait(t, "t", "v", ctx.cnt["v"])
                ctx.wait(t, "t", "s", ctx.cnt["s"])
                bank[0].mm(t, bank[0].h[:, 0:512],
                           ns.inde16[:, e * 128:(e + 1) * 128],
                           ns.cfs[0:16, tco:tco + 512],
                           start=True, stop=True)

            @block.scalar
            def _(sc, e=e):
                sc = ctx.ch("s", sc)
                ctx.wait(sc, "s", "t", ctx.cnt["t"])
                sc.copy(out=ns.ptt[:, 0:512], in_=bank[0].h[:, 0:512])
                bank[0].rd("s")
            s_cbc = ctx.cnt["s"]

            @block.tensor
            def _(t, tco=tco):
                ctx.wait(t, "t", "v", v_x3b)
                ctx.dwait(t, "t", ns.wme_ld)
                for k in range(8):
                    for mi in range(4):
                        bk = bank[4 + mi]
                        bk.mm(t, bk.h[:, 0:512],
                              ns.wme1[:, k * 512 + mi * 128:
                                      k * 512 + (mi + 1) * 128],
                              ns.x3b[:, k * TO + tco:k * TO + tco + 512],
                              start=(k == 0), stop=(k == 7))
            t_up = ctx.cnt["t"]

            v_hme_mi = []

            @block.vector
            def _(v, s_cbc=s_cbc, t_up=t_up):
                v = ctx.ch("v", v)
                ctx.wait(v, "v", "t", t_up)
                ctx.wait(v, "v", "s", s_cbc)
                for mi in range(4):
                    bk = bank[4 + mi]
                    v.scalar_tensor_tensor(out=ns.psq[:, 0:512],
                                           in0=bk.h[:, 0:512],
                                           scalar=0.0,
                                           in1=ns.ptt[:, 0:512],
                                           op0=ALU.max, op1=ALU.mult)
                    bk.rd("v")
                    v.tensor_tensor(out=ns.hme[:, mi * 512:(mi + 1) * 512],
                                    in0=ns.psq[:, 0:512],
                                    in1=ns.psq[:, 0:512], op=ALU.mult)
                    v_hme_mi.append(ctx.cnt["v"])

            for half in range(2):
                @block.tensor
                def _(t, half=half, v_hme_mi=v_hme_mi):
                    for kk in range(4):
                        ctx.wait(t, "t", "v", v_hme_mi[kk])
                        for mi in range(4):
                            mt = half * 4 + mi
                            bk = bank[mi]
                            bk.mm(t, bk.h[:, 0:512],
                                  ns.wme2[:, kk * D + mt * 128:
                                          kk * D + (mt + 1) * 128],
                                  ns.hme[:, kk * 512:(kk + 1) * 512],
                                  start=(kk == 0), stop=(kk == 3))
                    ns.wme_free = ctx.cnt["t"]

                @block.vector
                def _(v, half=half, tco=tco):
                    v = ctx.ch("v", v)
                    ctx.wait(v, "v", "t", ctx.cnt["t"])
                    for mi in range(4):
                        mt = half * 4 + mi
                        bk = bank[mi]
                        co = mt * TO + tco
                        v.tensor_tensor(out=ns.acc[:, co:co + 512],
                                        in0=bk.h[:, 0:512],
                                        in1=ns.acc[:, co:co + 512],
                                        op=ALU.add)
                        bk.rd("v")
    emit_phase_join(ns)
    es10.close()

    for ck in range(NCK):
        @block.vector
        def _(v, ck=ck):
            v = ctx.ch("v", v)
            for mt in range(8):
                co = mt * TO + ck * 512
                v.tensor_tensor(out=ns.acc[:, co:co + 512],
                                in0=ns.xa[:, co:co + 512],
                                in1=ns.acc[:, co:co + 512], op=ALU.add)
        emit_colstats(ns, ns.acc, ck * 512, 512, "ln", EPS_LN, bufW=TO)
    ns.x2buf = ns.acc


# version 07 (final)
# speedup vs baseline: 1.0160x; 1.0160x over previous
import numpy as np
from contextlib import ExitStack
from types import SimpleNamespace

import concourse.bass as bass
import concourse.mybir as mybir
from concourse.bass_utils import run_bass_kernel_spmd

F32 = mybir.dt.float32
BF16 = mybir.dt.bfloat16
AF = mybir.ActivationFunctionType
ALU = mybir.AluOpType
AX = mybir.AxisListType

B = 4
TF = 4096
TO = 2048
D = 1024
NHL = 8
FH = NHL * 64
DK = 64
SEC = 512
CHUNK = 128
NCH = SEC // CHUNK
NB32 = SEC // 32
NB8 = SEC // 8
EPS_RMS = 1e-6
EPS_GN = 1e-5
EPS_LN = 1e-5
DCLAMP = 40.0
MOE_NE = 16


class Ctx:
    def __init__(self, nc, es):
        self.nc = nc
        self.sem, self.cnt = {}, {}
        for e in ("t", "v", "s", "g"):
            self.sem[e] = es.enter_context(nc.semaphore("tick_" + e))
            self.cnt[e] = 0
        self.dsem, self.dcnt = {}, {}
        self.pool, self.rr = {}, {}
        for ch, n in (("lq", 4), ("lg", 4), ("sq", 2), ("cc", 1)):
            names = []
            for i in range(n):
                nm = f"dma_{ch}{i}"
                self.dsem[nm] = es.enter_context(nc.semaphore(nm))
                self.dcnt[nm] = 0
                names.append(nm)
            self.pool[ch] = names
            self.rr[ch] = 0
        self.waited = {}

    def inc(self, e, inst):
        self.cnt[e] += 1
        inst.then_inc(self.sem[e], 1)
        return self.cnt[e]

    def dgroup(self, eng, consumer, pairs, ch):
        nm = self.pool[ch][self.rr[ch] % len(self.pool[ch])]
        self.rr[ch] += 1
        self.wait(eng, consumer, nm, self.dcnt[nm])
        for out, in_ in pairs:
            self.dcnt[nm] += 16
            eng.dma_start(out=out, in_=in_).then_inc(self.dsem[nm], 16)
        return (nm, self.dcnt[nm])

    def ccg(self, g, kind, ins, outs, groups):
        nm = self.pool["cc"][0]
        self.dcnt[nm] += 1
        g.collective_compute(kind, ALU.bypass, replica_groups=groups,
                             ins=ins, outs=outs).then_inc(self.dsem[nm], 1)
        return (nm, self.dcnt[nm])

    def dwait(self, eng, consumer, key):
        if key is None:
            return
        self.wait(eng, consumer, key[0], key[1])

    def ch(self, ename, eng):
        return ChainEng(self, ename, eng)

    def wait(self, eng, consumer, sname, val):
        if val is None or val <= 0:
            return
        key = (consumer, sname)
        if self.waited.get(key, 0) >= val:
            return
        sem = self.sem[sname] if sname in self.sem else self.dsem[sname]
        eng.wait_ge(sem, val)
        self.waited[key] = val


_CHAIN_OPS = {
    "tensor_copy", "tensor_tensor", "tensor_add", "tensor_sub", "tensor_mul",
    "tensor_max", "tensor_scalar", "tensor_single_scalar", "tensor_scalar_add",
    "tensor_scalar_sub", "tensor_scalar_mul", "tensor_scalar_max",
    "tensor_scalar_min", "tensor_relu", "tensor_reduce", "reduce_max",
    "reduce_sum", "scalar_tensor_tensor", "tensor_tensor_scan", "engine_nop",
    "memset", "memzero", "reciprocal", "select", "copy_predicated",
    "transpose", "bn_stats", "bn_aggr", "max", "max_index", "stream_shuffle",
    "activation", "sign", "square", "sqrt", "mul", "copy", "add", "iota",
    "affine_select", "partition_broadcast", "partition_all_reduce",
}


class ChainEng:

    def __init__(self, ctx, ename, eng):
        self._ctx, self._e, self._raw = ctx, ename, eng

    def __getattr__(self, name):
        f = getattr(self._raw, name)
        if name not in _CHAIN_OPS:
            return f
        ctx, e, raw = self._ctx, self._e, self._raw

        def g(*a, **k):
            ctx.wait(raw, e, e, ctx.cnt[e])
            inst = f(*a, **k)
            ctx.inc(e, inst)
            return inst
        return g


class PBank:

    def __init__(self, ctx, es, name):
        self.ctx = ctx
        self.h = es.enter_context(ctx.nc.psum_tensor(name, [128, 512], F32))
        self.last_w = 0
        self.reads = {}

    def pre(self, t):
        self.ctx.wait(t, "t", "t", self.last_w)
        for e, v in self.reads.items():
            self.ctx.wait(t, "t", e, v)
        self.reads = {}

    def mm(self, t, out_ap, lhsT, rhs, start, stop, tile_position=None):
        self.pre(t)
        kw = {}
        if tile_position is not None:
            kw["tile_position"] = tile_position
        inst = t.matmul(out_ap, lhsT, rhs, start=start, stop=stop, **kw)
        self.last_w = self.ctx.inc("t", inst)
        return inst

    def tr(self, t, out_ap, in_ap, ident):
        self.pre(t)
        inst = t.transpose(out_ap, in_ap, ident)
        self.last_w = self.ctx.inc("t", inst)
        return inst

    def rd(self, e):
        self.reads[e] = self.ctx.cnt[e]


def bc_blocks(ap2d, base, bs, nblk, rep=None):
    rep = bs if rep is None else rep
    full = ap2d[:, :]
    pitch = full.ap[0][0]
    return bass.AP(full.tensor, full.offset + base,
                   [[pitch, 128], [bs, nblk], [0, rep]])


def cols(ap2d, base, stride, n):
    full = ap2d[:, :]
    pitch = full.ap[0][0]
    return bass.AP(full.tensor, full.offset + base, [[pitch, 128], [stride, n]])


def grp(ap2d, off, bs, nblk):
    return ap2d[:, off:off + bs * nblk].rearrange("p (b i) -> p b i", i=bs)


def build_program(taps=(), n_sec=TF // SEC, full=True, stop_after=None,
                  dbg_rope=False):
    nc = bass.Bass("TRN2", target_bir_lowering=False, debug=False)
    es = ExitStack()
    ctx = Ctx(nc, es)
    ns = SimpleNamespace(nc=nc, es=es, ctx=ctx, n_sec=n_sec, full=full,
                         stop_after=stop_after)
    ns.dbg_rope = dbg_rope

    def din(name, shape, dtype=F32):
        ap = nc.dram_tensor(name, list(shape), dtype, kind="ExternalInput").ap()
        setattr(ns, "I_" + name, ap)

    din("xf", (D, TF))
    din("xo", (D, TO))
    for nm in ("wr", "wk", "wv", "wg"):
        din(nm, (D, FH))
    din("wo", (D, D))
    din("w1", (D, 4096))
    din("w2", (4096, D))
    din("wqt", (D, D))
    din("l1wq", (D, FH))
    din("l1wk", (D, FH))
    din("l1wv", (D, FH))
    din("l1wo", (D, D))
    din("l1bq", (128, 4))
    din("l1bk", (128, 4))
    din("l1bv_row", (1, FH))
    din("l1bo", (128, 8))
    din("rope_perm", (128, 128))
    din("ropesin", (128, TF))
    din("ropecos", (128, TF))
    din("cmask", (128, 4 * 512), BF16)
    din("gate_wt", (D, 16))
    din("bias_e", (1, 16))
    din("moe_up", (MOE_NE * D, 512))
    din("moe_dn", (MOE_NE * 512, D))
    din("sh_upt", (D, 2048))
    din("sh_dnt", (2048, D))
    din("wla", (D, 64))
    din("wlb", (64, FH))
    for nm in ("mu_r", "mu_k", "mu_v", "mu_w", "mu_g", "ln1w", "ln2w"):
        din(nm, (128, 8))
    for nm in ("w0", "uu", "gnw", "gnb"):
        din(nm, (128, 4))
    din("ident", (128, 128), BF16)
    din("identf", (128, 128))
    din("inde16", (16, 16 * 128), BF16)
    din("indh", (128, 2))
    din("indhb", (128, 2), BF16)
    din("indb", (8, 512), BF16)
    din("indbf", (8, 512))
    din("indbt", (128, 32), BF16)
    din("indbtf", (128, 32))
    din("md8", (128, 128), BF16)
    din("m8", (128, 128), BF16)
    din("m16", (128, 128), BF16)
    din("mstr", (128, 128), BF16)
    din("bmask", (8, 3 * SEC), BF16)
    din("onesc", (1, 128))
    din("onesr", (128, 1))
    din("mab", (128, 2))

    ns.OUT = nc.dram_tensor("out", [D, TO], F32, kind="ExternalOutput").ap()
    ns.taps = {}
    for name, shape, dtype in taps:
        ns.taps[name] = nc.dram_tensor(name, list(shape), dtype,
                                       kind="ExternalOutput").ap()

    ns.ORW_B = nc.dram_tensor("orw_b", [FH, TF], BF16).ap()
    ns.ORW_F = nc.dram_tensor("orw_f", [2 * FH, TF], BF16).ap()
    ns.X2B = nc.dram_tensor("x2b", [D, TO], BF16).ap()
    ns.X2F = nc.dram_tensor("x2f", [2 * D, TO], BF16).ap()
    ns.OAT_B = nc.dram_tensor("oat_b", [FH, TF], BF16).ap()
    ns.OAT_F = nc.dram_tensor("oat_f", [2 * FH, TF], BF16).ap()
    if dbg_rope:
        ns.DBG_SC = nc.dram_tensor("dbg_sc", [128, 2 * 512], F32,
                                   kind="ExternalOutput").ap()
        ns.DBG_QK = nc.dram_tensor("dbg_qk", [128, 8 * TF], BF16,
                                   kind="ExternalOutput").ap()

    def sb(name, shape, dtype=F32, stack=None):
        setattr(ns, name, (stack or es).enter_context(
            nc.sbuf_tensor("sb_" + name, shape, dtype)))
    ns.sb = sb

    sb("ident", (128, 128), BF16)
    sb("indh", (128, 2))
    sb("indhb", (128, 2), BF16)
    sb("indb", (8, 512), BF16)
    sb("indbf", (8, 512))
    sb("indbt", (128, 32), BF16)
    sb("indbtf", (128, 32))
    sb("md8", (128, 128), BF16)
    sb("m8", (128, 128), BF16)
    sb("m16", (128, 128), BF16)
    sb("mstr", (128, 128), BF16)
    sb("bmask", (8, 3 * SEC), BF16)
    sb("onesc", (1, 128))
    sb("onesr", (128, 1))
    sb("mab", (128, 2))
    sb("zeros64", (64, 128), BF16)
    for nm in ("mu_r", "mu_k", "mu_v", "mu_w", "mu_g", "ln1w", "ln2w"):
        sb("v_" + nm, (128, 8))
    for nm in ("w0", "uu", "gnw", "gnb"):
        sb("v_" + nm, (128, 4))
    for m in ("r", "k", "v", "w", "g"):
        sb("omu_" + m, (128, 8))

    mixer_es = ExitStack()
    ns.mixer_es = mixer_es
    def sbm(name, shape, dtype=F32):
        ns.sb(name, shape, dtype, stack=mixer_es)
    sb = sbm
    sb("xs", (128, 8 * SEC))
    sb("hb", (128, 8 * (SEC + 1)), BF16)
    sb("mixb", (128, 8 * SEC), BF16)
    sb("sqb", (128, SEC))
    sb("wbA", (128, 8 * FH), BF16)
    sb("wbB", (128, 8 * FH), BF16)
    sb("wlab", (128, 8 * 64), BF16)
    sb("wlbb", (64, FH), BF16)
    sb("rb", (128, 4 * SEC), BF16)
    sb("kb", (128, 4 * (SEC + 3)), BF16)
    sb("vb", (128, 4 * (SEC + 3)), BF16)
    sb("gb", (128, 4 * SEC), BF16)
    sb("wfp", (128, 4 * SEC))
    sb("ewb", (128, 4 * SEC))
    sb("csb", (128, 4 * (SEC + 3)))
    sb("cscar", (128, 4))
    sb("vtm", (128, NCH * FH), BF16)
    sb("khtm", (128, NCH * FH), BF16)
    FAMS = ("r32", "kr32_1", "kr32_2", "kr32_3", "r16", "kr16_1", "r8",
            "kr8_1", "kd8", "rd8", "khat", "rchunk")
    ns.FAMS = FAMS
    for f in FAMS:
        sb("fam_" + f, (128, 4 * SEC), BF16)
    sb("b2f", (128, 4 * SEC), BF16)
    sb("b3f", (128, 4 * SEC), BF16)
    sb("arg1", (128, SEC))
    sb("arg2", (128, SEC))
    sb("ef32", (128, 4 * 17), BF16)
    sb("efh8", (128, 4 * NB8), BF16)
    sb("efh8b", (128, 4 * NB8), BF16)
    sb("f16k", (128, 4 * NB8), BF16)
    sb("f16r", (128, 4 * NB8), BF16)
    sb("fchk", (128, 4 * NB32), BF16)
    sb("fchkr", (128, 4 * NB32), BF16)
    sb("etot", (128, 4 * NCH))
    sb("prodb", (128, 4 * SEC), BF16)
    for d_ in range(4):
        sb("browm%d" % d_, (8, SEC), BF16)
    sb("mean8", (8, SEC))
    sb("rstd8", (8, SEC))
    sb("psb", (128, 4 * 128), BF16)
    sb("osec", (128, 4 * SEC))
    sb("omix", (128, 4 * SEC), BF16)
    sb("Ssb", (128, 4 * 64))
    sb("Sbf", (128, 4 * 64), BF16)
    sb("srow", (8, SEC))
    sb("svar", (8, SEC))

    sb = ns.sb
    ns.bank = [PBank(ctx, es, f"bank{i}") for i in range(8)]
    ns.groups = [[0, 1], [2, 3], [4, 5], [6, 7]]
    ns.mix_free = 0
    ns.w_free = {"wr": 0, "wk": 0, "wv": 0, "wg": 0}
    ns.bfree = {}
    ns.omix_keys = []
    ns.w_load = {}
    ns.fam_free = 0
    ns.sec_free_v = 0
    ns.carry_v = 0

    with nc.Block() as block:
        ns.block = block
        emit_preamble(ns)
        for s in range(n_sec):
            emit_mixer_section(ns, s)
        emit_mixer_taps(ns)
        if getattr(ns, "full", True):
            emit_phase_join(ns)
            mixer_es.close()
            emit_l0post(ns)
            if ns.stop_after not in ("wo", "ffn", "x1", "x2"):
                emit_l1attn(ns)
            if ns.stop_after not in ("wo", "ffn", "x1", "x2", "attn"):
                emit_moe(ns)
        emit_finish(ns)

    es.close()
    return nc, ns


def emit_preamble(ns):
    ctx, block = ns.ctx, ns.block

    @block.sync
    def _(q):
        pairs = [(ns.ident, ns.I_ident), (ns.indh, ns.I_indh),
                 (ns.indhb, ns.I_indhb), (ns.indb, ns.I_indb),
                 (ns.md8, ns.I_md8), (ns.m8, ns.I_m8), (ns.m16, ns.I_m16),
                 (ns.mstr, ns.I_mstr),
                 (ns.bmask, ns.I_bmask), (ns.onesc, ns.I_onesc),
                 (ns.indbf, ns.I_indbf),
                 (ns.indbt, ns.I_indbt), (ns.indbtf, ns.I_indbtf),
                 (ns.onesr, ns.I_onesr), (ns.mab, ns.I_mab)]
        for nm in ("mu_r", "mu_k", "mu_v", "mu_w", "mu_g", "ln1w", "ln2w",
                   "w0", "uu", "gnw", "gnb"):
            pairs.append((getattr(ns, "v_" + nm), getattr(ns, "I_" + nm)))
        ns.c_pre = ctx.dgroup(q, "q", [(d[:, :], s_) for d, s_ in pairs], "lq")

    @block.gpsimd
    def _(g):
        g = ctx.ch("g", g)
        ns.c_lora = ctx.dgroup(g, "g", [
            (ns.wlab[:, :].rearrange("p (a c) -> p a c", c=64),
             ns.I_wla.rearrange("(a p) c -> p a c", p=128)),
            (ns.wlbb[:, :], ns.I_wlb)], "lg")

    @block.vector
    def _(v):
        v = ctx.ch("v", v)
        ctx.dwait(v, "v", ns.c_pre)
        for m, nm in (("r", "mu_r"), ("k", "mu_k"), ("v", "mu_v"),
                      ("w", "mu_w"), ("g", "mu_g")):
            v.tensor_scalar(out=getattr(ns, "omu_" + m)[:, :],
                            in0=getattr(ns, "v_" + nm)[:, :],
                            scalar1=-1.0, scalar2=1.0,
                            op0=ALU.mult, op1=ALU.add)
        v.memset(ns.csb[:, :], 0.0)
        v.memset(ns.cscar[:, :], 0.0)
        v.memset(ns.kb[:, :], 0.0)
        v.memset(ns.vb[:, :], 0.0)
        v.memset(ns.hb[:, :], 0.0)
        v.memset(ns.Ssb[:, :], 0.0)
        v.memset(ns.zeros64[:, :], 0.0)
        v.memset(ns.bank[0].h[:, :], 0.0)
        v.memset(ns.bank[1].h[:, :], 0.0)
        ns.bank[0].rd("v")
        ns.bank[1].rd("v")
        v.engine_nop()
    ns.v_pre = ctx.cnt["v"]


def emit_mixer_section(ns, s):
    ctx, block, bank = ns.ctx, ns.block, ns.bank
    xoff = 0
    P3 = SEC + 3
    c_xs = [None]

    def csp(mt):
        return ns.csb[:, mt * P3:(mt + 1) * P3]

    @block.sync
    def _(q):
        ctx.wait(q, "q", "v", getattr(ns, "xs_free", 0))
        c_xs[0] = ctx.dgroup(q, "q", [
            (ns.xs[:, xoff + p * SEC:xoff + (p + 1) * SEC],
             ns.I_xf[128 * p:128 * (p + 1), s * SEC:(s + 1) * SEC])
            for p in range(8)], "lq")

    WMAP = [("wr", ns.I_wr, ns.wbA), ("wk", ns.I_wk, ns.wbB),
            ("wv", ns.I_wv, ns.wbA), ("wg", ns.I_wg, ns.wbB)]

    @block.gpsimd
    def _(g):
        g = ctx.ch("g", g)
        if s == 0:
            ctx.dwait(g, "g", ns.c_lora)
        for nm, src, buf in WMAP[:2]:
            ctx.wait(g, "g", "t", ns.bfree.get(id(buf), 0))
            ns.w_load[nm] = ctx.dgroup(g, "g", [
                (buf[:, p * FH:(p + 1) * FH], src[128 * p:128 * (p + 1), :])
                for p in range(8)], "lg")

    for p in range(8):
        @block.scalar
        def _(sc, p=p):
            sc = ctx.ch("s", sc)
            ctx.dwait(sc, "s", ns.c_pre)
            ctx.dwait(sc, "s", c_xs[0])
            ctx.wait(sc, "s", "t", getattr(ns, "sq_guard", 0))
            if p > 0:
                ctx.wait(sc, "s", "t", ns.sq_mm)
            inst = sc.activation(out=ns.sqb[:, :],
                                 in_=ns.xs[:, xoff + p * SEC:xoff + (p + 1) * SEC],
                                 func=AF.Square)

        @block.tensor
        def _(t, p=p):
            ctx.dwait(t, "t", ns.c_pre)
            ctx.wait(t, "t", "s", ctx.cnt["s"])
            bank[4].mm(t, bank[4].h[0:1, 0:SEC], ns.onesr[:, :], ns.sqb[:, :],
                       start=(p == 0), stop=(p == 7))
            ns.sq_mm = ctx.cnt["t"]

    @block.vector
    def _(v):
        v = ctx.ch("v", v)
        ctx.wait(v, "v", "t", ns.sq_mm)
        v.tensor_scalar(out=ns.arg1[0:1, 0:SEC], in0=bank[4].h[0:1, 0:SEC],
                        scalar1=1.0 / D, scalar2=EPS_RMS,
                        op0=ALU.mult, op1=ALU.add)
        bank[4].rd("v")
        inst = v.reciprocal(out=ns.arg2[0:1, 0:SEC], in_=ns.arg1[0:1, 0:SEC])

    @block.scalar
    def _(sc):
        sc = ctx.ch("s", sc)
        ctx.wait(sc, "s", "v", ctx.cnt["v"])
        inst = sc.activation(out=ns.sqb[0:1, 0:SEC], in_=ns.arg2[0:1, 0:SEC], func=AF.Sqrt)
    s_rstd = ctx.cnt["s"]

    @block.tensor
    def _(t):
        ctx.wait(t, "t", "s", s_rstd)
        bank[5].mm(t, bank[5].h[:, 0:SEC], ns.onesc[:, :], ns.sqb[0:1, 0:SEC],
                   start=True, stop=True)
    t_rstdbc = ctx.cnt["t"]
    ns.sq_guard = t_rstdbc

    @block.vector
    def _(v):
        v = ctx.ch("v", v)
        ctx.wait(v, "v", "t", t_rstdbc)
        ctx.wait(v, "v", "v", ns.carry_v)
        for p in range(8):
            hp = ns.hb[:, p * (SEC + 1):(p + 1) * (SEC + 1)]
            inst = v.scalar_tensor_tensor(
                out=hp[:, 1:SEC + 1],
                in0=ns.xs[:, xoff + p * SEC:xoff + (p + 1) * SEC],
                scalar=ns.v_ln1w[:, p:p + 1], in1=bank[5].h[:, 0:SEC],
                op0=ALU.mult, op1=ALU.mult)
        bank[5].rd("v")
    v_h = ctx.cnt["v"]
    ns.xs_free = v_h

    def emit_mix(m):
        @block.vector
        def _(v):
            v = ctx.ch("v", v)
            ctx.wait(v, "v", "v", v_h)
            ctx.wait(v, "v", "t", ns.mix_free)
            for p in range(8):
                hp = ns.hb[:, p * (SEC + 1):(p + 1) * (SEC + 1)]
                mp = ns.mixb[:, p * SEC:(p + 1) * SEC]
                v.tensor_scalar(out=mp, in0=hp[:, 1:SEC + 1],
                                scalar1=getattr(ns, "omu_" + m)[:, p:p + 1],
                                scalar2=None, op0=ALU.mult)
            for p in range(8):
                hp = ns.hb[:, p * (SEC + 1):(p + 1) * (SEC + 1)]
                mp = ns.mixb[:, p * SEC:(p + 1) * SEC]
                inst = v.scalar_tensor_tensor(
                    out=mp, in0=hp[:, 0:SEC],
                    scalar=getattr(ns, "v_mu_" + m)[:, p:p + 1],
                    in1=mp, op0=ALU.mult, op1=ALU.add)
        return ctx.cnt["v"]

    def emit_proj(wsb, nm, vwait):
        @block.tensor
        def _(t):
            ctx.wait(t, "t", "v", vwait)
            ctx.dwait(t, "t", ns.w_load[nm])
            for k in range(8):
                for mt in range(4):
                    bank[mt].mm(
                        t, bank[mt].h[:, 0:SEC],
                        wsb[:, k * FH + mt * 128:k * FH + (mt + 1) * 128],
                        ns.mixb[:, k * SEC:(k + 1) * SEC],
                        start=(k == 0), stop=(k == 7))
            ns.mix_free = ctx.cnt["t"]
            ns.w_free[nm] = ctx.cnt["t"]
            ns.bfree[id(wsb)] = ctx.cnt["t"]

    vw = emit_mix("w")

    @block.tensor
    def _(t):
        ctx.wait(t, "t", "v", vw)
        ctx.dwait(t, "t", ns.c_lora)
        for k in range(8):
            bank[4].mm(t, bank[4].h[0:64, 0:SEC],
                       ns.wlab[:, k * 64:(k + 1) * 64],
                       ns.mixb[:, k * SEC:(k + 1) * SEC],
                       start=(k == 0), stop=(k == 7))

    @block.scalar
    def _(sc):
        sc = ctx.ch("s", sc)
        ctx.wait(sc, "s", "t", ctx.cnt["t"])
        inst = sc.activation(out=ns.prodb[0:64, 0:SEC],
                             in_=ns.bank[4].h[0:64, 0:SEC], func=AF.Tanh)
        bank[4].rd("s")
    s_tanh = ctx.cnt["s"]

    @block.tensor
    def _(t):
        ctx.wait(t, "t", "s", s_tanh)
        for mt in range(4):
            bank[4 + mt].mm(t, bank[4 + mt].h[:, 0:SEC],
                            ns.wlbb[:, mt * 128:(mt + 1) * 128],
                            ns.prodb[0:64, 0:SEC], start=True, stop=True)
        ns.mix_free = max(ns.mix_free, ctx.cnt["t"])
    @block.vector
    def _(v):
        v = ctx.ch("v", v)
        ctx.wait(v, "v", "t", ctx.cnt["t"])
        for mt in range(4):
            bk = bank[4 + mt]
            inst = v.tensor_scalar(out=ns.wfp[:, mt * SEC:(mt + 1) * SEC],
                                   in0=bk.h[:, 0:SEC],
                                   scalar1=ns.v_w0[:, mt:mt + 1], scalar2=None,
                                   op0=ALU.add)
            bk.rd("v")
    v_w = ctx.cnt["v"]

    @block.scalar
    def _(sc):
        sc = ctx.ch("s", sc)
        ctx.wait(sc, "s", "v", v_w)
        for mt in range(4):
            inst = sc.activation(out=ns.ewb[:, mt * SEC:(mt + 1) * SEC],
                                 in_=ns.wfp[:, mt * SEC:(mt + 1) * SEC],
                                 func=AF.Exp)
    s_ew = ctx.cnt["s"]

    @block.vector
    def _(v):
        v = ctx.ch("v", v)
        ctx.wait(v, "v", "s", s_ew)
        ctx.wait(v, "v", "v", ns.carry_v)
        for mt in range(4):
            cp = csp(mt)
            inst = v.tensor_tensor_scan(
                out=cp[:, 3:SEC + 3],
                data0=ns.ewb[:, mt * SEC:(mt + 1) * SEC],
                data1=ns.ewb[:, mt * SEC:(mt + 1) * SEC],
                initial=ns.cscar[:, mt:mt + 1], op0=ALU.add, op1=ALU.bypass)

    @block.vector
    def _(v):
        v = ctx.ch("v", v)
        ctx.wait(v, "v", "v", ctx.cnt["v"])
        for mt in range(4):
            inst = v.tensor_copy(out=ns.cscar[:, mt:mt + 1],
                                 in_=csp(mt)[:, SEC + 2:SEC + 3])
    v_cs = ctx.cnt["v"]

    def evac_plain(dst):
        @block.vector
        def _(v):
            v = ctx.ch("v", v)
            ctx.wait(v, "v", "t", ctx.cnt["t"])
            for mt in range(4):
                inst = v.tensor_copy(out=dst[:, mt * SEC:(mt + 1) * SEC],
                                     in_=bank[mt].h[:, 0:SEC])
                bank[mt].rd("v")

    def evac_shift(dst):
        @block.vector
        def _(v):
            v = ctx.ch("v", v)
            ctx.wait(v, "v", "t", ctx.cnt["t"])
            ctx.wait(v, "v", "v", ns.carry_v)
            for mt in range(4):
                dp = dst[:, mt * P3:(mt + 1) * P3]
                inst = v.tensor_copy(out=dp[:, 3:SEC + 3],
                                     in_=bank[mt].h[:, 0:SEC])
                bank[mt].rd("v")

    def evac_g():
        @block.scalar
        def _(sc):
            sc = ctx.ch("s", sc)
            ctx.wait(sc, "s", "t", ctx.cnt["t"])
            for mt in range(4):
                inst = sc.activation(out=ns.gb[:, mt * SEC:(mt + 1) * SEC],
                                     in_=bank[mt].h[:, 0:SEC], func=AF.Silu)
                bank[mt].rd("s")

    vr = emit_mix("r")
    emit_proj(ns.wbA, "wr", vr)
    evac_plain(ns.rb)

    @block.gpsimd
    def _(g):
        g = ctx.ch("g", g)
        nm, src, buf = WMAP[2]
        ctx.wait(g, "g", "t", ns.bfree.get(id(buf), 0))
        ns.w_load[nm] = ctx.dgroup(g, "g", [
            (buf[:, p * FH:(p + 1) * FH], src[128 * p:128 * (p + 1), :])
            for p in range(8)], "lg")

    vk = emit_mix("k")
    emit_proj(ns.wbB, "wk", vk)
    evac_shift(ns.kb)

    @block.gpsimd
    def _(g):
        g = ctx.ch("g", g)
        nm, src, buf = WMAP[3]
        ctx.wait(g, "g", "t", ns.bfree.get(id(buf), 0))
        ns.w_load[nm] = ctx.dgroup(g, "g", [
            (buf[:, p * FH:(p + 1) * FH], src[128 * p:128 * (p + 1), :])
            for p in range(8)], "lg")
    vv = emit_mix("v")
    emit_proj(ns.wbA, "wv", vv)
    evac_shift(ns.vb)
    vg = emit_mix("g")
    emit_proj(ns.wbB, "wg", vg)
    evac_g()
    v_rkvg = ctx.cnt["v"]

    @block.tensor
    def _(t):
        ctx.wait(t, "t", "v", v_rkvg)
        for c in range(NCH):
            for mt in range(4):
                bk = bank[6 + (c * 4 + mt) % 2]
                src = ns.vb[:, mt * P3 + 3 + c * 128:mt * P3 + 3 + (c + 1) * 128]
                bk.tr(t, bk.h[:, 0:64].bitcast(BF16)[:, 0:128], src,
                      ns.ident[:, :])
                @block.scalar
                def _(sc, bk=bk, c=c, mt=mt):
                    sc = ctx.ch("s", sc)
                    ctx.wait(sc, "s", "t", ctx.cnt["t"])
                    inst = sc.copy(
                        out=ns.vtm[:, c * FH + mt * 128:c * FH + (mt + 1) * 128],
                        in_=bk.h[:, 0:64].bitcast(BF16)[:, 0:128])
                    bk.rd("s")
    s_vtm = ctx.cnt["s"]

    emit_families(ns, s, csp)

    emit_chunks(ns, s, csp)

    emit_bands_gn(ns, s, csp)

    @block.vector
    def _(v):
        v = ctx.ch("v", v)
        ctx.wait(v, "v", "t", ctx.cnt["t"])
        ctx.wait(v, "v", "v", ctx.cnt["v"])
        for mt in range(4):
            v.tensor_copy(out=ns.kb[:, mt * P3:mt * P3 + 3],
                          in_=ns.kb[:, mt * P3 + SEC:mt * P3 + SEC + 3])
            v.tensor_copy(out=ns.vb[:, mt * P3:mt * P3 + 3],
                          in_=ns.vb[:, mt * P3 + SEC:mt * P3 + SEC + 3])
            inst = v.tensor_copy(out=ns.csb[:, mt * P3:mt * P3 + 3],
                                 in_=ns.csb[:, mt * P3 + SEC:mt * P3 + SEC + 3])
        for p in range(8):
            hp = ns.hb[:, p * (SEC + 1):(p + 1) * (SEC + 1)]
            inst = v.tensor_copy(out=hp[:, 0:1], in_=hp[:, SEC:SEC + 1])
    ns.carry_v = ctx.cnt["v"]


def emit_families(ns, s, csp):
    ctx, block, bank = ns.ctx, ns.block, ns.bank
    P3 = SEC + 3

    def fam(f):
        return getattr(ns, "fam_" + f)

    for mt in range(4):
        cp = csp(mt)
        co = mt * SEC
        c3 = mt * P3

        def g32(base):
            return bc_blocks(cp, base, 32, 16)

        def g8(base):
            return bc_blocks(cp, base, 8, 64)

        def cs_in(off):
            return grp(cp, off, 32, 16)

        def cs_in8(off):
            return grp(cp, off, 8, 64)

        @block.vector
        def _(v):
            v = ctx.ch("v", v)
            ctx.wait(v, "v", "s", ctx.cnt["s"])
            ctx.wait(v, "v", "t", ns.fam_free)
            v.tensor_tensor(out=grp(ns.arg1, 0, 32, 16), in0=cs_in(3),
                            in1=g32(34), op=ALU.subtract)
            inst = v.tensor_tensor(out=grp(ns.arg2, 0, 32, 16), in0=g32(2),
                                   in1=cs_in(2), op=ALU.subtract)

        @block.scalar
        def _(sc):
            sc = ctx.ch("s", sc)
            ctx.wait(sc, "s", "v", ctx.cnt["v"])
            sc.activation(out=fam("kr32_1")[:, co:co + SEC], in_=ns.arg1[:, :],
                          func=AF.Exp)
            inst = sc.activation(out=fam("r32")[:, co:co + SEC],
                                 in_=ns.arg2[:, :], func=AF.Exp)

        @block.vector
        def _(v):
            v = ctx.ch("v", v)
            ctx.wait(v, "v", "s", ctx.cnt["s"])
            v.tensor_tensor(out=grp(ns.arg1, 0, 8, 64), in0=cs_in8(3),
                            in1=g8(10), op=ALU.subtract)
            inst = v.tensor_tensor(out=grp(ns.arg2, 0, 8, 64), in0=g8(2),
                                   in1=cs_in8(2), op=ALU.subtract)

        @block.scalar
        def _(sc):
            sc = ctx.ch("s", sc)
            ctx.wait(sc, "s", "v", ctx.cnt["v"])
            sc.activation(out=fam("kr8_1")[:, co:co + SEC], in_=ns.arg1[:, :],
                          func=AF.Exp)
            inst = sc.activation(out=fam("r8")[:, co:co + SEC],
                                 in_=ns.arg2[:, :], func=AF.Exp)

        @block.vector
        def _(v):
            v = ctx.ch("v", v)
            ctx.wait(v, "v", "s", ctx.cnt["s"])
            v.tensor_tensor(out=ns.arg1[:, :], in0=cp[:, 1:SEC + 1],
                            in1=cp[:, 2:SEC + 2], op=ALU.subtract)
            inst = v.tensor_tensor(out=ns.arg2[:, :], in0=cp[:, 0:SEC],
                                   in1=cp[:, 2:SEC + 2], op=ALU.subtract)

        @block.scalar
        def _(sc):
            sc = ctx.ch("s", sc)
            ctx.wait(sc, "s", "v", ctx.cnt["v"])
            sc.activation(out=ns.b2f[:, co:co + SEC], in_=ns.arg1[:, :],
                          func=AF.Exp)
            inst = sc.activation(out=ns.b3f[:, co:co + SEC], in_=ns.arg2[:, :],
                                 func=AF.Exp)
        s_bases = ctx.cnt["s"]

        @block.vector
        def _(v):
            v = ctx.ch("v", v)
            ctx.wait(v, "v", "s", ctx.cnt["s"])
            v.memset(ns.arg1[:, 0:17], 0.0)
            v.tensor_tensor(out=ns.arg1[:, 0:15],
                            in0=cols(cp, 34, 32, 15),
                            in1=cols(cp, 66, 32, 15),
                            op=ALU.subtract)
            v.tensor_tensor(out=ns.arg1[:, 32:32 + NB8],
                            in0=cols(cp, 10, 8, NB8),
                            in1=cols(cp, 6, 8, NB8),
                            op=ALU.subtract)
            v.tensor_tensor(out=ns.arg1[:, 96:96 + NB8],
                            in0=cols(cp, 6, 8, NB8),
                            in1=cols(cp, 2, 8, NB8),
                            op=ALU.subtract)
            v.tensor_scalar_min(out=ns.arg1[:, 32:32 + 2 * NB8],
                                in0=ns.arg1[:, 32:32 + 2 * NB8], scalar1=DCLAMP)
            v.tensor_tensor(out=grp(ns.arg1, 160, 2, 32),
                            in0=cols(cp, 10, 8, NB8).rearrange("p (b i) -> p b i", i=2),
                            in1=bc_blocks(cp, 18, 16, 32, rep=2),
                            op=ALU.subtract)
            inst = v.tensor_tensor(out=grp(ns.arg1, 224, 2, 32),
                                   in0=bc_blocks(cp, 2, 16, 32, rep=2),
                                   in1=cols(cp, 2, 8, NB8).rearrange("p (b i) -> p b i", i=2),
                                   op=ALU.subtract)
            v.tensor_tensor(out=grp(ns.arg1, 288, 4, 4),
                            in0=cols(cp, 34, 32, 16).rearrange("p (b i) -> p b i", i=4),
                            in1=bc_blocks(cp, 130, 128, 4, rep=4),
                            op=ALU.subtract)
            inst2 = v.tensor_tensor(out=grp(ns.arg1, 304, 4, 4),
                                    in0=bc_blocks(cp, 2, 128, 4, rep=4),
                                    in1=cols(cp, 2, 32, 16).rearrange("p (b i) -> p b i", i=4),
                                    op=ALU.subtract)
            inst3 = v.tensor_tensor(out=ns.arg1[:, 320:320 + NCH],
                                    in0=cols(cp, 2, 128, NCH),
                                    in1=cols(cp, 130, 128, NCH),
                                    op=ALU.subtract)

        @block.scalar
        def _(sc):
            sc = ctx.ch("s", sc)
            ctx.wait(sc, "s", "v", ctx.cnt["v"])
            sc.activation(out=ns.ef32[:, mt * 17:mt * 17 + 17],
                          in_=ns.arg1[:, 0:17], func=AF.Exp)
            sc.activation(out=ns.efh8[:, mt * NB8:(mt + 1) * NB8],
                          in_=ns.arg1[:, 32:32 + NB8], func=AF.Exp)
            sc.activation(out=ns.efh8b[:, mt * NB8:(mt + 1) * NB8],
                          in_=ns.arg1[:, 96:96 + NB8], func=AF.Exp)
            sc.activation(out=ns.f16k[:, mt * NB8:(mt + 1) * NB8],
                          in_=ns.arg1[:, 160:160 + NB8], func=AF.Exp)
            sc.activation(out=ns.f16r[:, mt * NB8:(mt + 1) * NB8],
                          in_=ns.arg1[:, 224:224 + NB8], func=AF.Exp)
            sc.activation(out=ns.fchk[:, mt * NB32:(mt + 1) * NB32],
                          in_=ns.arg1[:, 288:288 + NB32], func=AF.Exp)
            sc.activation(out=ns.fchkr[:, mt * NB32:(mt + 1) * NB32],
                          in_=ns.arg1[:, 304:304 + NB32], func=AF.Exp)
            inst = sc.activation(out=ns.etot[:, mt * NCH:(mt + 1) * NCH],
                                 in_=ns.arg1[:, 320:320 + NCH], func=AF.Exp)
        s_fac = ctx.cnt["s"]

        @block.vector
        def _(v):
            v = ctx.ch("v", v)
            ctx.wait(v, "v", "s", s_fac)
            kv = ns.kb[:, c3 + 3:c3 + 3 + SEC]
            rv = ns.rb[:, co:co + SEC]
            F = lambda f: fam(f)[:, co:co + SEC]
            v.tensor_tensor(out=F("kr32_2"), in0=F("kr32_1"),
                            in1=bc_blocks(ns.ef32, mt * 17, 1, 16, rep=32),
                            op=ALU.mult)
            v.tensor_tensor(out=F("kr32_3"), in0=F("kr32_2"),
                            in1=bc_blocks(ns.ef32, mt * 17 + 1, 1, 16, rep=32),
                            op=ALU.mult)
            v.tensor_tensor(out=F("kr16_1"), in0=F("kr8_1"),
                            in1=bc_blocks(ns.f16k, mt * NB8, 1, NB8, rep=8),
                            op=ALU.mult)
            v.tensor_tensor(out=F("r16"), in0=F("r8"),
                            in1=bc_blocks(ns.f16r, mt * NB8, 1, NB8, rep=8),
                            op=ALU.mult)
            v.tensor_tensor(out=F("kd8"), in0=F("kr8_1"),
                            in1=bc_blocks(ns.efh8, mt * NB8, 1, NB8, rep=8),
                            op=ALU.mult)
            v.tensor_tensor(out=F("rd8"), in0=F("r8"),
                            in1=bc_blocks(ns.efh8b, mt * NB8, 1, NB8, rep=8),
                            op=ALU.mult)
            v.tensor_tensor(out=F("khat"), in0=F("kr32_1"),
                            in1=bc_blocks(ns.fchk, mt * NB32, 1, NB32, rep=32),
                            op=ALU.mult)
            v.tensor_tensor(out=F("rchunk"), in0=F("r32"),
                            in1=bc_blocks(ns.fchkr, mt * NB32, 1, NB32, rep=32),
                            op=ALU.mult)
            for f in ("kr32_1", "kr32_2", "kr32_3", "kr16_1", "kr8_1", "kd8",
                      "khat"):
                v.tensor_tensor(out=F(f), in0=F(f), in1=kv, op=ALU.mult)
            for f in ("r32", "r16", "r8", "rd8", "rchunk"):
                inst = v.tensor_tensor(out=F(f), in0=F(f), in1=rv, op=ALU.mult)
    ns.v_fam = ctx.cnt["v"]

    @block.tensor
    def _(t):
        ctx.wait(t, "t", "v", ns.v_fam)
        for c in range(NCH):
            for mt in range(4):
                bk = bank[6 + (c * 4 + mt) % 2]
                src = ns.fam_khat[:, mt * SEC + c * 128:mt * SEC + (c + 1) * 128]
                bk.tr(t, bk.h[:, 0:64].bitcast(BF16)[:, 0:128], src,
                      ns.ident[:, :])

                @block.scalar
                def _(sc, bk=bk, c=c, mt=mt):
                    sc = ctx.ch("s", sc)
                    ctx.wait(sc, "s", "t", ctx.cnt["t"])
                    inst = sc.copy(
                        out=ns.khtm[:, c * FH + mt * 128:c * FH + (mt + 1) * 128],
                        in_=bk.h[:, 0:64].bitcast(BF16)[:, 0:128])
                    bk.rd("s")
    ns.s_khtm = ctx.cnt["s"]


def emit_chunks(ns, s, csp):
    ctx, block, bank = ns.ctx, ns.block, ns.bank

    def fsl(f, h, cols):
        mt, hr = h // 2, 64 * (h % 2)
        return getattr(ns, "fam_" + f)[hr:hr + 64,
                                       mt * SEC + cols[0]:mt * SEC + cols[1]]

    for c in range(NCH):
        @block.vector
        def _(v):
            v = ctx.ch("v", v)
            ctx.wait(v, "v", "v", ctx.cnt["v"])
            ctx.wait(v, "v", "t", getattr(ns, "S_read", 0))
            inst = v.tensor_copy(out=ns.Sbf[:, :], in_=ns.Ssb[:, :])
        v_sbf = ctx.cnt["v"]

        for h in range(NHL):
            mt, hr = h // 2, 64 * (h % 2)
            cc = c * 128
            pm = bank[h % 2]
            a1, a2, a3 = bank[2], bank[3], bank[6]
            ob = bank[4 + h % 2]

            @block.tensor
            def _(t):
                ctx.wait(t, "t", "v", max(ns.v_fam, v_sbf))
                if s > 0 or c > 0 or h > 1:
                    pass
                for j in range(1, 4):
                    for i in range(j):
                        pm.mm(t, pm.h[32 * i:32 * i + 32, 32 * j:32 * j + 32],
                              fsl("kr32_%d" % (j - i), h,
                                  (cc + 32 * i, cc + 32 * i + 32)),
                              fsl("r32", h, (cc + 32 * j, cc + 32 * j + 32)),
                              start=True, stop=True,
                              tile_position=(hr, 32 * i))
                a1.mm(t, a1.h[:, 0:128], fsl("kd8", h, (cc, cc + 128)),
                      fsl("rd8", h, (cc, cc + 128)), start=True, stop=True,
                      tile_position=(hr, 0))
                a2.mm(t, a2.h[:, 0:128], fsl("kr8_1", h, (cc, cc + 128)),
                      fsl("r8", h, (cc, cc + 128)), start=True, stop=True,
                      tile_position=(hr, 0))
                a3.mm(t, a3.h[:, 0:128], fsl("kr16_1", h, (cc, cc + 128)),
                      fsl("r16", h, (cc, cc + 128)), start=True, stop=True,
                      tile_position=(hr, 0))
            t_pieces = ctx.cnt["t"]

            @block.vector
            def _(v):
                v = ctx.ch("v", v)
                ctx.wait(v, "v", "t", t_pieces)
                v.tensor_tensor(out=ns.psb[:, 0:128], in0=pm.h[:, 0:128],
                                in1=ns.mstr[:, :], op=ALU.mult)
                pm.rd("v")
                v.tensor_tensor(out=ns.psb[:, 128:256], in0=a1.h[:, 0:128],
                                in1=ns.md8[:, :], op=ALU.mult)
                a1.rd("v")
                v.tensor_tensor(out=ns.psb[:, 256:384], in0=a2.h[:, 0:128],
                                in1=ns.m8[:, :], op=ALU.mult)
                a2.rd("v")
                inst = v.tensor_tensor(out=ns.psb[:, 384:512],
                                       in0=a3.h[:, 0:128],
                                       in1=ns.m16[:, :], op=ALU.mult)
                a3.rd("v")
            v_psb = ctx.cnt["v"]
            s_psb = ctx.cnt["s"]

            @block.tensor
            def _(t):
                ctx.wait(t, "t", "v", v_psb)
                ctx.wait(t, "t", "s", max(s_psb, ns.s_khtm,
                                          getattr(ns, "s_vtm", 0)))
                vsl = ns.vtm[:, c * FH + 64 * h:c * FH + 64 * h + 64]
                for pc in range(4):
                    ob.mm(t, ob.h[0:64, 0:128], vsl,
                          ns.psb[:, 128 * pc:128 * pc + 128],
                          start=(pc == 0), stop=False)
                ob.mm(t, ob.h[0:64, 0:128], ns.Sbf[hr:hr + 64, 64 * mt:64 * mt + 64],
                      fsl("rchunk", h, (cc, cc + 128)), start=False, stop=True)
                ksl = ns.khtm[:, c * FH + 64 * h:c * FH + 64 * h + 64]
                a1.mm(t, a1.h[hr:hr + 64, 0:64], ksl, vsl, start=True,
                      stop=True, tile_position=(0, hr))
                ns.S_read = ctx.cnt["t"]

            @block.scalar
            def _(sc):
                sc = ctx.ch("s", sc)
                ctx.wait(sc, "s", "t", ctx.cnt["t"])
                inst = sc.copy(out=ns.osec[hr:hr + 64,
                                           mt * SEC + cc:mt * SEC + cc + 128],
                               in_=ob.h[0:64, 0:128])
                ob.rd("s")

            @block.vector
            def _(v):
                v = ctx.ch("v", v)
                ctx.wait(v, "v", "t", ctx.cnt["t"])
                inst = v.scalar_tensor_tensor(
                    out=ns.Ssb[hr:hr + 64, 64 * mt:64 * mt + 64],
                    in0=ns.Ssb[hr:hr + 64, 64 * mt:64 * mt + 64],
                    scalar=ns.etot[hr:hr + 64, mt * NCH + c:mt * NCH + c + 1],
                    in1=a1.h[hr:hr + 64, 0:64],
                    op0=ALU.mult, op1=ALU.add)
                a1.rd("v")
    ns.fam_free = ctx.cnt["t"]


def emit_bands_gn(ns, s, csp):
    ctx, block, bank = ns.ctx, ns.block, ns.bank
    P3 = SEC + 3

    for d in range(4):
        @block.vector
        def _(v, d=d):
            v = ctx.ch("v", v)
            ctx.wait(v, "v", "t", getattr(ns, "band_mm", 0))
            for mt in range(4):
                rv = ns.rb[:, mt * SEC:(mt + 1) * SEC]
                pb = ns.prodb[:, mt * SEC:(mt + 1) * SEC]
                if d == 0:
                    v.tensor_scalar(out=pb, in0=rv,
                                    scalar1=ns.v_uu[:, mt:mt + 1],
                                    scalar2=None, op0=ALU.mult)
                    v.tensor_tensor(
                        out=pb, in0=pb,
                        in1=ns.kb[:, mt * P3 + 3:mt * P3 + 3 + SEC],
                        op=ALU.mult)
                else:
                    ksh = ns.kb[:, mt * P3 + 3 - d:mt * P3 + 3 - d + SEC]
                    v.tensor_tensor(out=pb, in0=rv, in1=ksh, op=ALU.mult)
                    if d == 2:
                        v.tensor_tensor(
                            out=pb, in0=pb,
                            in1=ns.b2f[:, mt * SEC:(mt + 1) * SEC],
                            op=ALU.mult)
                    elif d == 3:
                        v.tensor_tensor(
                            out=pb, in0=pb,
                            in1=ns.b3f[:, mt * SEC:(mt + 1) * SEC],
                            op=ALU.mult)
        v_prod = ctx.cnt["v"]

        @block.tensor
        def _(t, d=d):
            ctx.wait(t, "t", "v", v_prod)
            bk = bank[6 + d % 2]
            for mt in range(4):
                bk.mm(t, bk.h[0:8, 0:SEC], ns.indbt[:, 8 * mt:8 * mt + 8],
                      ns.prodb[:, mt * SEC:(mt + 1) * SEC],
                      start=(mt == 0), stop=(mt == 3))
            ns.band_mm = ctx.cnt["t"]

        @block.vector
        def _(v, d=d):
            v = ctx.ch("v", v)
            ctx.wait(v, "v", "t", ns.band_mm)
            bk = bank[6 + d % 2]
            bm = getattr(ns, "browm%d" % d)
            if d == 0:
                v.tensor_copy(out=bm[:, 0:SEC], in_=bk.h[0:8, 0:SEC])
            else:
                v.tensor_tensor(out=bm[:, 0:SEC], in0=bk.h[0:8, 0:SEC],
                                in1=ns.bmask[:, (d - 1) * SEC:d * SEC],
                                op=ALU.mult)
            bk.rd("v")
    v_brow = ctx.cnt["v"]

    for d in range(4):
        for mt in range(4):
            @block.tensor
            def _(t, d=d, mt=mt):
                ctx.wait(t, "t", "v", max(v_brow, getattr(ns, "band_ap", 0)))
                bk = bank[4 + mt % 2]
                bk.mm(t, bk.h[:, 0:SEC], ns.indb[:, 128 * mt:128 * mt + 128],
                      getattr(ns, "browm%d" % d)[:, 0:SEC], start=True,
                      stop=True)

                @block.vector
                def _(v, bk=bk, d=d, mt=mt):
                    v = ctx.ch("v", v)
                    ctx.wait(v, "v", "t", ctx.cnt["t"])
                    vsh = ns.vb[:, mt * P3 + 3 - d:mt * P3 + 3 - d + SEC]
                    v.tensor_tensor(out=ns.ewb[:, mt * SEC:(mt + 1) * SEC],
                                    in0=bk.h[:, 0:SEC], in1=vsh, op=ALU.mult)
                    bk.rd("v")
                    v.tensor_tensor(
                        out=ns.osec[:, mt * SEC:(mt + 1) * SEC],
                        in0=ns.osec[:, mt * SEC:(mt + 1) * SEC],
                        in1=ns.ewb[:, mt * SEC:(mt + 1) * SEC],
                        op=ALU.add)
                    ns.band_ap = ctx.cnt["v"]

    for mt in range(4):
        @block.scalar
        def _(sc, mt=mt):
            sc = ctx.ch("s", sc)
            ctx.wait(sc, "s", "v", ctx.cnt["v"])
            sc.activation(out=ns.ewb[:, mt * SEC:(mt + 1) * SEC],
                          in_=ns.osec[:, mt * SEC:(mt + 1) * SEC],
                          func=AF.Square)
    s_sq = ctx.cnt["s"]

    @block.tensor
    def _(t):
        ctx.wait(t, "t", "s", s_sq)
        ctx.wait(t, "t", "v", ctx.cnt["v"])
        for mt in range(4):
            bank[6].mm(t, bank[6].h[0:8, 0:SEC],
                       ns.indbtf[:, 8 * mt:8 * mt + 8],
                       ns.osec[:, mt * SEC:(mt + 1) * SEC],
                       start=(mt == 0), stop=(mt == 3))
        for mt in range(4):
            bank[7].mm(t, bank[7].h[0:8, 0:SEC],
                       ns.indbtf[:, 8 * mt:8 * mt + 8],
                       ns.ewb[:, mt * SEC:(mt + 1) * SEC],
                       start=(mt == 0), stop=(mt == 3))
    t_gsum = ctx.cnt["t"]

    @block.vector
    def _(v):
        v = ctx.ch("v", v)
        ctx.wait(v, "v", "t", t_gsum)
        sr = ns.srow
        v.tensor_scalar(out=ns.mean8[:, 0:SEC], in0=bank[6].h[0:8, 0:SEC],
                        scalar1=1.0 / 64, scalar2=None, op0=ALU.mult)
        v.tensor_scalar(out=sr[0:8, 0:SEC], in0=bank[7].h[0:8, 0:SEC],
                        scalar1=1.0 / 64, scalar2=None, op0=ALU.mult)
        bank[6].rd("v")
        bank[7].rd("v")
        v.tensor_tensor(out=ns.svar[:, 0:SEC], in0=ns.mean8[:, 0:SEC],
                        in1=ns.mean8[:, 0:SEC], op=ALU.mult)
        v.tensor_tensor(out=ns.svar[:, 0:SEC], in0=sr[0:8, 0:SEC],
                        in1=ns.svar[:, 0:SEC], op=ALU.subtract)
        v.tensor_scalar(out=ns.svar[:, 0:SEC], in0=ns.svar[:, 0:SEC],
                        scalar1=EPS_GN, scalar2=None, op0=ALU.add)
        v.reciprocal(out=ns.rstd8[:, 0:SEC], in_=ns.svar[:, 0:SEC])

    @block.scalar
    def _(sc):
        sc = ctx.ch("s", sc)
        ctx.wait(sc, "s", "v", ctx.cnt["v"])
        sc.activation(out=ns.rstd8[:, 0:SEC], in_=ns.rstd8[:, 0:SEC],
                      func=AF.Sqrt)
    s_rstd8 = ctx.cnt["s"]

    for mt in range(4):
        @block.tensor
        def _(t, mt=mt):
            ctx.wait(t, "t", "s", s_rstd8)
            bk2 = bank[mt % 2]
            bk3 = bank[2 + mt % 2]
            bk2.mm(t, bk2.h[:, 0:SEC], ns.indbf[:, 128 * mt:128 * mt + 128],
                   ns.mean8[:, 0:SEC], start=True, stop=True)
            bk3.mm(t, bk3.h[:, 0:SEC], ns.indbf[:, 128 * mt:128 * mt + 128],
                   ns.rstd8[:, 0:SEC], start=True, stop=True)

        @block.vector
        def _(v, mt=mt):
            v = ctx.ch("v", v)
            ctx.wait(v, "v", "t", ctx.cnt["t"])
            ctx.dwait(v, "v", getattr(ns, "omix_st", None))
            bk2 = bank[mt % 2]
            bk3 = bank[2 + mt % 2]
            ob = ns.osec[:, mt * SEC:(mt + 1) * SEC]
            v.tensor_tensor(out=ob, in0=ob, in1=bk2.h[:, 0:SEC],
                            op=ALU.subtract)
            bk2.rd("v")
            v.tensor_tensor(out=ob, in0=ob, in1=bk3.h[:, 0:SEC], op=ALU.mult)
            bk3.rd("v")
            v.tensor_scalar(out=ob, in0=ob, scalar1=ns.v_gnw[:, mt:mt + 1],
                            scalar2=ns.v_gnb[:, mt:mt + 1], op0=ALU.mult,
                            op1=ALU.add)
            v.tensor_tensor(out=ns.omix[:, mt * SEC:(mt + 1) * SEC], in0=ob,
                            in1=ns.gb[:, mt * SEC:(mt + 1) * SEC],
                            op=ALU.mult)
    v_omix = ctx.cnt["v"]

    @block.sync
    def _(q):
        ctx.wait(q, "q", "v", v_omix)
        ns.omix_st = ctx.dgroup(q, "q", [
            (ns.ORW_B[128 * mt:128 * (mt + 1), s * SEC:(s + 1) * SEC],
             ns.omix[:, mt * SEC:(mt + 1) * SEC]) for mt in range(4)], "sq")
        ns.omix_keys.append(ns.omix_st)


def emit_mixer_taps(ns):
    ctx, block = ns.ctx, ns.block
    if not ns.taps:
        return

    @block.sync
    def _(q):
        ctx.wait(q, "q", "v", ctx.cnt["v"])
        ctx.wait(q, "q", "s", ctx.cnt["s"])
        prs = []
        for nm, ap in ns.taps.items():
            src = getattr(ns, nm, None)
            if src is None:
                continue
            prs.append((ap, src[:, 0:ap.shape[1]]))
        ns.tap_st = ctx.dgroup(q, "q", prs, "sq")


def emit_finish(ns):
    ctx, block = ns.ctx, ns.block

    @block.sync
    def _(q):
        ctx.wait(q, "q", "v", ctx.cnt["v"])
        srcbuf = getattr(ns, "x2buf", None)
        if srcbuf is not None:
            key = ctx.dgroup(q, "q", [
                (ns.OUT[128 * p:128 * (p + 1), 0:TO],
                 srcbuf[:, p * TO:(p + 1) * TO]) for p in range(8)], "sq")
        else:
            key = ctx.dgroup(q, "q", [
                (ns.OUT[128 * p:128 * (p + 1), 0:TO], ns.xs[:, 0:TO])
                for p in range(8)], "sq")
        for nm, val in ctx.dcnt.items():
            if val > 0:
                q.wait_ge(ctx.dsem[nm], val)


def _build_consts():
    c = {}
    i = np.arange(128)
    c["ident"] = np.eye(128, dtype=np.float32)
    c["indh"] = np.stack([(i < 64).astype(np.float32),
                          (i >= 64).astype(np.float32)], 1)
    c["indhb"] = c["indh"]
    indb = np.zeros((8, 512), np.float32)
    for mt in range(4):
        indb[2 * mt, 128 * mt:128 * mt + 64] = 1.0
        indb[2 * mt + 1, 128 * mt + 64:128 * mt + 128] = 1.0
    c["indb"] = indb
    c["indbf"] = indb
    indbt = np.zeros((128, 32), np.float32)
    for mt in range(4):
        indbt[0:64, 8 * mt + 2 * mt] = 1.0
        indbt[64:128, 8 * mt + 2 * mt + 1] = 1.0
    c["indbt"] = indbt
    c["indbtf"] = indbt
    ss, tt = np.meshgrid(i, i, indexing="ij")
    same8 = (ss // 8) == (tt // 8)
    c["md8"] = (same8 & (tt - ss >= 4)).astype(np.float32)
    same32 = (tt // 32) == (ss // 32)
    c["m8"] = (((tt // 8) == (ss // 8) + 1) & same32).astype(np.float32)
    c["m16"] = (((tt // 16) == (ss // 16) + 1) &
                ((tt // 8) >= (ss // 8) + 2) & same32).astype(np.float32)
    c["mstr"] = ((tt // 32) > (ss // 32)).astype(np.float32)
    tmod = np.arange(SEC) % 8
    bm = np.zeros((8, 3 * SEC), np.float32)
    for d in (1, 2, 3):
        bm[:, (d - 1) * SEC:d * SEC] = (tmod >= d)[None, :]
    c["bmask"] = bm
    c["onesc"] = np.ones((1, 128), np.float32)
    c["onesr"] = np.ones((128, 1), np.float32)
    c["identf"] = np.eye(128, dtype=np.float32)
    perm = np.zeros((128, 128), np.float32)
    for mm_ in range(128):
        s_ = mm_ + 32 if (mm_ % 64) < 32 else mm_ - 32
        perm[s_, mm_] = 1.0
    c["rope_perm"] = perm
    inde = np.zeros((16, 16 * 128), np.float32)
    for e in range(16):
        inde[e, e * 128:(e + 1) * 128] = 1.0
    c["inde16"] = inde
    pos = np.arange(TF, dtype=np.float64)
    dim = np.arange(32, dtype=np.float64)
    ang = pos[None, :] / (10000.0 ** (dim / 32.0))[:, None]
    sn, cs_ = np.sin(ang), np.cos(ang)
    rs = np.zeros((128, TF)); rc = np.zeros((128, TF))
    for half in range(2):
        b = 64 * half
        rs[b:b + 32] = -sn
        rs[b + 32:b + 64] = sn
        rc[b:b + 32] = cs_
        rc[b + 32:b + 64] = cs_
    c["ropesin"] = rs
    c["ropecos"] = rc
    ii, qq = np.meshgrid(np.arange(128), np.arange(512), indexing="ij")
    cm = np.zeros((128, 4 * 512), np.float32)
    for j in range(4):
        cm[:, j * 512:(j + 1) * 512] = (qq - ii >= 128 * j)
    c["cmask"] = cm
    return c


def _pack_vec(x, ncols):
    return np.ascontiguousarray(x.reshape(ncols, 128).T)


def make_in_maps(inputs, n_sec=TF // SEC):
    consts = _build_consts()
    bf = np.dtype("bfloat16") if False else None
    import ml_dtypes
    BFD = ml_dtypes.bfloat16
    in_maps = []
    x = np.asarray(inputs["L0_attn_x"], np.float32)
    for core in range(8):
        b, rank = core // 2, core % 2
        hs = slice(rank * FH, (rank + 1) * FH)
        m = {}
        xf = np.ascontiguousarray(x[b].T)
        m["xf"] = xf
        m["xo"] = np.ascontiguousarray(xf[:, rank * TO:(rank + 1) * TO])
        m["wr"] = np.ascontiguousarray(inputs["L0_attn_Wr"][:, hs], np.float32)
        m["wk"] = np.ascontiguousarray(inputs["L0_attn_Wk"][:, hs], np.float32)
        m["wv"] = np.ascontiguousarray(inputs["L0_attn_Wv"][:, hs], np.float32)
        m["wg"] = np.ascontiguousarray(inputs["L0_attn_Wg"][:, hs], np.float32)
        m["wo"] = np.ascontiguousarray(inputs["L0_attn_Wo"], np.float32)
        m["w1"] = np.ascontiguousarray(inputs["L0_attn_ffn_w1"], np.float32)
        m["w2"] = np.ascontiguousarray(inputs["L0_attn_ffn_w2"], np.float32)
        m["wqt"] = np.ascontiguousarray(np.asarray(inputs["L0_ffn_weight"], np.float32).T)
        m["wla"] = np.ascontiguousarray(inputs["L0_attn_wA"], np.float32)
        m["wlb"] = np.ascontiguousarray(inputs["L0_attn_wB"][:, hs], np.float32)
        for nm, key in (("mu_r", "L0_attn_mu_r"), ("mu_k", "L0_attn_mu_k"),
                        ("mu_v", "L0_attn_mu_v"), ("mu_w", "L0_attn_mu_w"),
                        ("mu_g", "L0_attn_mu_g"), ("ln1w", "L0_attn_ln1_w"),
                        ("ln2w", "L0_attn_ln2_w")):
            m[nm] = _pack_vec(np.asarray(inputs[key], np.float32), 8)
        m["w0"] = _pack_vec(np.asarray(inputs["L0_attn_w0"], np.float32)[rank * FH:(rank + 1) * FH], 4)
        m["uu"] = _pack_vec(np.asarray(inputs["L0_attn_u"], np.float32).reshape(-1)[rank * FH:(rank + 1) * FH], 4)
        m["gnw"] = _pack_vec(np.asarray(inputs["L0_attn_gn_w"], np.float32).reshape(-1)[rank * FH:(rank + 1) * FH], 4)
        m["gnb"] = _pack_vec(np.asarray(inputs["L0_attn_gn_b"], np.float32).reshape(-1)[rank * FH:(rank + 1) * FH], 4)
        for k, v in consts.items():
            if k in ("ident", "indhb", "indb", "indbt", "md8", "m8", "m16", "mstr", "bmask", "inde16", "cmask"):
                m[k] = v.astype(BFD)
            else:
                m[k] = v.astype(np.float32)
        hsel = slice(rank * 8, (rank + 1) * 8)
        for nm, key in (("l1wq", "L1_attn_W_Q"), ("l1wk", "L1_attn_W_K"),
                        ("l1wv", "L1_attn_W_V")):
            w3 = np.asarray(inputs[key], np.float32)[hsel]
            m[nm] = np.ascontiguousarray(w3.transpose(1, 0, 2).reshape(D, FH))
        m["l1wo"] = np.ascontiguousarray(
            np.asarray(inputs["L1_attn_W_O"], np.float32).reshape(D, D))
        m["l1bq"] = _pack_vec(np.asarray(inputs["L1_attn_b_Q"],
                                         np.float32).reshape(-1)[hs], 4)
        m["l1bk"] = _pack_vec(np.asarray(inputs["L1_attn_b_K"],
                                         np.float32).reshape(-1)[hs], 4)
        m["l1bv_row"] = np.ascontiguousarray(
            np.asarray(inputs["L1_attn_b_V"], np.float32).reshape(1, -1)[:, hs])
        m["l1bo"] = _pack_vec(np.asarray(inputs["L1_attn_b_O"], np.float32), 8)
        m["gate_wt"] = np.ascontiguousarray(
            np.asarray(inputs["L1_ffn_gate_w"], np.float32).T)
        m["bias_e"] = np.ascontiguousarray(
            np.asarray(inputs["L1_ffn_bias_e"], np.float32).reshape(1, 16))
        m["moe_up"] = np.ascontiguousarray(
            np.asarray(inputs["L1_ffn_up_w"], np.float32).transpose(0, 2, 1)
            .reshape(MOE_NE * D, 512))
        m["moe_dn"] = np.ascontiguousarray(
            np.asarray(inputs["L1_ffn_down_w"], np.float32).transpose(0, 2, 1)
            .reshape(MOE_NE * 512, D))
        m["sh_upt"] = np.ascontiguousarray(
            np.asarray(inputs["L1_ffn_shared_up"], np.float32).T)
        m["sh_dnt"] = np.ascontiguousarray(
            np.asarray(inputs["L1_ffn_shared_down"], np.float32).T)
        mab = np.zeros((128, 2), np.float32)
        mab[:, 0] = 1.0 if rank == 0 else 0.0
        mab[:, 1] = 1.0 if rank == 1 else 0.0
        m["mab"] = mab
        in_maps.append(m)
    return in_maps


def kernel(**inputs):
    nc, ns = build_program()
    in_maps = make_in_maps(inputs)
    res = run_bass_kernel_spmd(nc, in_maps, core_ids=list(range(8)))
    out = np.zeros((B, TF, D), np.float32)
    for core in range(8):
        b, rank = core // 2, core % 2
        out[b, rank * TO:(rank + 1) * TO] = res.results[core]["out"].T
    return out


def emit_phase_join(ns):
    ctx, block = ns.ctx, ns.block
    tv, ts_, tt, tg = ctx.cnt["v"], ctx.cnt["s"], ctx.cnt["t"], ctx.cnt["g"]
    dkeys = [(nm, val) for nm, val in ctx.dcnt.items() if val > 0]

    @block.vector
    def _(v):
        v = ctx.ch("v", v)
        for e, c in (("s", ts_), ("t", tt), ("g", tg)):
            ctx.wait(v, "v", e, c)
        for nm, val in dkeys:
            ctx.wait(v, "v", nm, val)
        v.engine_nop()
    vj = ctx.cnt["v"]

    @block.scalar
    def _(sc):
        ctx.wait(sc, "s", "v", vj)

    @block.tensor
    def _(t):
        ctx.wait(t, "t", "v", vj)

    @block.gpsimd
    def _(g):
        ctx.wait(g, "g", "v", vj)

    @block.sync
    def _(q):
        ctx.wait(q, "q", "v", vj)
    ns.join_v = vj


def emit_colstats(ns, buf, coff, ncols, mode, eps, scale=None, out=None,
                  bufW=None, outW=None, ooff=0):
    ctx, block, bank = ns.ctx, ns.block, ns.bank
    W = bufW if bufW is not None else ns.postW
    out = out if out is not None else buf
    outW = outW if outW is not None else W
    if out is buf:
        ooff = coff

    for p in range(8):
        @block.scalar
        def _(sc, p=p):
            sc = ctx.ch("s", sc)
            if p == 0:
                ctx.wait(sc, "s", "v", ctx.cnt["v"])
            else:
                ctx.wait(sc, "s", "t", ctx.cnt["t"])
            sc.activation(out=ns.psq[:, 0:ncols],
                          in_=buf[:, p * W + coff:p * W + coff + ncols],
                          func=AF.Square)

        @block.tensor
        def _(t, p=p):
            ctx.wait(t, "t", "s", ctx.cnt["s"])
            bank[6].mm(t, bank[6].h[0:1, 0:ncols], ns.onesr[:, :],
                       ns.psq[:, 0:ncols], start=(p == 0), stop=(p == 7))
            if mode == "ln":
                bank[7].mm(t, bank[7].h[0:1, 0:ncols], ns.onesr[:, :],
                           buf[:, p * W + coff:p * W + coff + ncols],
                           start=(p == 0), stop=(p == 7))
    t_stats = ctx.cnt["t"]

    @block.vector
    def _(v):
        v = ctx.ch("v", v)
        ctx.wait(v, "v", "t", t_stats)
        r1 = ns.prow1[0:1, 0:ncols]
        r2 = ns.prow2[0:1, 0:ncols]
        if mode == "rms":
            v.tensor_scalar(out=r1, in0=bank[6].h[0:1, 0:ncols],
                            scalar1=1.0 / D, scalar2=eps, op0=ALU.mult,
                            op1=ALU.add)
            bank[6].rd("v")
            v.reciprocal(out=r2, in_=r1)
        else:
            v.tensor_scalar(out=ns.prow3[0:1, 0:ncols],
                            in0=bank[7].h[0:1, 0:ncols],
                            scalar1=1.0 / D, scalar2=None, op0=ALU.mult)
            bank[7].rd("v")
            v.tensor_scalar(out=r1, in0=bank[6].h[0:1, 0:ncols],
                            scalar1=1.0 / D, scalar2=None, op0=ALU.mult)
            bank[6].rd("v")
            v.tensor_tensor(out=r2, in0=ns.prow3[0:1, 0:ncols],
                            in1=ns.prow3[0:1, 0:ncols], op=ALU.mult)
            v.tensor_tensor(out=r1, in0=r1, in1=r2, op=ALU.subtract)
            v.tensor_scalar(out=r1, in0=r1, scalar1=eps, scalar2=None,
                            op0=ALU.add)
            v.reciprocal(out=r2, in_=r1)

    @block.scalar
    def _(sc):
        sc = ctx.ch("s", sc)
        ctx.wait(sc, "s", "v", ctx.cnt["v"])
        sc.activation(out=ns.prow2[0:1, 0:ncols], in_=ns.prow2[0:1, 0:ncols],
                      func=AF.Sqrt)
    s_r = ctx.cnt["s"]

    @block.tensor
    def _(t):
        ctx.wait(t, "t", "s", s_r)
        bank[6].mm(t, bank[6].h[:, 0:ncols], ns.onesc[:, :],
                   ns.prow2[0:1, 0:ncols], start=True, stop=True)
        if mode == "ln":
            bank[7].mm(t, bank[7].h[:, 0:ncols], ns.onesc[:, :],
                       ns.prow3[0:1, 0:ncols], start=True, stop=True)
    t_bc = ctx.cnt["t"]

    @block.vector
    def _(v):
        v = ctx.ch("v", v)
        ctx.wait(v, "v", "t", max(t_bc, ctx.cnt["t"]))
        for p in range(8):
            src = buf[:, p * W + coff:p * W + coff + ncols]
            dst = out[:, p * outW + ooff:p * outW + ooff + ncols]
            if mode == "rms":
                if scale is not None:
                    v.scalar_tensor_tensor(out=dst, in0=src,
                                           scalar=scale[:, p:p + 1],
                                           in1=bank[6].h[:, 0:ncols],
                                           op0=ALU.mult, op1=ALU.mult)
                else:
                    v.tensor_tensor(out=dst, in0=src,
                                    in1=bank[6].h[:, 0:ncols], op=ALU.mult)
            else:
                v.tensor_tensor(out=ns.ptt[:, 0:ncols], in0=src,
                                in1=bank[7].h[:, 0:ncols], op=ALU.subtract)
                v.tensor_tensor(out=dst, in0=ns.ptt[:, 0:ncols],
                                in1=bank[6].h[:, 0:ncols], op=ALU.mult)
        bank[6].rd("v")
        if mode == "ln":
            bank[7].rd("v")
    return ctx.cnt["v"]


def emit_l0post(ns):
    ctx, block, bank, nc = ns.ctx, ns.block, ns.bank, ns.nc
    sb = ns.sb
    es2 = ns.es.enter_context(ExitStack())
    sb("xa", (128, 8 * TO), F32, stack=es2)
    sb("psq", (128, 512), F32, stack=es2)
    sb("prow1", (1, 512), F32, stack=es2)
    sb("prow2", (1, 512), F32, stack=es2)
    sb("prow3", (1, 512), F32, stack=es2)
    sb("ptt", (128, 512), F32, stack=es2)
    ns.es2x = ExitStack()
    sb("x0w", (128, 8 * 512), F32, stack=ns.es2x)
    ns.postW = TO
    ns.postWout = TO
    NCK = TO // 512

    @block.gpsimd
    def _(g):
        for key in ns.omix_keys:
            ctx.dwait(g, "g", key)
        ns.cc_orw = ctx.ccg(g, "AllGather", [ns.ORW_B], [ns.ORW_F], ns.groups)

    es3 = ExitStack()
    sb("wob", (128, 8 * D), BF16, stack=es3)
    sb("ogh0", (128, 8 * 512), BF16, stack=es3)
    sb("ogh1", (128, 8 * 512), BF16, stack=es3)
    sb("ogg", (128, 8 * 512), BF16, stack=es3)

    @block.gpsimd
    def _(g):
        ns.wo_ld = ctx.dgroup(g, "g", [
            (ns.wob[:, p * D:(p + 1) * D], ns.I_wo[128 * p:128 * (p + 1), :])
            for p in range(8)], "lg")

    for ck in range(NCK):
        @block.sync
        def _(q, ck=ck):
            ctx.dwait(q, "q", ns.cc_orw)
            ctx.wait(q, "q", "v", getattr(ns, "og_free_v", 0))
            prs = []
            for p in range(8):
                prs.append((ns.ogh0[:, p * 512:(p + 1) * 512],
                            ns.ORW_F[128 * p:128 * (p + 1),
                                     ck * 512:ck * 512 + 512]))
                prs.append((ns.ogh1[:, p * 512:(p + 1) * 512],
                            ns.ORW_F[128 * p:128 * (p + 1),
                                     TO + ck * 512:TO + ck * 512 + 512]))
            ns.og_ld = ctx.dgroup(q, "q", prs, "lq")

        @block.vector
        def _(v, ck=ck):
            v = ctx.ch("v", v)
            ctx.dwait(v, "v", ns.og_ld)
            ctx.wait(v, "v", "t", getattr(ns, "og_free", 0))
            for p in range(8):
                v.tensor_scalar(out=ns.ogg[:, p * 512:(p + 1) * 512],
                                in0=ns.ogh0[:, p * 512:(p + 1) * 512],
                                scalar1=ns.mab[:, 0:1], scalar2=None,
                                op0=ALU.mult)
                v.scalar_tensor_tensor(out=ns.ogg[:, p * 512:(p + 1) * 512],
                                       in0=ns.ogh1[:, p * 512:(p + 1) * 512],
                                       scalar=ns.mab[:, 1:2],
                                       in1=ns.ogg[:, p * 512:(p + 1) * 512],
                                       op0=ALU.mult, op1=ALU.add)
            ns.og_free_v = ctx.cnt["v"]
        v_og = ctx.cnt["v"]

        @block.sync
        def _(q, ck=ck):
            ctx.wait(q, "q", "v", getattr(ns, "x0w_free", 0))
            ns.x0w_ld = ctx.dgroup(q, "q", [
                (ns.x0w[:, p * 512:(p + 1) * 512],
                 ns.I_xo[128 * p:128 * (p + 1), ck * 512:ck * 512 + 512])
                for p in range(8)], "lq")

        for half in range(2):
            @block.tensor
            def _(t, ck=ck, half=half):
                ctx.wait(t, "t", "v", v_og)
                ctx.dwait(t, "t", ns.wo_ld)
                for k in range(8):
                    for mi in range(4):
                        mt = half * 4 + mi
                        bk = bank[mi]
                        bk.mm(t, bk.h[:, 0:512],
                              ns.wob[:, k * D + mt * 128:k * D + (mt + 1) * 128],
                              ns.ogg[:, k * 512:(k + 1) * 512],
                              start=(k == 0), stop=(k == 7))
                ns.og_free = ctx.cnt["t"]

            @block.vector
            def _(v, ck=ck, half=half):
                v = ctx.ch("v", v)
                ctx.wait(v, "v", "t", ctx.cnt["t"])
                ctx.dwait(v, "v", ns.x0w_ld)
                for mi in range(4):
                    mt = half * 4 + mi
                    bk = bank[mi]
                    v.tensor_tensor(
                        out=ns.xa[:, mt * TO + ck * 512:mt * TO + ck * 512 + 512],
                        in0=bk.h[:, 0:512],
                        in1=ns.x0w[:, mt * 512:(mt + 1) * 512],
                        op=ALU.add)
                    bk.rd("v")
                ns.x0w_free = ctx.cnt["v"]
    emit_phase_join(ns)
    es3.close()
    ns.og_ld = None
    if ns.stop_after == "wo":
        ns.x2buf = ns.xa
        emit_phase_join(ns)
        ns.es2x.close()
        return

    es4 = ExitStack()
    sb("h2c", (128, 8 * TO), BF16, stack=es4)
    sb("wf1h", (128, 8 * 2048), BF16, stack=es4)
    sb("wf2h", (128, 16 * D), BF16, stack=es4)
    sb("hallh", (128, 16 * 256), BF16, stack=es4)

    ns.wf_ld = None
    ns.wf_free = 0
    for ck in range(NCK):
        emit_colstats(ns, ns.xa, ck * 512, 512, "rms", EPS_RMS,
                      scale=ns.v_ln2w, out=ns.h2c, bufW=TO, outW=TO,
                      ooff=ck * 512)
    v_h2 = ctx.cnt["v"]
    for hib2 in range(2):
        @block.gpsimd
        def _(g, hib2=hib2):
            g = ctx.ch("g", g)
            ctx.wait(g, "g", "t", ns.wf_free)
            prs = []
            for p in range(8):
                prs.append((ns.wf1h[:, p * 2048:(p + 1) * 2048],
                            ns.I_w1[128 * p:128 * (p + 1),
                                    hib2 * 2048:(hib2 + 1) * 2048]))
            for kk in range(16):
                prs.append((ns.wf2h[:, kk * D:(kk + 1) * D],
                            ns.I_w2[hib2 * 2048 + 128 * kk:
                                    hib2 * 2048 + 128 * (kk + 1), :]))
            ns.wf_ld = ctx.dgroup(g, "g", prs, "lg")

        for ck in range(NCK):
            for tc2 in range(2):
                tco = ck * 512 + tc2 * 256
                for ib4 in range(4):
                    @block.tensor
                    def _(t, ib4=ib4, tco=tco):
                        ctx.wait(t, "t", "v", v_h2)
                        ctx.dwait(t, "t", ns.wf_ld)
                        for k in range(8):
                            for mi in range(4):
                                bk = bank[4 + mi]
                                woff = (ib4 * 4 + mi) * 128
                                bk.mm(t, bk.h[:, 0:256],
                                      ns.wf1h[:, k * 2048 + woff:
                                              k * 2048 + woff + 128],
                                      ns.h2c[:, k * TO + tco:
                                             k * TO + tco + 256],
                                      start=(k == 0), stop=(k == 7))
                    t_up = ctx.cnt["t"]

                    @block.scalar
                    def _(sc, ib4=ib4):
                        sc = ctx.ch("s", sc)
                        ctx.wait(sc, "s", "t", t_up)
                        for mi in range(4):
                            bk = bank[4 + mi]
                            sc.activation(
                                out=ns.hallh[:, (ib4 * 4 + mi) * 256:
                                             (ib4 * 4 + mi + 1) * 256],
                                in_=bk.h[:, 0:256], func=AF.Gelu)
                            bk.rd("s")
                s_hall = ctx.cnt["s"]

                for half in range(2):
                    @block.tensor
                    def _(t, ck=ck, tc2=tc2, half=half, s_hall=s_hall):
                        ctx.wait(t, "t", "s", s_hall)
                        for kk in range(16):
                            for mi in range(4):
                                mt = half * 4 + mi
                                bk = bank[mi]
                                bk.mm(t, bk.h[:, 0:256],
                                      ns.wf2h[:, kk * D + mt * 128:
                                              kk * D + (mt + 1) * 128],
                                      ns.hallh[:, kk * 256:(kk + 1) * 256],
                                      start=(kk == 0), stop=(kk == 15))
                        ns.wf_free = ctx.cnt["t"]

                    @block.vector
                    def _(v, ck=ck, tc2=tc2, half=half):
                        v = ctx.ch("v", v)
                        ctx.wait(v, "v", "t", ctx.cnt["t"])
                        for mi in range(4):
                            mt = half * 4 + mi
                            bk = bank[mi]
                            co = mt * TO + ck * 512 + tc2 * 256
                            v.tensor_tensor(out=ns.xa[:, co:co + 256],
                                            in0=bk.h[:, 0:256],
                                            in1=ns.xa[:, co:co + 256],
                                            op=ALU.add)
                            bk.rd("v")
    emit_phase_join(ns)
    es4.close()
    if ns.stop_after == "ffn":
        ns.x2buf = ns.xa
        emit_phase_join(ns)
        ns.es2x.close()
        return

    es5a = ExitStack()
    ns.x0c = ns.x0w
    for ck in range(NCK):
        @block.sync
        def _(q, ck=ck):
            ctx.wait(q, "q", "v", getattr(ns, "x0c_free", 0))
            ns.x0c_ld = ctx.dgroup(q, "q", [
                (ns.x0c[:, p * 512:(p + 1) * 512],
                 ns.I_xo[128 * p:128 * (p + 1), ck * 512:ck * 512 + 512])
                for p in range(8)], "lq")

        @block.vector
        def _(v, ck=ck):
            v = ctx.ch("v", v)
            ctx.dwait(v, "v", ns.x0c_ld)
            for p in range(8):
                co = p * TO + ck * 512
                v.tensor_tensor(out=ns.xa[:, co:co + 512],
                                in0=ns.x0c[:, p * 512:(p + 1) * 512],
                                in1=ns.xa[:, co:co + 512], op=ALU.add)
            ns.x0c_free = ctx.cnt["v"]
        emit_colstats(ns, ns.xa, ck * 512, 512, "ln", EPS_LN, bufW=TO)
    emit_phase_join(ns)
    es5a.close()
    if ns.stop_after == "x1":
        ns.x2buf = ns.xa
        emit_phase_join(ns)
        ns.es2x.close()
        return

    es5 = ExitStack()
    sb("wqf", (128, 8 * D), F32, stack=es5)
    sb("wqb", (128, 8 * D), BF16, stack=es5)
    sb("qsc", (128, 1), F32, stack=es5)
    sb("x1b", (128, 8 * 512), BF16, stack=es5)

    @block.sync
    def _(q):
        ns.wq_ld = ctx.dgroup(q, "q", [
            (ns.wqf[:, p * D:(p + 1) * D], ns.I_wqt[128 * p:128 * (p + 1), :])
            for p in range(8)], "lq")

    @block.vector
    def _(v):
        v = ctx.ch("v", v)
        ctx.dwait(v, "v", ns.wq_ld)
        v.tensor_reduce(out=ns.psq[:, 0:1], in_=ns.wqf[:, :], axis=AX.X,
                        op=ALU.add, apply_absolute_value=True)
    v_abs = ctx.cnt["v"]

    @block.tensor
    def _(t):
        ctx.wait(t, "t", "v", v_abs)
        bank[6].mm(t, bank[6].h[0:1, 0:1], ns.onesr[:, :], ns.psq[:, 0:1],
                   start=True, stop=True)
    t_abs = ctx.cnt["t"]

    @block.vector
    def _(v):
        v = ctx.ch("v", v)
        ctx.wait(v, "v", "t", t_abs)
        v.tensor_scalar(out=ns.prow1[0:1, 0:1], in0=bank[6].h[0:1, 0:1],
                        scalar1=1.0 / (D * D), scalar2=None, op0=ALU.mult)
        bank[6].rd("v")
        v.tensor_scalar_max(out=ns.prow1[0:1, 0:1], in0=ns.prow1[0:1, 0:1],
                            scalar1=1e-5)
    v_sc = ctx.cnt["v"]

    @block.tensor
    def _(t):
        ctx.wait(t, "t", "v", v_sc)
        bank[6].mm(t, bank[6].h[:, 0:1], ns.onesc[:, :], ns.prow1[0:1, 0:1],
                   start=True, stop=True)
    t_scbc = ctx.cnt["t"]

    @block.vector
    def _(v):
        v = ctx.ch("v", v)
        ctx.wait(v, "v", "t", t_scbc)
        v.tensor_copy(out=ns.qsc[:, :], in_=bank[6].h[:, 0:1])
        bank[6].rd("v")
        v.reciprocal(out=ns.qsc[:, :], in_=ns.qsc[:, :])
        v.tensor_single_scalar(out=ns.wqf[:, :], in_=ns.wqf[:, :],
                               scalar=ns.qsc[:, :], op=ALU.mult)
        v.tensor_single_scalar(out=ns.wqb[:, :], in_=ns.wqf[:, :], scalar=0.5,
                               op=ALU.is_gt)
        v.tensor_single_scalar(out=ns.wqf[:, :], in_=ns.wqf[:, :], scalar=-0.5,
                               op=ALU.is_lt)
        v.tensor_tensor(out=ns.wqf[:, :], in0=ns.wqb[:, :], in1=ns.wqf[:, :],
                        op=ALU.subtract)
        v.reciprocal(out=ns.qsc[:, :], in_=ns.qsc[:, :])
        v.tensor_single_scalar(out=ns.wqb[:, :], in_=ns.wqf[:, :],
                               scalar=ns.qsc[:, :], op=ALU.mult)
    v_wq = ctx.cnt["v"]

    for ck in range(NCK):
        @block.vector
        def _(v, ck=ck):
            v = ctx.ch("v", v)
            ctx.wait(v, "v", "t", ctx.cnt["t"])
            for p in range(8):
                v.tensor_copy(out=ns.x1b[:, p * 512:(p + 1) * 512],
                              in_=ns.xa[:, p * TO + ck * 512:p * TO + ck * 512 + 512])
        v_x1b = ctx.cnt["v"]
        for half in range(2):
            @block.tensor
            def _(t, ck=ck, half=half, v_x1b=v_x1b):
                ctx.wait(t, "t", "v", max(v_wq, v_x1b))
                for k in range(8):
                    for mi in range(4):
                        mt = half * 4 + mi
                        bk = bank[mi]
                        bk.mm(t, bk.h[:, 0:512],
                              ns.wqb[:, k * D + mt * 128:k * D + (mt + 1) * 128],
                              ns.x1b[:, k * 512:(k + 1) * 512],
                              start=(k == 0), stop=(k == 7))

            @block.vector
            def _(v, ck=ck, half=half):
                v = ctx.ch("v", v)
                ctx.wait(v, "v", "t", ctx.cnt["t"])
                for mi in range(4):
                    mt = half * 4 + mi
                    bk = bank[mi]
                    co = mt * TO + ck * 512
                    v.tensor_tensor(out=ns.xa[:, co:co + 512],
                                    in0=bk.h[:, 0:512],
                                    in1=ns.xa[:, co:co + 512], op=ALU.add)
                    bk.rd("v")
    emit_phase_join(ns)
    es5.close()

    for ck in range(NCK):
        emit_colstats(ns, ns.xa, ck * 512, 512, "ln", EPS_LN, bufW=TO)
    ns.x2buf = ns.xa
    ns.x2_done = ctx.cnt["v"]
    emit_phase_join(ns)
    ns.es2x.close()


def emit_l1attn(ns):
    ctx, block, bank, nc, sb = ns.ctx, ns.block, ns.bank, ns.nc, ns.sb
    NCK = TO // 512

    esx = ExitStack()
    sb("x2c", (128, 8 * 512), BF16, stack=esx)
    for ck in range(NCK):
        @block.vector
        def _(v, ck=ck):
            v = ctx.ch("v", v)
            ctx.dwait(v, "v", getattr(ns, "x2c_st", None))
            for p in range(8):
                v.tensor_copy(out=ns.x2c[:, p * 512:(p + 1) * 512],
                              in_=ns.xa[:, p * TO + ck * 512:p * TO + ck * 512 + 512])
        vv = ctx.cnt["v"]

        @block.sync
        def _(q, ck=ck, vv=vv):
            ctx.wait(q, "q", "v", vv)
            ns.x2c_st = ctx.dgroup(q, "q", [
                (ns.X2B[128 * p:128 * (p + 1), ck * 512:ck * 512 + 512],
                 ns.x2c[:, p * 512:(p + 1) * 512]) for p in range(8)], "sq")
            ns.x2_sts = getattr(ns, "x2_sts", [])
            ns.x2_sts.append(ns.x2c_st)
    emit_phase_join(ns)
    esx.close()

    @block.gpsimd
    def _(g):
        for key in ns.x2_sts:
            ctx.dwait(g, "g", key)
        ns.cc_x2 = ctx.ccg(g, "AllGather", [ns.X2B], [ns.X2F], ns.groups)

    es6 = ExitStack()
    sb("qb2", (128, 4 * TF), BF16, stack=es6)
    sb("kb2", (128, 4 * TF), BF16, stack=es6)
    sb("xst", (128, 8 * 512), BF16, stack=es6)

    es6q = ExitStack()
    sb("sinc", (128, 512), F32, stack=es6q)
    sb("cosc", (128, 512), F32, stack=es6q)
    sb("wl1", (128, 8 * FH), BF16, stack=es6q)
    sb("wl2", (128, 8 * FH), BF16, stack=es6q)
    sb("qraw", (128, 4 * 512), F32, stack=es6q)
    sb("rope_perm", (128, 128), F32, stack=es6q)
    sb("v_l1bq", (128, 4), F32, stack=es6q)
    sb("v_l1bk", (128, 4), F32, stack=es6q)

    @block.gpsimd
    def _(g):
        g = ctx.ch("g", g)
        prs = []
        for nm, buf in (("l1wq", ns.wl1), ("l1wk", ns.wl2)):
            srcd = getattr(ns, "I_" + nm)
            for p in range(8):
                prs.append((buf[:, p * FH:(p + 1) * FH],
                            srcd[128 * p:128 * (p + 1), :]))
        prs.append((ns.rope_perm[:, :], ns.I_rope_perm))
        prs.append((ns.v_l1bq[:, :], ns.I_l1bq))
        prs.append((ns.v_l1bk[:, :], ns.I_l1bk))
        ns.l1w_ld = ctx.dgroup(g, "g", prs, "lg")

    for ck8 in range(TF // 512):
        blk = ck8 // 4
        co = (ck8 % 4) * 512

        @block.sync
        def _(q, ck8=ck8, blk=blk, co=co):
            ctx.dwait(q, "q", ns.cc_x2)
            ctx.wait(q, "q", "t", getattr(ns, "xst_free", 0))
            ns.xst_ld = ctx.dgroup(q, "q", [
                (ns.xst[:, p * 512:(p + 1) * 512],
                 ns.X2F[blk * D + 128 * p:blk * D + 128 * (p + 1),
                        co:co + 512]) for p in range(8)], "lq")

        @block.sync
        def _(q, ck8=ck8):
            ctx.wait(q, "q", "v", getattr(ns, "tab_free", 0))
            ns.tab_ld = ctx.dgroup(q, "q", [
                (ns.sinc[:, :], ns.I_ropesin[:, ck8 * 512:(ck8 + 1) * 512]),
                (ns.cosc[:, :], ns.I_ropecos[:, ck8 * 512:(ck8 + 1) * 512])],
                "lq")
        if ns.dbg_rope and ck8 == 7:
            @block.sync
            def _(q):
                ctx.dwait(q, "q", ns.tab_ld)
                ns.dbg1 = ctx.dgroup(q, "q", [
                    (ns.DBG_SC[:, 0:512], ns.sinc[:, :]),
                    (ns.DBG_SC[:, 512:1024], ns.cosc[:, :])], "sq")

        for which, wbuf, bvec in (("q", ns.wl1, ns.v_l1bq),
                                  ("k", ns.wl2, ns.v_l1bk)):
            @block.tensor
            def _(t, ck8=ck8, wbuf=wbuf):
                ctx.dwait(t, "t", ns.xst_ld)
                ctx.dwait(t, "t", ns.l1w_ld)
                ctx.wait(t, "t", "v", ctx.cnt["v"])
                for k in range(8):
                    for mt in range(4):
                        bk = bank[mt]
                        bk.mm(t, bk.h[:, 0:512],
                              wbuf[:, k * FH + mt * 128:k * FH + (mt + 1) * 128],
                              ns.xst[:, k * 512:(k + 1) * 512],
                              start=(k == 0), stop=(k == 7))
                ns.xst_free = ctx.cnt["t"]
            t_p = ctx.cnt["t"]

            @block.vector
            def _(v, ck8=ck8, which=which, bvec=bvec, t_p=t_p):
                v = ctx.ch("v", v)
                ctx.wait(v, "v", "t", t_p)
                for mt in range(4):
                    bk = bank[mt]
                    v.tensor_scalar(out=ns.qraw[:, mt * 512:(mt + 1) * 512],
                                    in0=bk.h[:, 0:512],
                                    scalar1=bvec[:, mt:mt + 1], scalar2=None,
                                    op0=ALU.add)
                    bk.rd("v")
            v_raw = ctx.cnt["v"]

            @block.tensor
            def _(t, ck8=ck8, which=which, v_raw=v_raw):
                ctx.wait(t, "t", "v", v_raw)
                for mt in range(4):
                    bk = bank[4 + mt % 2]
                    bk.mm(t, bk.h[:, 0:512], ns.rope_perm[:, :],
                          ns.qraw[:, mt * 512:(mt + 1) * 512],
                          start=True, stop=True)

                    @block.vector
                    def _(v, bk=bk, mt=mt, ck8=ck8, which=which):
                        v = ctx.ch("v", v)
                        ctx.wait(v, "v", "t", ctx.cnt["t"])
                        dst = (ns.qb2 if which == "q" else ns.kb2)[
                            :, mt * TF + ck8 * 512:mt * TF + ck8 * 512 + 512]
                        ctx.dwait(v, "v", ns.tab_ld)
                        v.tensor_tensor(out=ns.ptt[:, 0:512],
                                        in0=bk.h[:, 0:512], in1=ns.sinc[:, :],
                                        op=ALU.mult)
                        bk.rd("v")
                        v.tensor_tensor(out=ns.psq[:, 0:512],
                                        in0=ns.qraw[:, mt * 512:(mt + 1) * 512],
                                        in1=ns.cosc[:, :], op=ALU.mult)
                        v.tensor_tensor(out=dst, in0=ns.ptt[:, 0:512],
                                        in1=ns.psq[:, 0:512], op=ALU.add)
        ns.tab_free = ctx.cnt["v"]
    v_qkv = ctx.cnt["v"]
    if ns.dbg_rope:
        @block.sync
        def _(q):
            ctx.wait(q, "q", "v", ctx.cnt["v"])
            ns.dbg2 = ctx.dgroup(q, "q", [
                (ns.DBG_QK[:, 0:4 * TF], ns.qb2[:, :]),
                (ns.DBG_QK[:, 4 * TF:8 * TF], ns.kb2[:, :])], "sq")
    emit_phase_join(ns)
    es6q.close()


    es6z = ExitStack()
    sb("vtm2", (128, (TF // 128) * 520), BF16, stack=es6z)

    es6v = ExitStack()
    sb("wl3", (128, 8 * FH), BF16, stack=es6v)
    sb("onesbf", (1, 128), BF16, stack=es6v)
    sb("bvrow", (1, FH), BF16, stack=es6v)

    @block.gpsimd
    def _(g):
        prs = [(ns.wl3[:, p * FH:(p + 1) * FH],
                ns.I_l1wv[128 * p:128 * (p + 1), :]) for p in range(8)]
        prs.append((ns.bvrow[:, :], ns.I_l1bv_row))
        ns.l1v_ld = ctx.dgroup(g, "g", prs, "lg")

    @block.vector
    def _(v):
        v = ctx.ch("v", v)
        v.memset(ns.onesbf[:, :], 1.0)
        full = ns.vtm2[:, :]
        pitch = full.ap[0][0]
        for h in range(NHL):
            onescols = bass.AP(full.tensor, full.offset + 65 * h + 64,
                               [[pitch, 128], [520, TF // 128], [1, 1]])
            v.memset(onescols, 1.0)
    v_ones = ctx.cnt["v"]

    ns.xst_free = 0
    for ck8 in range(TF // 512):
        blk = ck8 // 4
        co = (ck8 % 4) * 512

        @block.sync
        def _(q, ck8=ck8, blk=blk, co=co):
            ctx.dwait(q, "q", ns.cc_x2)
            ctx.wait(q, "q", "t", ns.xst_free)
            ns.xst_ld = ctx.dgroup(q, "q", [
                (ns.xst[:, p * 512:(p + 1) * 512],
                 ns.X2F[blk * D + 128 * p:blk * D + 128 * (p + 1),
                        co:co + 512]) for p in range(8)], "lq")

        for sub in range(4):
            @block.tensor
            def _(t, ck8=ck8, sub=sub):
                ctx.dwait(t, "t", ns.xst_ld)
                ctx.dwait(t, "t", ns.l1v_ld)
                ctx.wait(t, "t", "v", max(v_ones, ctx.cnt["v"]))
                bk = bank[6 + sub % 2]
                for k in range(8):
                    bk.mm(t, bk.h[:, 0:512],
                          ns.xst[:, k * 512 + sub * 128:k * 512 + sub * 128 + 128],
                          ns.wl3[:, k * FH:(k + 1) * FH],
                          start=(k == 0), stop=False)
                bk.mm(t, bk.h[:, 0:512], ns.onesbf[:, 0:128],
                      ns.bvrow[:, :], start=False, stop=True)
                if sub == 3:
                    ns.xst_free = ctx.cnt["t"]

                @block.vector
                def _(v, bk=bk, ck8=ck8, sub=sub):
                    v = ctx.ch("v", v)
                    ctx.wait(v, "v", "t", ctx.cnt["t"])
                    tokb = ck8 * 4 + sub
                    full = ns.vtm2[:, :]
                    pitch = full.ap[0][0]
                    dst = bass.AP(full.tensor, full.offset + tokb * 520,
                                  [[pitch, 128], [65, 8], [1, 64]])
                    v.tensor_copy(out=dst,
                                  in_=bk.h[:, 0:512].rearrange(
                                      "p (h c) -> p h c", c=64))
                    bk.rd("v")
    emit_phase_join(ns)
    es6v.close()

    es6d = ExitStack()
    sb("pt", (128, 512), BF16, stack=es6d)
    sb("pt2", (128, 512), BF16, stack=es6d)
    sb("pt3", (128, 512), BF16, stack=es6d)
    sb("pt4", (128, 512), BF16, stack=es6d)
    sb("orow", (1, 512), F32, stack=es6d)
    sb("ost", (128, 512), BF16, stack=es6d)
    sb("expb", (128, 1), F32, stack=es6d)
    sb("cmask", (128, 4 * 512), BF16, stack=es6d)

    @block.vector
    def _(v):
        v = ctx.ch("v", v)
        v.memset(ns.expb[:, :], -30.0)
    v_expb = ctx.cnt["v"]

    @block.sync
    def _(q):
        ns.cm_ld = ctx.dgroup(q, "q", [(ns.cmask[:, :], ns.I_cmask)], "lq")
    ns.ptfree = {0: 0, 1: 0, 2: 0, 3: 0}

    PTS = (None,)
    for h in range(NHL):
        mt, hr = h // 2, 64 * (h % 2)
        for qt in range(TF // 512):
            nkc = 4 * (qt + 1)
            ob = bank[0] if (h * 8 + qt) % 2 == 0 else bank[4]
            gate = {}
            for kc in range(nkc + 3):
                if kc < nkc:
                    sbnk = bank[1 + kc % 3]
                    ptb = (ns.pt, ns.pt2, ns.pt3, ns.pt4)[kc % 4]

                    @block.tensor
                    def _(t, h=h, qt=qt, kc=kc, sbnk=sbnk, mt=mt, hr=hr):
                        ctx.wait(t, "t", "v", v_qkv)
                        sbnk.mm(t, sbnk.h[:, 0:512],
                                ns.kb2[hr:hr + 64,
                                       mt * TF + kc * 128:
                                       mt * TF + (kc + 1) * 128],
                                ns.qb2[hr:hr + 64,
                                       mt * TF + qt * 512:
                                       mt * TF + (qt + 1) * 512],
                                start=True, stop=True)
                    t_sc = ctx.cnt["t"]

                    @block.scalar
                    def _(sc, kc=kc, sbnk=sbnk, t_sc=t_sc, ptb=ptb):
                        sc = ctx.ch("s", sc)
                        ctx.wait(sc, "s", "t", t_sc)
                        ctx.wait(sc, "s", "t", ns.ptfree[kc % 4])
                        ctx.wait(sc, "s", "v", v_expb)
                        sc.activation(out=ptb[:, :], in_=sbnk.h[:, 0:512],
                                      func=AF.Exp, scale=0.125,
                                      bias=ns.expb[:, 0:1])
                        sbnk.rd("s")
                    s_exp = ctx.cnt["s"]

                    if kc >= 4 * qt:
                        @block.vector
                        def _(v, qt=qt, kc=kc, s_exp=s_exp, ptb=ptb):
                            v = ctx.ch("v", v)
                            ctx.wait(v, "v", "s", s_exp)
                            ctx.dwait(v, "v", ns.cm_ld)
                            j = kc - 4 * qt
                            v.tensor_tensor(
                                out=ptb[:, :], in0=ptb[:, :],
                                in1=ns.cmask[:, j * 512:(j + 1) * 512],
                                op=ALU.mult)
                        gate[kc] = ("v", ctx.cnt["v"])
                    else:
                        gate[kc] = ("s", s_exp)

                pv = kc - 3
                if pv >= 0:
                    ptb_pv = (ns.pt, ns.pt2, ns.pt3, ns.pt4)[pv % 4]

                    @block.tensor
                    def _(t, h=h, pv=pv, nkc=nkc, ob=ob, ptb_pv=ptb_pv,
                          g8=gate[pv]):
                        ctx.wait(t, "t", g8[0], g8[1])
                        ob.mm(t, ob.h[0:65, 0:512],
                              ns.vtm2[:, pv * 520 + 65 * h:
                                      pv * 520 + 65 * h + 65],
                              ptb_pv[:, :], start=(pv == 0),
                              stop=(pv == nkc - 1))
                        ns.ptfree[pv % 4] = ctx.cnt["t"]

            @block.vector
            def _(v, h=h, qt=qt, ob=ob):
                v = ctx.ch("v", v)
                ctx.wait(v, "v", "t", ctx.cnt["t"])
                v.reciprocal(out=ns.orow[:, :], in_=ob.h[64:65, 0:512])
            v_r = ctx.cnt["v"]

            @block.tensor
            def _(t, h=h, qt=qt, v_r=v_r):
                ctx.wait(t, "t", "v", v_r)
                bank[5].mm(t, bank[5].h[0:64, 0:512], ns.onesc[:, 0:64],
                           ns.orow[:, :], start=True, stop=True)
            t_bc = ctx.cnt["t"]

            @block.scalar
            def _(sc, h=h, qt=qt, t_bc=t_bc):
                sc = ctx.ch("s", sc)
                ctx.wait(sc, "s", "t", t_bc)
                sc.copy(out=ns.ptt[0:64, 0:512], in_=bank[5].h[0:64, 0:512])
                bank[5].rd("s")
            s_bc = ctx.cnt["s"]

            @block.vector
            def _(v, h=h, qt=qt, ob=ob, s_bc=s_bc):
                v = ctx.ch("v", v)
                ctx.wait(v, "v", "s", s_bc)
                ctx.dwait(v, "v", getattr(ns, "oat_st", None))
                v.tensor_tensor(out=ns.ost[0:64, :], in0=ob.h[0:64, 0:512],
                                in1=ns.ptt[0:64, 0:512], op=ALU.mult)
                ob.rd("v")
            v_o = ctx.cnt["v"]

            @block.sync
            def _(q, h=h, qt=qt, v_o=v_o):
                ctx.wait(q, "q", "v", v_o)
                ns.oat_st = ctx.dgroup(q, "q", [
                    (ns.OAT_B[64 * h:64 * h + 64, qt * 512:(qt + 1) * 512],
                     ns.ost[0:64, :])], "sq")

    emit_phase_join(ns)
    es6d.close()
    es6z.close()
    es6.close()

    @block.gpsimd
    def _(g):
        for nm, val in ctx.dcnt.items():
            ctx.wait(g, "g", nm, val)
        ns.cc_oat = ctx.ccg(g, "AllGather", [ns.OAT_B], [ns.OAT_F], ns.groups)

    es7 = ExitStack()
    sb("wob2", (128, 8 * D), BF16, stack=es7)
    sb("v_l1bo", (128, 8), F32, stack=es7)
    sb("ogh0b", (128, 8 * 512), BF16, stack=es7)
    sb("ogh1b", (128, 8 * 512), BF16, stack=es7)
    sb("oggb", (128, 8 * 512), BF16, stack=es7)

    @block.gpsimd
    def _(g):
        ns.wo2_ld = ctx.dgroup(g, "g", [
            (ns.wob2[:, p * D:(p + 1) * D],
             ns.I_l1wo[128 * p:128 * (p + 1), :]) for p in range(8)] +
            [(ns.v_l1bo[:, :], ns.I_l1bo)], "lg")

    for ck in range(NCK):
        @block.sync
        def _(q, ck=ck):
            ctx.dwait(q, "q", ns.cc_oat)
            ctx.wait(q, "q", "v", getattr(ns, "og2_free_v", 0))
            prs = []
            for p in range(8):
                prs.append((ns.ogh0b[:, p * 512:(p + 1) * 512],
                            ns.OAT_F[128 * p:128 * (p + 1),
                                     ck * 512:ck * 512 + 512]))
                prs.append((ns.ogh1b[:, p * 512:(p + 1) * 512],
                            ns.OAT_F[128 * p:128 * (p + 1),
                                     TO + ck * 512:TO + ck * 512 + 512]))
            ns.og2_ld = ctx.dgroup(q, "q", prs, "lq")

        @block.vector
        def _(v, ck=ck):
            v = ctx.ch("v", v)
            ctx.dwait(v, "v", ns.og2_ld)
            ctx.wait(v, "v", "t", getattr(ns, "og2_free", 0))
            for p in range(8):
                v.tensor_scalar(out=ns.oggb[:, p * 512:(p + 1) * 512],
                                in0=ns.ogh0b[:, p * 512:(p + 1) * 512],
                                scalar1=ns.mab[:, 0:1], scalar2=None,
                                op0=ALU.mult)
                v.scalar_tensor_tensor(out=ns.oggb[:, p * 512:(p + 1) * 512],
                                       in0=ns.ogh1b[:, p * 512:(p + 1) * 512],
                                       scalar=ns.mab[:, 1:2],
                                       in1=ns.oggb[:, p * 512:(p + 1) * 512],
                                       op0=ALU.mult, op1=ALU.add)
            ns.og2_free_v = ctx.cnt["v"]
        v_og2 = ctx.cnt["v"]

        for half in range(2):
            @block.tensor
            def _(t, ck=ck, half=half, v_og2=v_og2):
                ctx.wait(t, "t", "v", v_og2)
                ctx.dwait(t, "t", ns.wo2_ld)
                for k in range(8):
                    for mi in range(4):
                        mt2 = half * 4 + mi
                        bk = bank[mi]
                        bk.mm(t, bk.h[:, 0:512],
                              ns.wob2[:, k * D + mt2 * 128:k * D + (mt2 + 1) * 128],
                              ns.oggb[:, k * 512:(k + 1) * 512],
                              start=(k == 0), stop=(k == 7))
                ns.og2_free = ctx.cnt["t"]

            @block.vector
            def _(v, ck=ck, half=half):
                v = ctx.ch("v", v)
                ctx.wait(v, "v", "t", ctx.cnt["t"])
                for mi in range(4):
                    mt2 = half * 4 + mi
                    bk = bank[mi]
                    co = mt2 * TO + ck * 512
                    v.tensor_scalar(out=ns.ptt[:, 0:512], in0=bk.h[:, 0:512],
                                    scalar1=ns.v_l1bo[:, mt2:mt2 + 1],
                                    scalar2=None, op0=ALU.add)
                    bk.rd("v")
                    v.tensor_tensor(out=ns.xa[:, co:co + 512],
                                    in0=ns.ptt[:, 0:512],
                                    in1=ns.xa[:, co:co + 512], op=ALU.add)
        emit_colstats(ns, ns.xa, ck * 512, 512, "ln", EPS_LN, bufW=TO)
    emit_phase_join(ns)
    es7.close()
    ns.x2buf = ns.xa



def emit_moe(ns):
    ctx, block, bank, nc, sb = ns.ctx, ns.block, ns.bank, ns.nc, ns.sb
    NCK = TO // 512
    es8 = ns.es.enter_context(ExitStack())
    sb("acc", (128, 8 * TO), F32, stack=es8)
    sb("x3b", (128, 8 * TO), BF16, stack=es8)
    sb("cfs", (16, TO), BF16, stack=es8)
    sb("sqeps", (16, 1), F32, stack=es8)
    sb("inde16", (16, 16 * 128), BF16, stack=es8)
    es8r = ExitStack()
    sb("gwt", (128, 8 * 16), F32, stack=es8r)
    sb("cmb", (128, 16 * 16), F32, stack=es8r)
    sb("bebc", (128, 16), F32, stack=es8r)
    sb("mrow", (128, 2), F32, stack=es8r)
    sb("rsc1", (128, 16), F32, stack=es8r)
    sb("rsc2", (128, 16), F32, stack=es8r)
    sb("identf", (128, 128), F32, stack=es8r)
    sb("berow", (1, 16), F32, stack=es8r)

    @block.vector
    def _(v):
        v = ctx.ch("v", v)
        v.memset(ns.sqeps[:, :], 1e-30)
        for p in range(8):
            v.tensor_copy(out=ns.x3b[:, p * TO:(p + 1) * TO],
                          in_=ns.xa[:, p * TO:(p + 1) * TO])
    v_x3b = ctx.cnt["v"]

    @block.sync
    def _(q):
        ns.gw_ld = ctx.dgroup(q, "q", [
            (ns.gwt[:, p * 16:(p + 1) * 16],
             ns.I_gate_wt[128 * p:128 * (p + 1), :]) for p in range(8)] +
            [(ns.identf[:, :], ns.I_identf), (ns.inde16[:, :], ns.I_inde16),
             (ns.berow[:, :], ns.I_bias_e)], "lq")

    @block.tensor
    def _(t):
        ctx.dwait(t, "t", ns.gw_ld)
        bank[6].mm(t, bank[6].h[:, 0:16], ns.onesc[:, :], ns.berow[:, :],
                   start=True, stop=True)
    t_be = ctx.cnt["t"]

    @block.vector
    def _(v):
        v = ctx.ch("v", v)
        ctx.wait(v, "v", "t", t_be)
        v.tensor_copy(out=ns.bebc[:, :], in_=bank[6].h[:, 0:16])
        bank[6].rd("v")
    v_be = ctx.cnt["v"]

    for ck in range(NCK):
        @block.tensor
        def _(t, ck=ck):
            ctx.dwait(t, "t", ns.gw_ld)
            ctx.wait(t, "t", "v", ctx.cnt["v"])
            for k in range(8):
                bank[7].mm(t, bank[7].h[0:16, 0:512],
                           ns.gwt[:, k * 16:(k + 1) * 16],
                           ns.xa[:, k * TO + ck * 512:k * TO + ck * 512 + 512],
                           start=(k == 0), stop=(k == 7))
        t_lg = ctx.cnt["t"]

        @block.vector
        def _(v, ck=ck):
            v = ctx.ch("v", v)
            ctx.wait(v, "v", "t", t_lg)
            v.tensor_copy(out=ns.psq[0:16, 0:512], in_=bank[7].h[0:16, 0:512])
            bank[7].rd("v")
        v_lg = ctx.cnt["v"]

        for sub in range(4):
            ti = ck * 4 + sub

            @block.tensor
            def _(t, ck=ck, sub=sub):
                ctx.wait(t, "t", "v", v_lg)
                ctx.dwait(t, "t", ns.gw_ld)
                bk = bank[4 + sub % 2]
                bk.tr(t, bk.h[:, 0:16], ns.psq[0:16, sub * 128:(sub + 1) * 128],
                      ns.identf[0:16, 0:16])
            t_tr = ctx.cnt["t"]

            @block.scalar
            def _(sc, ti=ti):
                sc = ctx.ch("s", sc)
                ctx.wait(sc, "s", "t", t_tr)
                bk = bank[4 + ti % 2]
                sc.activation(out=ns.cmb[:, ti * 16:(ti + 1) * 16],
                              in_=bk.h[:, 0:16], func=AF.Sigmoid)
                bk.rd("s")
            s_sig = ctx.cnt["s"]

            @block.vector
            def _(v, ti=ti):
                v = ctx.ch("v", v)
                ctx.wait(v, "v", "s", s_sig)
                ctx.wait(v, "v", "v", v_be)
                sco = ns.cmb[:, ti * 16:(ti + 1) * 16]
                sc_ = ns.rsc1[:, :]
                v.tensor_tensor(out=sc_, in0=sco, in1=ns.bebc[:, :], op=ALU.add)
                g4 = lambda a, i: bass.AP(a[:, :].tensor, a[:, :].offset + i,
                                          [[a[:, :].ap[0][0], 128], [4, 4]])
                pairs = [(0, 1), (0, 2), (0, 3), (1, 2), (1, 3), (2, 3)]
                first = True
                for (i, j) in pairs:
                    v.tensor_tensor(out=ns.rsc2[:, 0:4], in0=g4(ns.rsc1, i),
                                    in1=g4(ns.rsc1, j), op=ALU.min)
                    v.tensor_tensor(out=ns.rsc2[:, 4:8], in0=g4(ns.rsc1, i),
                                    in1=g4(ns.rsc1, j), op=ALU.add)
                    if first:
                        v.tensor_copy(out=ns.rsc2[:, 8:12], in_=ns.rsc2[:, 0:4])
                        v.tensor_copy(out=ns.rsc2[:, 12:16], in_=ns.rsc2[:, 4:8])
                        first = False
                    else:
                        v.tensor_tensor(out=ns.rsc2[:, 8:12],
                                        in0=ns.rsc2[:, 8:12],
                                        in1=ns.rsc2[:, 0:4], op=ALU.max)
                        v.tensor_tensor(out=ns.rsc2[:, 12:16],
                                        in0=ns.rsc2[:, 12:16],
                                        in1=ns.rsc2[:, 4:8], op=ALU.max)
                v.tensor_reduce(out=ns.rsc2[:, 0:4],
                                in_=ns.rsc1[:, :].rearrange("p (g i) -> p g i", i=4),
                                axis=AX.X, op=ALU.max)
                gs = ns.rsc2[:, 12:16]
                first = True
                for (i, j) in pairs:
                    v.tensor_tensor(out=ns.rsc2[:, 0:1], in0=gs[:, i:i + 1],
                                    in1=gs[:, j:j + 1], op=ALU.min)
                    if first:
                        v.tensor_copy(out=ns.rsc2[:, 4:5], in_=ns.rsc2[:, 0:1])
                        first = False
                    else:
                        v.tensor_tensor(out=ns.rsc2[:, 4:5],
                                        in0=ns.rsc2[:, 4:5],
                                        in1=ns.rsc2[:, 0:1], op=ALU.max)
                v.tensor_scalar(out=ns.rsc2[:, 6:7], in0=ns.rsc2[:, 4:5],
                                scalar1=-1.0, scalar2=None, op0=ALU.mult)
                v.tensor_single_scalar(out=ns.rsc2[:, 8:12], in_=gs,
                                       scalar=ns.rsc2[:, 6:7], op=ALU.add)
                v.tensor_single_scalar(out=ns.rsc2[:, 8:12],
                                       in_=ns.rsc2[:, 8:12], scalar=0.0,
                                       op=ALU.is_ge)
                gselb = bass.AP(ns.rsc2[:, :].tensor,
                                ns.rsc2[:, :].offset + 8,
                                [[ns.rsc2[:, :].ap[0][0], 128], [1, 4], [0, 4]])
                v.tensor_tensor(out=ns.rsc1[:, :].rearrange("p (g i) -> p g i", i=4),
                                in0=ns.rsc1[:, :].rearrange("p (g i) -> p g i", i=4),
                                in1=gselb, op=ALU.mult)
                for _r in range(4):
                    v.tensor_reduce(out=ns.mrow[:, 0:1], in_=ns.rsc1[:, :],
                                    axis=AX.X, op=ALU.max)
                    v.tensor_scalar(out=ns.mrow[:, 1:2], in0=ns.mrow[:, 0:1],
                                    scalar1=-1.0, scalar2=None, op0=ALU.mult)
                    v.tensor_single_scalar(out=ns.rsc2[:, 0:16],
                                           in_=ns.rsc1[:, :],
                                           scalar=ns.mrow[:, 1:2],
                                           op=ALU.add)
                    v.tensor_single_scalar(out=ns.rsc2[:, 0:16],
                                           in_=ns.rsc2[:, 0:16], scalar=0.0,
                                           op=ALU.is_equal)
                    v.scalar_tensor_tensor(out=ns.rsc1[:, :],
                                           in0=ns.rsc2[:, 0:16],
                                           scalar=-1e30, in1=ns.rsc1[:, :],
                                           op0=ALU.mult, op1=ALU.add)
                v.tensor_single_scalar(out=ns.rsc2[:, 0:16], in_=ns.rsc1[:, :],
                                       scalar=-1e29, op=ALU.is_lt)
                v.tensor_tensor(out=sco, in0=sco, in1=ns.rsc2[:, 0:16],
                                op=ALU.mult)
                v.tensor_reduce(out=ns.mrow[:, 0:1], in_=sco, axis=AX.X,
                                op=ALU.add)
                v.tensor_scalar(out=ns.mrow[:, 0:1], in0=ns.mrow[:, 0:1],
                                scalar1=1e-20, scalar2=None, op0=ALU.add)
                v.reciprocal(out=ns.mrow[:, 0:1], in_=ns.mrow[:, 0:1])
                v.tensor_scalar(out=ns.mrow[:, 0:1], in0=ns.mrow[:, 0:1],
                                scalar1=2.5, scalar2=None, op0=ALU.mult)
                v.tensor_single_scalar(out=sco, in_=sco,
                                       scalar=ns.mrow[:, 0:1], op=ALU.mult)
    v_cmb = ctx.cnt["v"]

    for ti in range(16):
        @block.tensor
        def _(t, ti=ti):
            ctx.wait(t, "t", "v", v_cmb)
            ctx.dwait(t, "t", ns.gw_ld)
            bk = bank[4 + ti % 2]
            bk.tr(t, bk.h[0:16, 0:128], ns.cmb[:, ti * 16:(ti + 1) * 16],
                  ns.identf[:, :])

            @block.scalar
            def _(sc, bk=bk, ti=ti):
                sc = ctx.ch("s", sc)
                ctx.wait(sc, "s", "t", ctx.cnt["t"])
                sc.activation(out=ns.cfs[:, ti * 128:(ti + 1) * 128],
                              in_=bk.h[0:16, 0:128], func=AF.Sqrt,
                              bias=ns.sqeps[:, 0:1])
                bk.rd("s")
    s_cfs = ctx.cnt["s"]

    emit_phase_join(ns)
    es8r.close()

    es9 = ExitStack()
    sb("wsh1", (128, 8 * 512), BF16, stack=es9)
    sb("wsh2", (128, 4 * 1024), BF16, stack=es9)
    sb("hsh", (128, 4 * 512), BF16, stack=es9)
    ns.wsh_free = 0
    for qsh in range(4):
        @block.gpsimd
        def _(g, qsh=qsh):
            g = ctx.ch("g", g)
            ctx.wait(g, "g", "t", ns.wsh_free)
            prs = []
            for p in range(8):
                prs.append((ns.wsh1[:, p * 512:(p + 1) * 512],
                            ns.I_sh_upt[128 * p:128 * (p + 1),
                                        qsh * 512:(qsh + 1) * 512]))
            for kk in range(4):
                prs.append((ns.wsh2[:, kk * 1024:(kk + 1) * 1024],
                            ns.I_sh_dnt[qsh * 512 + 128 * kk:
                                        qsh * 512 + 128 * (kk + 1), :]))
            ns.wsh_ld = ctx.dgroup(g, "g", prs, "lg")

        for ck in range(NCK):
            tco = ck * 512

            @block.tensor
            def _(t, qsh=qsh, tco=tco):
                ctx.wait(t, "t", "v", v_x3b)
                ctx.dwait(t, "t", ns.wsh_ld)
                for k in range(8):
                    for mi in range(4):
                        bk = bank[4 + mi]
                        bk.mm(t, bk.h[:, 0:512],
                              ns.wsh1[:, k * 512 + mi * 128:
                                      k * 512 + (mi + 1) * 128],
                              ns.x3b[:, k * TO + tco:k * TO + tco + 512],
                              start=(k == 0), stop=(k == 7))
            t_up = ctx.cnt["t"]

            v_hsh_mi = []

            @block.vector
            def _(v, t_up=t_up):
                v = ctx.ch("v", v)
                ctx.wait(v, "v", "t", t_up)
                for mi in range(4):
                    bk = bank[4 + mi]
                    v.tensor_scalar_max(out=ns.psq[:, 0:512],
                                        in0=bk.h[:, 0:512], scalar1=0.0)
                    bk.rd("v")
                    v.tensor_tensor(out=ns.hsh[:, mi * 512:(mi + 1) * 512],
                                    in0=ns.psq[:, 0:512], in1=ns.psq[:, 0:512],
                                    op=ALU.mult)
                    v_hsh_mi.append(ctx.cnt["v"])

            for half in range(2):
                @block.tensor
                def _(t, qsh=qsh, tco=tco, half=half, v_hsh_mi=v_hsh_mi):
                    for kk in range(4):
                        ctx.wait(t, "t", "v", v_hsh_mi[kk])
                        for mi in range(4):
                            mt = half * 4 + mi
                            bk = bank[mi]
                            bk.mm(t, bk.h[:, 0:512],
                                  ns.wsh2[:, kk * 1024 + mt * 128:
                                          kk * 1024 + (mt + 1) * 128],
                                  ns.hsh[:, kk * 512:(kk + 1) * 512],
                                  start=(kk == 0), stop=(kk == 3))
                    ns.wsh_free = ctx.cnt["t"]

                @block.vector
                def _(v, qsh=qsh, tco=tco, half=half):
                    v = ctx.ch("v", v)
                    ctx.wait(v, "v", "t", ctx.cnt["t"])
                    for mi in range(4):
                        mt = half * 4 + mi
                        bk = bank[mi]
                        co = mt * TO + tco
                        if qsh == 0:
                            v.tensor_copy(out=ns.acc[:, co:co + 512],
                                          in_=bk.h[:, 0:512])
                        else:
                            v.tensor_tensor(out=ns.acc[:, co:co + 512],
                                            in0=bk.h[:, 0:512],
                                            in1=ns.acc[:, co:co + 512],
                                            op=ALU.add)
                        bk.rd("v")
    emit_phase_join(ns)
    es9.close()

    es10 = ExitStack()
    sb("wme1", (128, 8 * 512), BF16, stack=es10)
    sb("wme2", (128, 4 * 1024), BF16, stack=es10)
    sb("hme", (128, 4 * 512), BF16, stack=es10)
    ns.wme1_free = 0
    ns.wme2_free = 0
    for e in range(MOE_NE):
        @block.gpsimd
        def _(g, e=e):
            g = ctx.ch("g", g)
            ctx.wait(g, "g", "t", ns.wme1_free)
            ns.wme1_ld = ctx.dgroup(g, "g", [
                (ns.wme1[:, p * 512:(p + 1) * 512],
                 ns.I_moe_up[e * D + 128 * p:e * D + 128 * (p + 1), :])
                for p in range(8)], "lg")
            ctx.wait(g, "g", "t", ns.wme2_free)
            ns.wme2_ld = ctx.dgroup(g, "g", [
                (ns.wme2[:, kk * D:(kk + 1) * D],
                 ns.I_moe_dn[e * 512 + 128 * kk:
                             e * 512 + 128 * (kk + 1), :])
                for kk in range(4)], "lg")

        for ck in range(NCK):
            tco = ck * 512

            @block.tensor
            def _(t, e=e, tco=tco):
                ctx.wait(t, "t", "s", s_cfs)
                ctx.wait(t, "t", "v", ctx.cnt["v"])
                ctx.wait(t, "t", "s", ctx.cnt["s"])
                bank[0].mm(t, bank[0].h[:, 0:512],
                           ns.inde16[:, e * 128:(e + 1) * 128],
                           ns.cfs[0:16, tco:tco + 512],
                           start=True, stop=True)

            @block.scalar
            def _(sc, e=e):
                sc = ctx.ch("s", sc)
                ctx.wait(sc, "s", "t", ctx.cnt["t"])
                sc.copy(out=ns.ptt[:, 0:512], in_=bank[0].h[:, 0:512])
                bank[0].rd("s")
            s_cbc = ctx.cnt["s"]

            @block.tensor
            def _(t, tco=tco, ck=ck):
                ctx.wait(t, "t", "v", v_x3b)
                ctx.dwait(t, "t", ns.wme1_ld)
                for k in range(8):
                    for mi in range(4):
                        bk = bank[4 + mi]
                        bk.mm(t, bk.h[:, 0:512],
                              ns.wme1[:, k * 512 + mi * 128:
                                      k * 512 + (mi + 1) * 128],
                              ns.x3b[:, k * TO + tco:k * TO + tco + 512],
                              start=(k == 0), stop=(k == 7))
                if ck == NCK - 1:
                    ns.wme1_free = ctx.cnt["t"]
            t_up = ctx.cnt["t"]

            v_hme_mi = []

            @block.vector
            def _(v, s_cbc=s_cbc, t_up=t_up):
                v = ctx.ch("v", v)
                ctx.wait(v, "v", "t", t_up)
                ctx.wait(v, "v", "s", s_cbc)
                for mi in range(4):
                    bk = bank[4 + mi]
                    v.scalar_tensor_tensor(out=ns.psq[:, 0:512],
                                           in0=bk.h[:, 0:512],
                                           scalar=0.0,
                                           in1=ns.ptt[:, 0:512],
                                           op0=ALU.max, op1=ALU.mult)
                    bk.rd("v")
                    v.tensor_tensor(out=ns.hme[:, mi * 512:(mi + 1) * 512],
                                    in0=ns.psq[:, 0:512],
                                    in1=ns.psq[:, 0:512], op=ALU.mult)
                    v_hme_mi.append(ctx.cnt["v"])

            for half in range(2):
                @block.tensor
                def _(t, half=half, v_hme_mi=v_hme_mi):
                    ctx.dwait(t, "t", ns.wme2_ld)
                    for kk in range(4):
                        ctx.wait(t, "t", "v", v_hme_mi[kk])
                        for mi in range(4):
                            mt = half * 4 + mi
                            bk = bank[mi]
                            bk.mm(t, bk.h[:, 0:512],
                                  ns.wme2[:, kk * D + mt * 128:
                                          kk * D + (mt + 1) * 128],
                                  ns.hme[:, kk * 512:(kk + 1) * 512],
                                  start=(kk == 0), stop=(kk == 3))
                    ns.wme2_free = ctx.cnt["t"]

                @block.vector
                def _(v, half=half, tco=tco):
                    v = ctx.ch("v", v)
                    ctx.wait(v, "v", "t", ctx.cnt["t"])
                    for mi in range(4):
                        mt = half * 4 + mi
                        bk = bank[mi]
                        co = mt * TO + tco
                        v.tensor_tensor(out=ns.acc[:, co:co + 512],
                                        in0=bk.h[:, 0:512],
                                        in1=ns.acc[:, co:co + 512],
                                        op=ALU.add)
                        bk.rd("v")
    emit_phase_join(ns)
    es10.close()

    for ck in range(NCK):
        @block.vector
        def _(v, ck=ck):
            v = ctx.ch("v", v)
            for mt in range(8):
                co = mt * TO + ck * 512
                v.tensor_tensor(out=ns.acc[:, co:co + 512],
                                in0=ns.xa[:, co:co + 512],
                                in1=ns.acc[:, co:co + 512], op=ALU.add)
        emit_colstats(ns, ns.acc, ck * 512, 512, "ln", EPS_LN, bufW=TO)
    ns.x2buf = ns.acc


# version 08
# speedup vs baseline: 1.0198x; 1.0038x over previous
import numpy as np
from contextlib import ExitStack
from types import SimpleNamespace

import concourse.bass as bass
import concourse.mybir as mybir
from concourse.bass_utils import run_bass_kernel_spmd

F32 = mybir.dt.float32
BF16 = mybir.dt.bfloat16
AF = mybir.ActivationFunctionType
ALU = mybir.AluOpType
AX = mybir.AxisListType

B = 4
TF = 4096
TO = 2048
D = 1024
NHL = 8
FH = NHL * 64
DK = 64
SEC = 512
CHUNK = 128
NCH = SEC // CHUNK
NB32 = SEC // 32
NB8 = SEC // 8
EPS_RMS = 1e-6
EPS_GN = 1e-5
EPS_LN = 1e-5
DCLAMP = 40.0
MOE_NE = 16


class Ctx:
    def __init__(self, nc, es):
        self.nc = nc
        self.sem, self.cnt = {}, {}
        for e in ("t", "v", "s", "g"):
            self.sem[e] = es.enter_context(nc.semaphore("tick_" + e))
            self.cnt[e] = 0
        self.dsem, self.dcnt = {}, {}
        self.pool, self.rr = {}, {}
        for ch, n in (("lq", 4), ("lg", 4), ("sq", 2), ("cc", 1)):
            names = []
            for i in range(n):
                nm = f"dma_{ch}{i}"
                self.dsem[nm] = es.enter_context(nc.semaphore(nm))
                self.dcnt[nm] = 0
                names.append(nm)
            self.pool[ch] = names
            self.rr[ch] = 0
        self.waited = {}

    def inc(self, e, inst):
        self.cnt[e] += 1
        inst.then_inc(self.sem[e], 1)
        return self.cnt[e]

    def dgroup(self, eng, consumer, pairs, ch):
        nm = self.pool[ch][self.rr[ch] % len(self.pool[ch])]
        self.rr[ch] += 1
        self.wait(eng, consumer, nm, self.dcnt[nm])
        for out, in_ in pairs:
            self.dcnt[nm] += 16
            eng.dma_start(out=out, in_=in_).then_inc(self.dsem[nm], 16)
        return (nm, self.dcnt[nm])

    def ccg(self, g, kind, ins, outs, groups):
        nm = self.pool["cc"][0]
        self.dcnt[nm] += 1
        g.collective_compute(kind, ALU.bypass, replica_groups=groups,
                             ins=ins, outs=outs).then_inc(self.dsem[nm], 1)
        return (nm, self.dcnt[nm])

    def dwait(self, eng, consumer, key):
        if key is None:
            return
        self.wait(eng, consumer, key[0], key[1])

    def ch(self, ename, eng):
        return ChainEng(self, ename, eng)

    def wait(self, eng, consumer, sname, val):
        if val is None or val <= 0:
            return
        key = (consumer, sname)
        if self.waited.get(key, 0) >= val:
            return
        sem = self.sem[sname] if sname in self.sem else self.dsem[sname]
        eng.wait_ge(sem, val)
        self.waited[key] = val


_CHAIN_OPS = {
    "tensor_copy", "tensor_tensor", "tensor_add", "tensor_sub", "tensor_mul",
    "tensor_max", "tensor_scalar", "tensor_single_scalar", "tensor_scalar_add",
    "tensor_scalar_sub", "tensor_scalar_mul", "tensor_scalar_max",
    "tensor_scalar_min", "tensor_relu", "tensor_reduce", "reduce_max",
    "reduce_sum", "scalar_tensor_tensor", "tensor_tensor_scan", "engine_nop",
    "memset", "memzero", "reciprocal", "select", "copy_predicated",
    "transpose", "bn_stats", "bn_aggr", "max", "max_index", "stream_shuffle",
    "activation", "sign", "square", "sqrt", "mul", "copy", "add", "iota",
    "affine_select", "partition_broadcast", "partition_all_reduce",
}


class ChainEng:

    def __init__(self, ctx, ename, eng):
        self._ctx, self._e, self._raw = ctx, ename, eng

    def __getattr__(self, name):
        f = getattr(self._raw, name)
        if name not in _CHAIN_OPS:
            return f
        ctx, e, raw = self._ctx, self._e, self._raw

        def g(*a, **k):
            ctx.wait(raw, e, e, ctx.cnt[e])
            inst = f(*a, **k)
            ctx.inc(e, inst)
            return inst
        return g


class PBank:

    def __init__(self, ctx, es, name):
        self.ctx = ctx
        self.h = es.enter_context(ctx.nc.psum_tensor(name, [128, 512], F32))
        self.last_w = 0
        self.reads = {}

    def pre(self, t):
        self.ctx.wait(t, "t", "t", self.last_w)
        for e, v in self.reads.items():
            self.ctx.wait(t, "t", e, v)
        self.reads = {}

    def mm(self, t, out_ap, lhsT, rhs, start, stop, tile_position=None):
        self.pre(t)
        kw = {}
        if tile_position is not None:
            kw["tile_position"] = tile_position
        inst = t.matmul(out_ap, lhsT, rhs, start=start, stop=stop, **kw)
        self.last_w = self.ctx.inc("t", inst)
        return inst

    def tr(self, t, out_ap, in_ap, ident):
        self.pre(t)
        inst = t.transpose(out_ap, in_ap, ident)
        self.last_w = self.ctx.inc("t", inst)
        return inst

    def rd(self, e):
        self.reads[e] = self.ctx.cnt[e]


def bc_blocks(ap2d, base, bs, nblk, rep=None):
    rep = bs if rep is None else rep
    full = ap2d[:, :]
    pitch = full.ap[0][0]
    return bass.AP(full.tensor, full.offset + base,
                   [[pitch, 128], [bs, nblk], [0, rep]])


def cols(ap2d, base, stride, n):
    full = ap2d[:, :]
    pitch = full.ap[0][0]
    return bass.AP(full.tensor, full.offset + base, [[pitch, 128], [stride, n]])


def grp(ap2d, off, bs, nblk):
    return ap2d[:, off:off + bs * nblk].rearrange("p (b i) -> p b i", i=bs)


def build_program(taps=(), n_sec=TF // SEC, full=True, stop_after=None,
                  dbg_rope=False):
    nc = bass.Bass("TRN2", target_bir_lowering=False, debug=False)
    es = ExitStack()
    ctx = Ctx(nc, es)
    ns = SimpleNamespace(nc=nc, es=es, ctx=ctx, n_sec=n_sec, full=full,
                         stop_after=stop_after)
    ns.dbg_rope = dbg_rope

    def din(name, shape, dtype=F32):
        ap = nc.dram_tensor(name, list(shape), dtype, kind="ExternalInput").ap()
        setattr(ns, "I_" + name, ap)

    din("xf", (D, TF))
    din("xo", (D, TO))
    for nm in ("wr", "wk", "wv", "wg"):
        din(nm, (D, FH))
    din("wo", (D, D))
    din("w1", (D, 4096))
    din("w2", (4096, D))
    din("wqt", (D, D))
    din("l1wq", (D, FH))
    din("l1wk", (D, FH))
    din("l1wv", (D, FH))
    din("l1wo", (D, D))
    din("l1bq", (128, 4))
    din("l1bk", (128, 4))
    din("l1bv_row", (1, FH))
    din("l1bo", (128, 8))
    din("rope_perm", (128, 128))
    din("ropesin", (128, TF))
    din("ropecos", (128, TF))
    din("cmask", (128, 4 * 512), BF16)
    din("gate_wt", (D, 16))
    din("bias_e", (1, 16))
    din("moe_up", (MOE_NE * D, 512))
    din("moe_dn", (MOE_NE * 512, D))
    din("sh_upt", (D, 2048))
    din("sh_dnt", (2048, D))
    din("wla", (D, 64))
    din("wlb", (64, FH))
    for nm in ("mu_r", "mu_k", "mu_v", "mu_w", "mu_g", "ln1w", "ln2w"):
        din(nm, (128, 8))
    for nm in ("w0", "uu", "gnw", "gnb"):
        din(nm, (128, 4))
    din("ident", (128, 128), BF16)
    din("identf", (128, 128))
    din("inde16", (16, 16 * 128), BF16)
    din("indh", (128, 2))
    din("indhb", (128, 2), BF16)
    din("indb", (8, 512), BF16)
    din("indbf", (8, 512))
    din("indbt", (128, 32), BF16)
    din("indbtf", (128, 32))
    din("md8", (128, 128), BF16)
    din("m8", (128, 128), BF16)
    din("m16", (128, 128), BF16)
    din("mstr", (128, 128), BF16)
    din("bmask", (8, 3 * SEC), BF16)
    din("onesc", (1, 128))
    din("onesr", (128, 1))
    din("mab", (128, 2))

    ns.OUT = nc.dram_tensor("out", [D, TO], F32, kind="ExternalOutput").ap()
    ns.taps = {}
    for name, shape, dtype in taps:
        ns.taps[name] = nc.dram_tensor(name, list(shape), dtype,
                                       kind="ExternalOutput").ap()

    ns.ORW_B = nc.dram_tensor("orw_b", [FH, TF], BF16).ap()
    ns.ORW_F = nc.dram_tensor("orw_f", [2 * FH, TF], BF16).ap()
    ns.X2B = nc.dram_tensor("x2b", [D, TO], BF16).ap()
    ns.X2F = nc.dram_tensor("x2f", [2 * D, TO], BF16).ap()
    ns.OAT_B = nc.dram_tensor("oat_b", [FH, TF], BF16).ap()
    ns.OAT_F = nc.dram_tensor("oat_f", [2 * FH, TF], BF16).ap()
    if dbg_rope:
        ns.DBG_SC = nc.dram_tensor("dbg_sc", [128, 2 * 512], F32,
                                   kind="ExternalOutput").ap()
        ns.DBG_QK = nc.dram_tensor("dbg_qk", [128, 8 * TF], BF16,
                                   kind="ExternalOutput").ap()

    def sb(name, shape, dtype=F32, stack=None):
        setattr(ns, name, (stack or es).enter_context(
            nc.sbuf_tensor("sb_" + name, shape, dtype)))
    ns.sb = sb

    sb("ident", (128, 128), BF16)
    sb("indh", (128, 2))
    sb("indhb", (128, 2), BF16)
    sb("indb", (8, 512), BF16)
    sb("indbf", (8, 512))
    sb("indbt", (128, 32), BF16)
    sb("indbtf", (128, 32))
    sb("md8", (128, 128), BF16)
    sb("m8", (128, 128), BF16)
    sb("m16", (128, 128), BF16)
    sb("mstr", (128, 128), BF16)
    sb("bmask", (8, 3 * SEC), BF16)
    sb("onesc", (1, 128))
    sb("onesr", (128, 1))
    sb("mab", (128, 2))
    sb("zeros64", (64, 128), BF16)
    for nm in ("mu_r", "mu_k", "mu_v", "mu_w", "mu_g", "ln1w", "ln2w"):
        sb("v_" + nm, (128, 8))
    for nm in ("w0", "uu", "gnw", "gnb"):
        sb("v_" + nm, (128, 4))
    for m in ("r", "k", "v", "w", "g"):
        sb("omu_" + m, (128, 8))

    mixer_es = ExitStack()
    ns.mixer_es = mixer_es
    def sbm(name, shape, dtype=F32):
        ns.sb(name, shape, dtype, stack=mixer_es)
    sb = sbm
    sb("xs", (128, 8 * SEC))
    sb("hb", (128, 8 * (SEC + 1)), BF16)
    sb("mixb", (128, 8 * SEC), BF16)
    sb("sqb", (128, SEC))
    sb("wbA", (128, 8 * FH), BF16)
    sb("wbB", (128, 8 * FH), BF16)
    sb("wlab", (128, 8 * 64), BF16)
    sb("wlbb", (64, FH), BF16)
    sb("rb", (128, 4 * SEC), BF16)
    sb("kb", (128, 4 * (SEC + 3)), BF16)
    sb("vb", (128, 4 * (SEC + 3)), BF16)
    sb("gb", (128, 4 * SEC), BF16)
    sb("wfp", (128, 4 * SEC))
    sb("ewb", (128, 4 * SEC))
    sb("csb", (128, 4 * (SEC + 3)))
    sb("cscar", (128, 4))
    sb("vtm", (128, NCH * FH), BF16)
    sb("khtm", (128, NCH * FH), BF16)
    FAMS = ("r32", "kr32_1", "kr32_2", "kr32_3", "r16", "kr16_1", "r8",
            "kr8_1", "kd8", "rd8", "khat", "rchunk")
    ns.FAMS = FAMS
    for f in FAMS:
        sb("fam_" + f, (128, 4 * SEC), BF16)
    sb("b2f", (128, 4 * SEC), BF16)
    sb("b3f", (128, 4 * SEC), BF16)
    sb("arg1", (128, SEC))
    sb("arg2", (128, SEC))
    sb("ef32", (128, 4 * 17), BF16)
    sb("efh8", (128, 4 * NB8), BF16)
    sb("efh8b", (128, 4 * NB8), BF16)
    sb("f16k", (128, 4 * NB8), BF16)
    sb("f16r", (128, 4 * NB8), BF16)
    sb("fchk", (128, 4 * NB32), BF16)
    sb("fchkr", (128, 4 * NB32), BF16)
    sb("etot", (128, 4 * NCH))
    sb("prodb", (128, 4 * SEC), BF16)
    for d_ in range(4):
        sb("browm%d" % d_, (8, SEC), BF16)
    sb("mean8", (8, SEC))
    sb("rstd8", (8, SEC))
    sb("psb", (128, 4 * 128), BF16)
    sb("osec", (128, 4 * SEC))
    sb("omix", (128, 4 * SEC), BF16)
    sb("Ssb", (128, 4 * 64))
    sb("Sbf", (128, 4 * 64), BF16)
    sb("srow", (8, SEC))
    sb("svar", (8, SEC))

    sb = ns.sb
    ns.bank = [PBank(ctx, es, f"bank{i}") for i in range(8)]
    ns.groups = [[0, 1], [2, 3], [4, 5], [6, 7]]
    ns.mix_free = 0
    ns.w_free = {"wr": 0, "wk": 0, "wv": 0, "wg": 0}
    ns.bfree = {}
    ns.omix_keys = []
    ns.w_load = {}
    ns.fam_free = 0
    ns.sec_free_v = 0
    ns.carry_v = 0

    with nc.Block() as block:
        ns.block = block
        emit_preamble(ns)
        for s in range(n_sec):
            emit_mixer_section(ns, s)
        emit_mixer_taps(ns)
        if getattr(ns, "full", True):
            emit_phase_join(ns)
            mixer_es.close()
            emit_l0post(ns)
            if ns.stop_after not in ("wo", "ffn", "x1", "x2"):
                emit_l1attn(ns)
            if ns.stop_after not in ("wo", "ffn", "x1", "x2", "attn"):
                emit_moe(ns)
        emit_finish(ns)

    es.close()
    return nc, ns


def emit_preamble(ns):
    ctx, block = ns.ctx, ns.block

    @block.sync
    def _(q):
        pairs = [(ns.ident, ns.I_ident), (ns.indh, ns.I_indh),
                 (ns.indhb, ns.I_indhb), (ns.indb, ns.I_indb),
                 (ns.md8, ns.I_md8), (ns.m8, ns.I_m8), (ns.m16, ns.I_m16),
                 (ns.mstr, ns.I_mstr),
                 (ns.bmask, ns.I_bmask), (ns.onesc, ns.I_onesc),
                 (ns.indbf, ns.I_indbf),
                 (ns.indbt, ns.I_indbt), (ns.indbtf, ns.I_indbtf),
                 (ns.onesr, ns.I_onesr), (ns.mab, ns.I_mab)]
        for nm in ("mu_r", "mu_k", "mu_v", "mu_w", "mu_g", "ln1w", "ln2w",
                   "w0", "uu", "gnw", "gnb"):
            pairs.append((getattr(ns, "v_" + nm), getattr(ns, "I_" + nm)))
        ns.c_pre = ctx.dgroup(q, "q", [(d[:, :], s_) for d, s_ in pairs], "lq")

    @block.gpsimd
    def _(g):
        g = ctx.ch("g", g)
        ns.c_lora = ctx.dgroup(g, "g", [
            (ns.wlab[:, :].rearrange("p (a c) -> p a c", c=64),
             ns.I_wla.rearrange("(a p) c -> p a c", p=128)),
            (ns.wlbb[:, :], ns.I_wlb)], "lg")

    @block.vector
    def _(v):
        v = ctx.ch("v", v)
        ctx.dwait(v, "v", ns.c_pre)
        for m, nm in (("r", "mu_r"), ("k", "mu_k"), ("v", "mu_v"),
                      ("w", "mu_w"), ("g", "mu_g")):
            v.tensor_scalar(out=getattr(ns, "omu_" + m)[:, :],
                            in0=getattr(ns, "v_" + nm)[:, :],
                            scalar1=-1.0, scalar2=1.0,
                            op0=ALU.mult, op1=ALU.add)
        v.memset(ns.csb[:, :], 0.0)
        v.memset(ns.cscar[:, :], 0.0)
        v.memset(ns.kb[:, :], 0.0)
        v.memset(ns.vb[:, :], 0.0)
        v.memset(ns.hb[:, :], 0.0)
        v.memset(ns.Ssb[:, :], 0.0)
        v.memset(ns.zeros64[:, :], 0.0)
        v.memset(ns.bank[0].h[:, :], 0.0)
        v.memset(ns.bank[1].h[:, :], 0.0)
        ns.bank[0].rd("v")
        ns.bank[1].rd("v")
        v.engine_nop()
    ns.v_pre = ctx.cnt["v"]


def emit_mixer_section(ns, s):
    ctx, block, bank = ns.ctx, ns.block, ns.bank
    xoff = 0
    P3 = SEC + 3
    c_xs = [None]

    def csp(mt):
        return ns.csb[:, mt * P3:(mt + 1) * P3]

    @block.sync
    def _(q):
        ctx.wait(q, "q", "v", getattr(ns, "xs_free", 0))
        c_xs[0] = ctx.dgroup(q, "q", [
            (ns.xs[:, xoff + p * SEC:xoff + (p + 1) * SEC],
             ns.I_xf[128 * p:128 * (p + 1), s * SEC:(s + 1) * SEC])
            for p in range(8)], "lq")

    WMAP = [("wr", ns.I_wr, ns.wbA), ("wk", ns.I_wk, ns.wbB),
            ("wv", ns.I_wv, ns.wbA), ("wg", ns.I_wg, ns.wbB)]

    @block.gpsimd
    def _(g):
        g = ctx.ch("g", g)
        if s == 0:
            ctx.dwait(g, "g", ns.c_lora)
        for nm, src, buf in WMAP[:2]:
            ctx.wait(g, "g", "t", ns.bfree.get(id(buf), 0))
            ns.w_load[nm] = ctx.dgroup(g, "g", [
                (buf[:, p * FH:(p + 1) * FH], src[128 * p:128 * (p + 1), :])
                for p in range(8)], "lg")

    for p in range(8):
        @block.scalar
        def _(sc, p=p):
            sc = ctx.ch("s", sc)
            ctx.dwait(sc, "s", ns.c_pre)
            ctx.dwait(sc, "s", c_xs[0])
            ctx.wait(sc, "s", "t", getattr(ns, "sq_guard", 0))
            if p > 0:
                ctx.wait(sc, "s", "t", ns.sq_mm)
            inst = sc.activation(out=ns.sqb[:, :],
                                 in_=ns.xs[:, xoff + p * SEC:xoff + (p + 1) * SEC],
                                 func=AF.Square)

        @block.tensor
        def _(t, p=p):
            ctx.dwait(t, "t", ns.c_pre)
            ctx.wait(t, "t", "s", ctx.cnt["s"])
            bank[4].mm(t, bank[4].h[0:1, 0:SEC], ns.onesr[:, :], ns.sqb[:, :],
                       start=(p == 0), stop=(p == 7))
            ns.sq_mm = ctx.cnt["t"]

    @block.vector
    def _(v):
        v = ctx.ch("v", v)
        ctx.wait(v, "v", "t", ns.sq_mm)
        v.tensor_scalar(out=ns.arg1[0:1, 0:SEC], in0=bank[4].h[0:1, 0:SEC],
                        scalar1=1.0 / D, scalar2=EPS_RMS,
                        op0=ALU.mult, op1=ALU.add)
        bank[4].rd("v")
        inst = v.reciprocal(out=ns.arg2[0:1, 0:SEC], in_=ns.arg1[0:1, 0:SEC])

    @block.scalar
    def _(sc):
        sc = ctx.ch("s", sc)
        ctx.wait(sc, "s", "v", ctx.cnt["v"])
        inst = sc.activation(out=ns.sqb[0:1, 0:SEC], in_=ns.arg2[0:1, 0:SEC], func=AF.Sqrt)
    s_rstd = ctx.cnt["s"]

    @block.tensor
    def _(t):
        ctx.wait(t, "t", "s", s_rstd)
        bank[5].mm(t, bank[5].h[:, 0:SEC], ns.onesc[:, :], ns.sqb[0:1, 0:SEC],
                   start=True, stop=True)
    t_rstdbc = ctx.cnt["t"]
    ns.sq_guard = t_rstdbc

    @block.vector
    def _(v):
        v = ctx.ch("v", v)
        ctx.wait(v, "v", "t", t_rstdbc)
        ctx.wait(v, "v", "v", ns.carry_v)
        for p in range(8):
            hp = ns.hb[:, p * (SEC + 1):(p + 1) * (SEC + 1)]
            inst = v.scalar_tensor_tensor(
                out=hp[:, 1:SEC + 1],
                in0=ns.xs[:, xoff + p * SEC:xoff + (p + 1) * SEC],
                scalar=ns.v_ln1w[:, p:p + 1], in1=bank[5].h[:, 0:SEC],
                op0=ALU.mult, op1=ALU.mult)
        bank[5].rd("v")
    v_h = ctx.cnt["v"]
    ns.xs_free = v_h

    def emit_mix(m):
        @block.vector
        def _(v):
            v = ctx.ch("v", v)
            ctx.wait(v, "v", "v", v_h)
            ctx.wait(v, "v", "t", ns.mix_free)
            for p in range(8):
                hp = ns.hb[:, p * (SEC + 1):(p + 1) * (SEC + 1)]
                mp = ns.mixb[:, p * SEC:(p + 1) * SEC]
                v.tensor_scalar(out=mp, in0=hp[:, 1:SEC + 1],
                                scalar1=getattr(ns, "omu_" + m)[:, p:p + 1],
                                scalar2=None, op0=ALU.mult)
            for p in range(8):
                hp = ns.hb[:, p * (SEC + 1):(p + 1) * (SEC + 1)]
                mp = ns.mixb[:, p * SEC:(p + 1) * SEC]
                inst = v.scalar_tensor_tensor(
                    out=mp, in0=hp[:, 0:SEC],
                    scalar=getattr(ns, "v_mu_" + m)[:, p:p + 1],
                    in1=mp, op0=ALU.mult, op1=ALU.add)
        return ctx.cnt["v"]

    def emit_proj(wsb, nm, vwait):
        @block.tensor
        def _(t):
            ctx.wait(t, "t", "v", vwait)
            ctx.dwait(t, "t", ns.w_load[nm])
            for k in range(8):
                for mt in range(4):
                    bank[mt].mm(
                        t, bank[mt].h[:, 0:SEC],
                        wsb[:, k * FH + mt * 128:k * FH + (mt + 1) * 128],
                        ns.mixb[:, k * SEC:(k + 1) * SEC],
                        start=(k == 0), stop=(k == 7))
            ns.mix_free = ctx.cnt["t"]
            ns.w_free[nm] = ctx.cnt["t"]
            ns.bfree[id(wsb)] = ctx.cnt["t"]

    vw = emit_mix("w")

    @block.tensor
    def _(t):
        ctx.wait(t, "t", "v", vw)
        ctx.dwait(t, "t", ns.c_lora)
        for k in range(8):
            bank[4].mm(t, bank[4].h[0:64, 0:SEC],
                       ns.wlab[:, k * 64:(k + 1) * 64],
                       ns.mixb[:, k * SEC:(k + 1) * SEC],
                       start=(k == 0), stop=(k == 7))

    @block.scalar
    def _(sc):
        sc = ctx.ch("s", sc)
        ctx.wait(sc, "s", "t", ctx.cnt["t"])
        inst = sc.activation(out=ns.prodb[0:64, 0:SEC],
                             in_=ns.bank[4].h[0:64, 0:SEC], func=AF.Tanh)
        bank[4].rd("s")
    s_tanh = ctx.cnt["s"]

    @block.tensor
    def _(t):
        ctx.wait(t, "t", "s", s_tanh)
        for mt in range(4):
            bank[4 + mt].mm(t, bank[4 + mt].h[:, 0:SEC],
                            ns.wlbb[:, mt * 128:(mt + 1) * 128],
                            ns.prodb[0:64, 0:SEC], start=True, stop=True)
        ns.mix_free = max(ns.mix_free, ctx.cnt["t"])
    @block.vector
    def _(v):
        v = ctx.ch("v", v)
        ctx.wait(v, "v", "t", ctx.cnt["t"])
        for mt in range(4):
            bk = bank[4 + mt]
            inst = v.tensor_scalar(out=ns.wfp[:, mt * SEC:(mt + 1) * SEC],
                                   in0=bk.h[:, 0:SEC],
                                   scalar1=ns.v_w0[:, mt:mt + 1], scalar2=None,
                                   op0=ALU.add)
            bk.rd("v")
    v_w = ctx.cnt["v"]

    @block.scalar
    def _(sc):
        sc = ctx.ch("s", sc)
        ctx.wait(sc, "s", "v", v_w)
        for mt in range(4):
            inst = sc.activation(out=ns.ewb[:, mt * SEC:(mt + 1) * SEC],
                                 in_=ns.wfp[:, mt * SEC:(mt + 1) * SEC],
                                 func=AF.Exp)
    s_ew = ctx.cnt["s"]

    @block.vector
    def _(v):
        v = ctx.ch("v", v)
        ctx.wait(v, "v", "s", s_ew)
        ctx.wait(v, "v", "v", ns.carry_v)
        for mt in range(4):
            cp = csp(mt)
            inst = v.tensor_tensor_scan(
                out=cp[:, 3:SEC + 3],
                data0=ns.ewb[:, mt * SEC:(mt + 1) * SEC],
                data1=ns.ewb[:, mt * SEC:(mt + 1) * SEC],
                initial=ns.cscar[:, mt:mt + 1], op0=ALU.add, op1=ALU.bypass)

    @block.vector
    def _(v):
        v = ctx.ch("v", v)
        ctx.wait(v, "v", "v", ctx.cnt["v"])
        for mt in range(4):
            inst = v.tensor_copy(out=ns.cscar[:, mt:mt + 1],
                                 in_=csp(mt)[:, SEC + 2:SEC + 3])
    v_cs = ctx.cnt["v"]

    def evac_plain(dst):
        @block.vector
        def _(v):
            v = ctx.ch("v", v)
            ctx.wait(v, "v", "t", ctx.cnt["t"])
            for mt in range(4):
                inst = v.tensor_copy(out=dst[:, mt * SEC:(mt + 1) * SEC],
                                     in_=bank[mt].h[:, 0:SEC])
                bank[mt].rd("v")

    def evac_shift(dst):
        @block.vector
        def _(v):
            v = ctx.ch("v", v)
            ctx.wait(v, "v", "t", ctx.cnt["t"])
            ctx.wait(v, "v", "v", ns.carry_v)
            for mt in range(4):
                dp = dst[:, mt * P3:(mt + 1) * P3]
                inst = v.tensor_copy(out=dp[:, 3:SEC + 3],
                                     in_=bank[mt].h[:, 0:SEC])
                bank[mt].rd("v")

    def evac_g():
        @block.scalar
        def _(sc):
            sc = ctx.ch("s", sc)
            ctx.wait(sc, "s", "t", ctx.cnt["t"])
            for mt in range(4):
                inst = sc.activation(out=ns.gb[:, mt * SEC:(mt + 1) * SEC],
                                     in_=bank[mt].h[:, 0:SEC], func=AF.Silu)
                bank[mt].rd("s")

    vr = emit_mix("r")
    emit_proj(ns.wbA, "wr", vr)
    evac_plain(ns.rb)

    @block.gpsimd
    def _(g):
        g = ctx.ch("g", g)
        nm, src, buf = WMAP[2]
        ctx.wait(g, "g", "t", ns.bfree.get(id(buf), 0))
        ns.w_load[nm] = ctx.dgroup(g, "g", [
            (buf[:, p * FH:(p + 1) * FH], src[128 * p:128 * (p + 1), :])
            for p in range(8)], "lg")

    vk = emit_mix("k")
    emit_proj(ns.wbB, "wk", vk)
    evac_shift(ns.kb)

    @block.gpsimd
    def _(g):
        g = ctx.ch("g", g)
        nm, src, buf = WMAP[3]
        ctx.wait(g, "g", "t", ns.bfree.get(id(buf), 0))
        ns.w_load[nm] = ctx.dgroup(g, "g", [
            (buf[:, p * FH:(p + 1) * FH], src[128 * p:128 * (p + 1), :])
            for p in range(8)], "lg")
    vv = emit_mix("v")
    emit_proj(ns.wbA, "wv", vv)
    evac_shift(ns.vb)
    vg = emit_mix("g")
    emit_proj(ns.wbB, "wg", vg)
    evac_g()
    v_rkvg = ctx.cnt["v"]

    @block.tensor
    def _(t):
        ctx.wait(t, "t", "v", v_rkvg)
        for c in range(NCH):
            for mt in range(4):
                bk = bank[6 + (c * 4 + mt) % 2]
                src = ns.vb[:, mt * P3 + 3 + c * 128:mt * P3 + 3 + (c + 1) * 128]
                bk.tr(t, bk.h[:, 0:64].bitcast(BF16)[:, 0:128], src,
                      ns.ident[:, :])
                @block.scalar
                def _(sc, bk=bk, c=c, mt=mt):
                    sc = ctx.ch("s", sc)
                    ctx.wait(sc, "s", "t", ctx.cnt["t"])
                    inst = sc.copy(
                        out=ns.vtm[:, c * FH + mt * 128:c * FH + (mt + 1) * 128],
                        in_=bk.h[:, 0:64].bitcast(BF16)[:, 0:128])
                    bk.rd("s")
    s_vtm = ctx.cnt["s"]

    emit_families(ns, s, csp)

    emit_chunks(ns, s, csp)

    emit_bands_gn(ns, s, csp)

    @block.vector
    def _(v):
        v = ctx.ch("v", v)
        ctx.wait(v, "v", "t", ctx.cnt["t"])
        ctx.wait(v, "v", "v", ctx.cnt["v"])
        for mt in range(4):
            v.tensor_copy(out=ns.kb[:, mt * P3:mt * P3 + 3],
                          in_=ns.kb[:, mt * P3 + SEC:mt * P3 + SEC + 3])
            v.tensor_copy(out=ns.vb[:, mt * P3:mt * P3 + 3],
                          in_=ns.vb[:, mt * P3 + SEC:mt * P3 + SEC + 3])
            inst = v.tensor_copy(out=ns.csb[:, mt * P3:mt * P3 + 3],
                                 in_=ns.csb[:, mt * P3 + SEC:mt * P3 + SEC + 3])
        for p in range(8):
            hp = ns.hb[:, p * (SEC + 1):(p + 1) * (SEC + 1)]
            inst = v.tensor_copy(out=hp[:, 0:1], in_=hp[:, SEC:SEC + 1])
    ns.carry_v = ctx.cnt["v"]


def emit_families(ns, s, csp):
    ctx, block, bank = ns.ctx, ns.block, ns.bank
    P3 = SEC + 3

    def fam(f):
        return getattr(ns, "fam_" + f)

    for mt in range(4):
        cp = csp(mt)
        co = mt * SEC
        c3 = mt * P3

        def g32(base):
            return bc_blocks(cp, base, 32, 16)

        def g8(base):
            return bc_blocks(cp, base, 8, 64)

        def cs_in(off):
            return grp(cp, off, 32, 16)

        def cs_in8(off):
            return grp(cp, off, 8, 64)

        @block.vector
        def _(v):
            v = ctx.ch("v", v)
            ctx.wait(v, "v", "s", ctx.cnt["s"])
            ctx.wait(v, "v", "t", ns.fam_free)
            v.tensor_tensor(out=grp(ns.arg1, 0, 32, 16), in0=cs_in(3),
                            in1=g32(34), op=ALU.subtract)
            inst = v.tensor_tensor(out=grp(ns.arg2, 0, 32, 16), in0=g32(2),
                                   in1=cs_in(2), op=ALU.subtract)

        @block.scalar
        def _(sc):
            sc = ctx.ch("s", sc)
            ctx.wait(sc, "s", "v", ctx.cnt["v"])
            sc.activation(out=fam("kr32_1")[:, co:co + SEC], in_=ns.arg1[:, :],
                          func=AF.Exp)
            inst = sc.activation(out=fam("r32")[:, co:co + SEC],
                                 in_=ns.arg2[:, :], func=AF.Exp)

        @block.vector
        def _(v):
            v = ctx.ch("v", v)
            ctx.wait(v, "v", "s", ctx.cnt["s"])
            v.tensor_tensor(out=grp(ns.arg1, 0, 8, 64), in0=cs_in8(3),
                            in1=g8(10), op=ALU.subtract)
            inst = v.tensor_tensor(out=grp(ns.arg2, 0, 8, 64), in0=g8(2),
                                   in1=cs_in8(2), op=ALU.subtract)

        @block.scalar
        def _(sc):
            sc = ctx.ch("s", sc)
            ctx.wait(sc, "s", "v", ctx.cnt["v"])
            sc.activation(out=fam("kr8_1")[:, co:co + SEC], in_=ns.arg1[:, :],
                          func=AF.Exp)
            inst = sc.activation(out=fam("r8")[:, co:co + SEC],
                                 in_=ns.arg2[:, :], func=AF.Exp)

        @block.vector
        def _(v):
            v = ctx.ch("v", v)
            ctx.wait(v, "v", "s", ctx.cnt["s"])
            v.tensor_tensor(out=ns.arg1[:, :], in0=cp[:, 1:SEC + 1],
                            in1=cp[:, 2:SEC + 2], op=ALU.subtract)
            inst = v.tensor_tensor(out=ns.arg2[:, :], in0=cp[:, 0:SEC],
                                   in1=cp[:, 2:SEC + 2], op=ALU.subtract)

        @block.scalar
        def _(sc):
            sc = ctx.ch("s", sc)
            ctx.wait(sc, "s", "v", ctx.cnt["v"])
            sc.activation(out=ns.b2f[:, co:co + SEC], in_=ns.arg1[:, :],
                          func=AF.Exp)
            inst = sc.activation(out=ns.b3f[:, co:co + SEC], in_=ns.arg2[:, :],
                                 func=AF.Exp)
        s_bases = ctx.cnt["s"]

        @block.vector
        def _(v):
            v = ctx.ch("v", v)
            ctx.wait(v, "v", "s", ctx.cnt["s"])
            v.memset(ns.arg1[:, 0:17], 0.0)
            v.tensor_tensor(out=ns.arg1[:, 0:15],
                            in0=cols(cp, 34, 32, 15),
                            in1=cols(cp, 66, 32, 15),
                            op=ALU.subtract)
            v.tensor_tensor(out=ns.arg1[:, 32:32 + NB8],
                            in0=cols(cp, 10, 8, NB8),
                            in1=cols(cp, 6, 8, NB8),
                            op=ALU.subtract)
            v.tensor_tensor(out=ns.arg1[:, 96:96 + NB8],
                            in0=cols(cp, 6, 8, NB8),
                            in1=cols(cp, 2, 8, NB8),
                            op=ALU.subtract)
            v.tensor_scalar_min(out=ns.arg1[:, 32:32 + 2 * NB8],
                                in0=ns.arg1[:, 32:32 + 2 * NB8], scalar1=DCLAMP)
            v.tensor_tensor(out=grp(ns.arg1, 160, 2, 32),
                            in0=cols(cp, 10, 8, NB8).rearrange("p (b i) -> p b i", i=2),
                            in1=bc_blocks(cp, 18, 16, 32, rep=2),
                            op=ALU.subtract)
            inst = v.tensor_tensor(out=grp(ns.arg1, 224, 2, 32),
                                   in0=bc_blocks(cp, 2, 16, 32, rep=2),
                                   in1=cols(cp, 2, 8, NB8).rearrange("p (b i) -> p b i", i=2),
                                   op=ALU.subtract)
            v.tensor_tensor(out=grp(ns.arg1, 288, 4, 4),
                            in0=cols(cp, 34, 32, 16).rearrange("p (b i) -> p b i", i=4),
                            in1=bc_blocks(cp, 130, 128, 4, rep=4),
                            op=ALU.subtract)
            inst2 = v.tensor_tensor(out=grp(ns.arg1, 304, 4, 4),
                                    in0=bc_blocks(cp, 2, 128, 4, rep=4),
                                    in1=cols(cp, 2, 32, 16).rearrange("p (b i) -> p b i", i=4),
                                    op=ALU.subtract)
            inst3 = v.tensor_tensor(out=ns.arg1[:, 320:320 + NCH],
                                    in0=cols(cp, 2, 128, NCH),
                                    in1=cols(cp, 130, 128, NCH),
                                    op=ALU.subtract)

        @block.scalar
        def _(sc):
            sc = ctx.ch("s", sc)
            ctx.wait(sc, "s", "v", ctx.cnt["v"])
            sc.activation(out=ns.ef32[:, mt * 17:mt * 17 + 17],
                          in_=ns.arg1[:, 0:17], func=AF.Exp)
            sc.activation(out=ns.efh8[:, mt * NB8:(mt + 1) * NB8],
                          in_=ns.arg1[:, 32:32 + NB8], func=AF.Exp)
            sc.activation(out=ns.efh8b[:, mt * NB8:(mt + 1) * NB8],
                          in_=ns.arg1[:, 96:96 + NB8], func=AF.Exp)
            sc.activation(out=ns.f16k[:, mt * NB8:(mt + 1) * NB8],
                          in_=ns.arg1[:, 160:160 + NB8], func=AF.Exp)
            sc.activation(out=ns.f16r[:, mt * NB8:(mt + 1) * NB8],
                          in_=ns.arg1[:, 224:224 + NB8], func=AF.Exp)
            sc.activation(out=ns.fchk[:, mt * NB32:(mt + 1) * NB32],
                          in_=ns.arg1[:, 288:288 + NB32], func=AF.Exp)
            sc.activation(out=ns.fchkr[:, mt * NB32:(mt + 1) * NB32],
                          in_=ns.arg1[:, 304:304 + NB32], func=AF.Exp)
            inst = sc.activation(out=ns.etot[:, mt * NCH:(mt + 1) * NCH],
                                 in_=ns.arg1[:, 320:320 + NCH], func=AF.Exp)
        s_fac = ctx.cnt["s"]

        @block.vector
        def _(v):
            v = ctx.ch("v", v)
            ctx.wait(v, "v", "s", s_fac)
            kv = ns.kb[:, c3 + 3:c3 + 3 + SEC]
            rv = ns.rb[:, co:co + SEC]
            F = lambda f: fam(f)[:, co:co + SEC]
            v.tensor_tensor(out=F("kr32_2"), in0=F("kr32_1"),
                            in1=bc_blocks(ns.ef32, mt * 17, 1, 16, rep=32),
                            op=ALU.mult)
            v.tensor_tensor(out=F("kr32_3"), in0=F("kr32_2"),
                            in1=bc_blocks(ns.ef32, mt * 17 + 1, 1, 16, rep=32),
                            op=ALU.mult)
            v.tensor_tensor(out=F("kr16_1"), in0=F("kr8_1"),
                            in1=bc_blocks(ns.f16k, mt * NB8, 1, NB8, rep=8),
                            op=ALU.mult)
            v.tensor_tensor(out=F("r16"), in0=F("r8"),
                            in1=bc_blocks(ns.f16r, mt * NB8, 1, NB8, rep=8),
                            op=ALU.mult)
            v.tensor_tensor(out=F("kd8"), in0=F("kr8_1"),
                            in1=bc_blocks(ns.efh8, mt * NB8, 1, NB8, rep=8),
                            op=ALU.mult)
            v.tensor_tensor(out=F("rd8"), in0=F("r8"),
                            in1=bc_blocks(ns.efh8b, mt * NB8, 1, NB8, rep=8),
                            op=ALU.mult)
            v.tensor_tensor(out=F("khat"), in0=F("kr32_1"),
                            in1=bc_blocks(ns.fchk, mt * NB32, 1, NB32, rep=32),
                            op=ALU.mult)
            v.tensor_tensor(out=F("rchunk"), in0=F("r32"),
                            in1=bc_blocks(ns.fchkr, mt * NB32, 1, NB32, rep=32),
                            op=ALU.mult)
            for f in ("kr32_1", "kr32_2", "kr32_3", "kr16_1", "kr8_1", "kd8",
                      "khat"):
                v.tensor_tensor(out=F(f), in0=F(f), in1=kv, op=ALU.mult)
            for f in ("r32", "r16", "r8", "rd8", "rchunk"):
                inst = v.tensor_tensor(out=F(f), in0=F(f), in1=rv, op=ALU.mult)
    ns.v_fam = ctx.cnt["v"]

    @block.tensor
    def _(t):
        ctx.wait(t, "t", "v", ns.v_fam)
        for c in range(NCH):
            for mt in range(4):
                bk = bank[6 + (c * 4 + mt) % 2]
                src = ns.fam_khat[:, mt * SEC + c * 128:mt * SEC + (c + 1) * 128]
                bk.tr(t, bk.h[:, 0:64].bitcast(BF16)[:, 0:128], src,
                      ns.ident[:, :])

                @block.scalar
                def _(sc, bk=bk, c=c, mt=mt):
                    sc = ctx.ch("s", sc)
                    ctx.wait(sc, "s", "t", ctx.cnt["t"])
                    inst = sc.copy(
                        out=ns.khtm[:, c * FH + mt * 128:c * FH + (mt + 1) * 128],
                        in_=bk.h[:, 0:64].bitcast(BF16)[:, 0:128])
                    bk.rd("s")
    ns.s_khtm = ctx.cnt["s"]


def emit_chunks(ns, s, csp):
    ctx, block, bank = ns.ctx, ns.block, ns.bank

    def fsl(f, h, cols):
        mt, hr = h // 2, 64 * (h % 2)
        return getattr(ns, "fam_" + f)[hr:hr + 64,
                                       mt * SEC + cols[0]:mt * SEC + cols[1]]

    for c in range(NCH):
        @block.vector
        def _(v):
            v = ctx.ch("v", v)
            ctx.wait(v, "v", "v", ctx.cnt["v"])
            ctx.wait(v, "v", "t", getattr(ns, "S_read", 0))
            inst = v.tensor_copy(out=ns.Sbf[:, :], in_=ns.Ssb[:, :])
        v_sbf = ctx.cnt["v"]

        for h in range(NHL):
            mt, hr = h // 2, 64 * (h % 2)
            cc = c * 128
            pm = bank[h % 2]
            a1, a2, a3 = bank[2], bank[3], bank[6]
            ob = bank[4 + h % 2]

            @block.tensor
            def _(t):
                ctx.wait(t, "t", "v", max(ns.v_fam, v_sbf))
                if s > 0 or c > 0 or h > 1:
                    pass
                for j in range(1, 4):
                    for i in range(j):
                        pm.mm(t, pm.h[32 * i:32 * i + 32, 32 * j:32 * j + 32],
                              fsl("kr32_%d" % (j - i), h,
                                  (cc + 32 * i, cc + 32 * i + 32)),
                              fsl("r32", h, (cc + 32 * j, cc + 32 * j + 32)),
                              start=True, stop=True,
                              tile_position=(hr, 32 * i))
                a1.mm(t, a1.h[:, 0:128], fsl("kd8", h, (cc, cc + 128)),
                      fsl("rd8", h, (cc, cc + 128)), start=True, stop=True,
                      tile_position=(hr, 0))
                a2.mm(t, a2.h[:, 0:128], fsl("kr8_1", h, (cc, cc + 128)),
                      fsl("r8", h, (cc, cc + 128)), start=True, stop=True,
                      tile_position=(hr, 0))
                a3.mm(t, a3.h[:, 0:128], fsl("kr16_1", h, (cc, cc + 128)),
                      fsl("r16", h, (cc, cc + 128)), start=True, stop=True,
                      tile_position=(hr, 0))
            t_pieces = ctx.cnt["t"]

            @block.vector
            def _(v):
                v = ctx.ch("v", v)
                ctx.wait(v, "v", "t", t_pieces)
                v.tensor_tensor(out=ns.psb[:, 0:128], in0=pm.h[:, 0:128],
                                in1=ns.mstr[:, :], op=ALU.mult)
                pm.rd("v")
                v.tensor_tensor(out=ns.psb[:, 128:256], in0=a1.h[:, 0:128],
                                in1=ns.md8[:, :], op=ALU.mult)
                a1.rd("v")
                v.tensor_tensor(out=ns.psb[:, 256:384], in0=a2.h[:, 0:128],
                                in1=ns.m8[:, :], op=ALU.mult)
                a2.rd("v")
                inst = v.tensor_tensor(out=ns.psb[:, 384:512],
                                       in0=a3.h[:, 0:128],
                                       in1=ns.m16[:, :], op=ALU.mult)
                a3.rd("v")
            v_psb = ctx.cnt["v"]
            s_psb = ctx.cnt["s"]

            @block.tensor
            def _(t):
                ctx.wait(t, "t", "v", v_psb)
                ctx.wait(t, "t", "s", max(s_psb, ns.s_khtm,
                                          getattr(ns, "s_vtm", 0)))
                vsl = ns.vtm[:, c * FH + 64 * h:c * FH + 64 * h + 64]
                for pc in range(4):
                    ob.mm(t, ob.h[0:64, 0:128], vsl,
                          ns.psb[:, 128 * pc:128 * pc + 128],
                          start=(pc == 0), stop=False)
                ob.mm(t, ob.h[0:64, 0:128], ns.Sbf[hr:hr + 64, 64 * mt:64 * mt + 64],
                      fsl("rchunk", h, (cc, cc + 128)), start=False, stop=True)
                ksl = ns.khtm[:, c * FH + 64 * h:c * FH + 64 * h + 64]
                a1.mm(t, a1.h[hr:hr + 64, 0:64], ksl, vsl, start=True,
                      stop=True, tile_position=(0, hr))
                ns.S_read = ctx.cnt["t"]

            @block.scalar
            def _(sc):
                sc = ctx.ch("s", sc)
                ctx.wait(sc, "s", "t", ctx.cnt["t"])
                inst = sc.copy(out=ns.osec[hr:hr + 64,
                                           mt * SEC + cc:mt * SEC + cc + 128],
                               in_=ob.h[0:64, 0:128])
                ob.rd("s")

            @block.vector
            def _(v):
                v = ctx.ch("v", v)
                ctx.wait(v, "v", "t", ctx.cnt["t"])
                inst = v.scalar_tensor_tensor(
                    out=ns.Ssb[hr:hr + 64, 64 * mt:64 * mt + 64],
                    in0=ns.Ssb[hr:hr + 64, 64 * mt:64 * mt + 64],
                    scalar=ns.etot[hr:hr + 64, mt * NCH + c:mt * NCH + c + 1],
                    in1=a1.h[hr:hr + 64, 0:64],
                    op0=ALU.mult, op1=ALU.add)
                a1.rd("v")
    ns.fam_free = ctx.cnt["t"]


def emit_bands_gn(ns, s, csp):
    ctx, block, bank = ns.ctx, ns.block, ns.bank
    P3 = SEC + 3

    for d in range(4):
        @block.vector
        def _(v, d=d):
            v = ctx.ch("v", v)
            ctx.wait(v, "v", "t", getattr(ns, "band_mm", 0))
            for mt in range(4):
                rv = ns.rb[:, mt * SEC:(mt + 1) * SEC]
                pb = ns.prodb[:, mt * SEC:(mt + 1) * SEC]
                if d == 0:
                    v.tensor_scalar(out=pb, in0=rv,
                                    scalar1=ns.v_uu[:, mt:mt + 1],
                                    scalar2=None, op0=ALU.mult)
                    v.tensor_tensor(
                        out=pb, in0=pb,
                        in1=ns.kb[:, mt * P3 + 3:mt * P3 + 3 + SEC],
                        op=ALU.mult)
                else:
                    ksh = ns.kb[:, mt * P3 + 3 - d:mt * P3 + 3 - d + SEC]
                    v.tensor_tensor(out=pb, in0=rv, in1=ksh, op=ALU.mult)
                    if d == 2:
                        v.tensor_tensor(
                            out=pb, in0=pb,
                            in1=ns.b2f[:, mt * SEC:(mt + 1) * SEC],
                            op=ALU.mult)
                    elif d == 3:
                        v.tensor_tensor(
                            out=pb, in0=pb,
                            in1=ns.b3f[:, mt * SEC:(mt + 1) * SEC],
                            op=ALU.mult)
        v_prod = ctx.cnt["v"]

        @block.tensor
        def _(t, d=d):
            ctx.wait(t, "t", "v", v_prod)
            bk = bank[6 + d % 2]
            for mt in range(4):
                bk.mm(t, bk.h[0:8, 0:SEC], ns.indbt[:, 8 * mt:8 * mt + 8],
                      ns.prodb[:, mt * SEC:(mt + 1) * SEC],
                      start=(mt == 0), stop=(mt == 3))
            ns.band_mm = ctx.cnt["t"]

        @block.vector
        def _(v, d=d):
            v = ctx.ch("v", v)
            ctx.wait(v, "v", "t", ns.band_mm)
            bk = bank[6 + d % 2]
            bm = getattr(ns, "browm%d" % d)
            if d == 0:
                v.tensor_copy(out=bm[:, 0:SEC], in_=bk.h[0:8, 0:SEC])
            else:
                v.tensor_tensor(out=bm[:, 0:SEC], in0=bk.h[0:8, 0:SEC],
                                in1=ns.bmask[:, (d - 1) * SEC:d * SEC],
                                op=ALU.mult)
            bk.rd("v")
    v_brow = ctx.cnt["v"]

    for d in range(4):
        for mt in range(4):
            @block.tensor
            def _(t, d=d, mt=mt):
                ctx.wait(t, "t", "v", max(v_brow, getattr(ns, "band_ap", 0)))
                bk = bank[4 + mt % 2]
                bk.mm(t, bk.h[:, 0:SEC], ns.indb[:, 128 * mt:128 * mt + 128],
                      getattr(ns, "browm%d" % d)[:, 0:SEC], start=True,
                      stop=True)

                @block.vector
                def _(v, bk=bk, d=d, mt=mt):
                    v = ctx.ch("v", v)
                    ctx.wait(v, "v", "t", ctx.cnt["t"])
                    vsh = ns.vb[:, mt * P3 + 3 - d:mt * P3 + 3 - d + SEC]
                    v.tensor_tensor(out=ns.ewb[:, mt * SEC:(mt + 1) * SEC],
                                    in0=bk.h[:, 0:SEC], in1=vsh, op=ALU.mult)
                    bk.rd("v")
                    v.tensor_tensor(
                        out=ns.osec[:, mt * SEC:(mt + 1) * SEC],
                        in0=ns.osec[:, mt * SEC:(mt + 1) * SEC],
                        in1=ns.ewb[:, mt * SEC:(mt + 1) * SEC],
                        op=ALU.add)
                    ns.band_ap = ctx.cnt["v"]

    for mt in range(4):
        @block.scalar
        def _(sc, mt=mt):
            sc = ctx.ch("s", sc)
            ctx.wait(sc, "s", "v", ctx.cnt["v"])
            sc.activation(out=ns.ewb[:, mt * SEC:(mt + 1) * SEC],
                          in_=ns.osec[:, mt * SEC:(mt + 1) * SEC],
                          func=AF.Square)
    s_sq = ctx.cnt["s"]

    @block.tensor
    def _(t):
        ctx.wait(t, "t", "s", s_sq)
        ctx.wait(t, "t", "v", ctx.cnt["v"])
        for mt in range(4):
            bank[6].mm(t, bank[6].h[0:8, 0:SEC],
                       ns.indbtf[:, 8 * mt:8 * mt + 8],
                       ns.osec[:, mt * SEC:(mt + 1) * SEC],
                       start=(mt == 0), stop=(mt == 3))
        for mt in range(4):
            bank[7].mm(t, bank[7].h[0:8, 0:SEC],
                       ns.indbtf[:, 8 * mt:8 * mt + 8],
                       ns.ewb[:, mt * SEC:(mt + 1) * SEC],
                       start=(mt == 0), stop=(mt == 3))
    t_gsum = ctx.cnt["t"]

    @block.vector
    def _(v):
        v = ctx.ch("v", v)
        ctx.wait(v, "v", "t", t_gsum)
        sr = ns.srow
        v.tensor_scalar(out=ns.mean8[:, 0:SEC], in0=bank[6].h[0:8, 0:SEC],
                        scalar1=1.0 / 64, scalar2=None, op0=ALU.mult)
        v.tensor_scalar(out=sr[0:8, 0:SEC], in0=bank[7].h[0:8, 0:SEC],
                        scalar1=1.0 / 64, scalar2=None, op0=ALU.mult)
        bank[6].rd("v")
        bank[7].rd("v")
        v.tensor_tensor(out=ns.svar[:, 0:SEC], in0=ns.mean8[:, 0:SEC],
                        in1=ns.mean8[:, 0:SEC], op=ALU.mult)
        v.tensor_tensor(out=ns.svar[:, 0:SEC], in0=sr[0:8, 0:SEC],
                        in1=ns.svar[:, 0:SEC], op=ALU.subtract)
        v.tensor_scalar(out=ns.svar[:, 0:SEC], in0=ns.svar[:, 0:SEC],
                        scalar1=EPS_GN, scalar2=None, op0=ALU.add)
        v.reciprocal(out=ns.rstd8[:, 0:SEC], in_=ns.svar[:, 0:SEC])

    @block.scalar
    def _(sc):
        sc = ctx.ch("s", sc)
        ctx.wait(sc, "s", "v", ctx.cnt["v"])
        sc.activation(out=ns.rstd8[:, 0:SEC], in_=ns.rstd8[:, 0:SEC],
                      func=AF.Sqrt)
    s_rstd8 = ctx.cnt["s"]

    for mt in range(4):
        @block.tensor
        def _(t, mt=mt):
            ctx.wait(t, "t", "s", s_rstd8)
            bk2 = bank[mt % 2]
            bk3 = bank[2 + mt % 2]
            bk2.mm(t, bk2.h[:, 0:SEC], ns.indbf[:, 128 * mt:128 * mt + 128],
                   ns.mean8[:, 0:SEC], start=True, stop=True)
            bk3.mm(t, bk3.h[:, 0:SEC], ns.indbf[:, 128 * mt:128 * mt + 128],
                   ns.rstd8[:, 0:SEC], start=True, stop=True)

        @block.vector
        def _(v, mt=mt):
            v = ctx.ch("v", v)
            ctx.wait(v, "v", "t", ctx.cnt["t"])
            ctx.dwait(v, "v", getattr(ns, "omix_st", None))
            bk2 = bank[mt % 2]
            bk3 = bank[2 + mt % 2]
            ob = ns.osec[:, mt * SEC:(mt + 1) * SEC]
            v.tensor_tensor(out=ob, in0=ob, in1=bk2.h[:, 0:SEC],
                            op=ALU.subtract)
            bk2.rd("v")
            v.tensor_tensor(out=ob, in0=ob, in1=bk3.h[:, 0:SEC], op=ALU.mult)
            bk3.rd("v")
            v.tensor_scalar(out=ob, in0=ob, scalar1=ns.v_gnw[:, mt:mt + 1],
                            scalar2=ns.v_gnb[:, mt:mt + 1], op0=ALU.mult,
                            op1=ALU.add)
            v.tensor_tensor(out=ns.omix[:, mt * SEC:(mt + 1) * SEC], in0=ob,
                            in1=ns.gb[:, mt * SEC:(mt + 1) * SEC],
                            op=ALU.mult)
    v_omix = ctx.cnt["v"]

    @block.sync
    def _(q):
        ctx.wait(q, "q", "v", v_omix)
        ns.omix_st = ctx.dgroup(q, "q", [
            (ns.ORW_B[128 * mt:128 * (mt + 1), s * SEC:(s + 1) * SEC],
             ns.omix[:, mt * SEC:(mt + 1) * SEC]) for mt in range(4)], "sq")
        ns.omix_keys.append(ns.omix_st)


def emit_mixer_taps(ns):
    ctx, block = ns.ctx, ns.block
    if not ns.taps:
        return

    @block.sync
    def _(q):
        ctx.wait(q, "q", "v", ctx.cnt["v"])
        ctx.wait(q, "q", "s", ctx.cnt["s"])
        prs = []
        for nm, ap in ns.taps.items():
            src = getattr(ns, nm, None)
            if src is None:
                continue
            prs.append((ap, src[:, 0:ap.shape[1]]))
        ns.tap_st = ctx.dgroup(q, "q", prs, "sq")


def emit_finish(ns):
    ctx, block = ns.ctx, ns.block

    @block.sync
    def _(q):
        ctx.wait(q, "q", "v", ctx.cnt["v"])
        srcbuf = getattr(ns, "x2buf", None)
        if srcbuf is not None:
            key = ctx.dgroup(q, "q", [
                (ns.OUT[128 * p:128 * (p + 1), 0:TO],
                 srcbuf[:, p * TO:(p + 1) * TO]) for p in range(8)], "sq")
        else:
            key = ctx.dgroup(q, "q", [
                (ns.OUT[128 * p:128 * (p + 1), 0:TO], ns.xs[:, 0:TO])
                for p in range(8)], "sq")
        for nm, val in ctx.dcnt.items():
            if val > 0:
                q.wait_ge(ctx.dsem[nm], val)


def _build_consts():
    c = {}
    i = np.arange(128)
    c["ident"] = np.eye(128, dtype=np.float32)
    c["indh"] = np.stack([(i < 64).astype(np.float32),
                          (i >= 64).astype(np.float32)], 1)
    c["indhb"] = c["indh"]
    indb = np.zeros((8, 512), np.float32)
    for mt in range(4):
        indb[2 * mt, 128 * mt:128 * mt + 64] = 1.0
        indb[2 * mt + 1, 128 * mt + 64:128 * mt + 128] = 1.0
    c["indb"] = indb
    c["indbf"] = indb
    indbt = np.zeros((128, 32), np.float32)
    for mt in range(4):
        indbt[0:64, 8 * mt + 2 * mt] = 1.0
        indbt[64:128, 8 * mt + 2 * mt + 1] = 1.0
    c["indbt"] = indbt
    c["indbtf"] = indbt
    ss, tt = np.meshgrid(i, i, indexing="ij")
    same8 = (ss // 8) == (tt // 8)
    c["md8"] = (same8 & (tt - ss >= 4)).astype(np.float32)
    same32 = (tt // 32) == (ss // 32)
    c["m8"] = (((tt // 8) == (ss // 8) + 1) & same32).astype(np.float32)
    c["m16"] = (((tt // 16) == (ss // 16) + 1) &
                ((tt // 8) >= (ss // 8) + 2) & same32).astype(np.float32)
    c["mstr"] = ((tt // 32) > (ss // 32)).astype(np.float32)
    tmod = np.arange(SEC) % 8
    bm = np.zeros((8, 3 * SEC), np.float32)
    for d in (1, 2, 3):
        bm[:, (d - 1) * SEC:d * SEC] = (tmod >= d)[None, :]
    c["bmask"] = bm
    c["onesc"] = np.ones((1, 128), np.float32)
    c["onesr"] = np.ones((128, 1), np.float32)
    c["identf"] = np.eye(128, dtype=np.float32)
    perm = np.zeros((128, 128), np.float32)
    for mm_ in range(128):
        s_ = mm_ + 32 if (mm_ % 64) < 32 else mm_ - 32
        perm[s_, mm_] = 1.0
    c["rope_perm"] = perm
    inde = np.zeros((16, 16 * 128), np.float32)
    for e in range(16):
        inde[e, e * 128:(e + 1) * 128] = 1.0
    c["inde16"] = inde
    pos = np.arange(TF, dtype=np.float64)
    dim = np.arange(32, dtype=np.float64)
    ang = pos[None, :] / (10000.0 ** (dim / 32.0))[:, None]
    sn, cs_ = np.sin(ang), np.cos(ang)
    rs = np.zeros((128, TF)); rc = np.zeros((128, TF))
    for half in range(2):
        b = 64 * half
        rs[b:b + 32] = -sn
        rs[b + 32:b + 64] = sn
        rc[b:b + 32] = cs_
        rc[b + 32:b + 64] = cs_
    c["ropesin"] = rs
    c["ropecos"] = rc
    ii, qq = np.meshgrid(np.arange(128), np.arange(512), indexing="ij")
    cm = np.zeros((128, 4 * 512), np.float32)
    for j in range(4):
        cm[:, j * 512:(j + 1) * 512] = (qq - ii >= 128 * j)
    c["cmask"] = cm
    return c


def _pack_vec(x, ncols):
    return np.ascontiguousarray(x.reshape(ncols, 128).T)


def make_in_maps(inputs, n_sec=TF // SEC):
    consts = _build_consts()
    bf = np.dtype("bfloat16") if False else None
    import ml_dtypes
    BFD = ml_dtypes.bfloat16
    in_maps = []
    x = np.asarray(inputs["L0_attn_x"], np.float32)
    for core in range(8):
        b, rank = core // 2, core % 2
        hs = slice(rank * FH, (rank + 1) * FH)
        m = {}
        xf = np.ascontiguousarray(x[b].T)
        m["xf"] = xf
        m["xo"] = np.ascontiguousarray(xf[:, rank * TO:(rank + 1) * TO])
        m["wr"] = np.ascontiguousarray(inputs["L0_attn_Wr"][:, hs], np.float32)
        m["wk"] = np.ascontiguousarray(inputs["L0_attn_Wk"][:, hs], np.float32)
        m["wv"] = np.ascontiguousarray(inputs["L0_attn_Wv"][:, hs], np.float32)
        m["wg"] = np.ascontiguousarray(inputs["L0_attn_Wg"][:, hs], np.float32)
        m["wo"] = np.ascontiguousarray(inputs["L0_attn_Wo"], np.float32)
        m["w1"] = np.ascontiguousarray(inputs["L0_attn_ffn_w1"], np.float32)
        m["w2"] = np.ascontiguousarray(inputs["L0_attn_ffn_w2"], np.float32)
        m["wqt"] = np.ascontiguousarray(np.asarray(inputs["L0_ffn_weight"], np.float32).T)
        m["wla"] = np.ascontiguousarray(inputs["L0_attn_wA"], np.float32)
        m["wlb"] = np.ascontiguousarray(inputs["L0_attn_wB"][:, hs], np.float32)
        for nm, key in (("mu_r", "L0_attn_mu_r"), ("mu_k", "L0_attn_mu_k"),
                        ("mu_v", "L0_attn_mu_v"), ("mu_w", "L0_attn_mu_w"),
                        ("mu_g", "L0_attn_mu_g"), ("ln1w", "L0_attn_ln1_w"),
                        ("ln2w", "L0_attn_ln2_w")):
            m[nm] = _pack_vec(np.asarray(inputs[key], np.float32), 8)
        m["w0"] = _pack_vec(np.asarray(inputs["L0_attn_w0"], np.float32)[rank * FH:(rank + 1) * FH], 4)
        m["uu"] = _pack_vec(np.asarray(inputs["L0_attn_u"], np.float32).reshape(-1)[rank * FH:(rank + 1) * FH], 4)
        m["gnw"] = _pack_vec(np.asarray(inputs["L0_attn_gn_w"], np.float32).reshape(-1)[rank * FH:(rank + 1) * FH], 4)
        m["gnb"] = _pack_vec(np.asarray(inputs["L0_attn_gn_b"], np.float32).reshape(-1)[rank * FH:(rank + 1) * FH], 4)
        for k, v in consts.items():
            if k in ("ident", "indhb", "indb", "indbt", "md8", "m8", "m16", "mstr", "bmask", "inde16", "cmask"):
                m[k] = v.astype(BFD)
            else:
                m[k] = v.astype(np.float32)
        hsel = slice(rank * 8, (rank + 1) * 8)
        for nm, key in (("l1wq", "L1_attn_W_Q"), ("l1wk", "L1_attn_W_K"),
                        ("l1wv", "L1_attn_W_V")):
            w3 = np.asarray(inputs[key], np.float32)[hsel]
            m[nm] = np.ascontiguousarray(w3.transpose(1, 0, 2).reshape(D, FH))
        m["l1wo"] = np.ascontiguousarray(
            np.asarray(inputs["L1_attn_W_O"], np.float32).reshape(D, D))
        m["l1bq"] = _pack_vec(np.asarray(inputs["L1_attn_b_Q"],
                                         np.float32).reshape(-1)[hs], 4)
        m["l1bk"] = _pack_vec(np.asarray(inputs["L1_attn_b_K"],
                                         np.float32).reshape(-1)[hs], 4)
        m["l1bv_row"] = np.ascontiguousarray(
            np.asarray(inputs["L1_attn_b_V"], np.float32).reshape(1, -1)[:, hs])
        m["l1bo"] = _pack_vec(np.asarray(inputs["L1_attn_b_O"], np.float32), 8)
        m["gate_wt"] = np.ascontiguousarray(
            np.asarray(inputs["L1_ffn_gate_w"], np.float32).T)
        m["bias_e"] = np.ascontiguousarray(
            np.asarray(inputs["L1_ffn_bias_e"], np.float32).reshape(1, 16))
        m["moe_up"] = np.ascontiguousarray(
            np.asarray(inputs["L1_ffn_up_w"], np.float32).transpose(0, 2, 1)
            .reshape(MOE_NE * D, 512))
        m["moe_dn"] = np.ascontiguousarray(
            np.asarray(inputs["L1_ffn_down_w"], np.float32).transpose(0, 2, 1)
            .reshape(MOE_NE * 512, D))
        m["sh_upt"] = np.ascontiguousarray(
            np.asarray(inputs["L1_ffn_shared_up"], np.float32).T)
        m["sh_dnt"] = np.ascontiguousarray(
            np.asarray(inputs["L1_ffn_shared_down"], np.float32).T)
        mab = np.zeros((128, 2), np.float32)
        mab[:, 0] = 1.0 if rank == 0 else 0.0
        mab[:, 1] = 1.0 if rank == 1 else 0.0
        m["mab"] = mab
        in_maps.append(m)
    return in_maps


def kernel(**inputs):
    nc, ns = build_program()
    in_maps = make_in_maps(inputs)
    res = run_bass_kernel_spmd(nc, in_maps, core_ids=list(range(8)))
    out = np.zeros((B, TF, D), np.float32)
    for core in range(8):
        b, rank = core // 2, core % 2
        out[b, rank * TO:(rank + 1) * TO] = res.results[core]["out"].T
    return out


def emit_phase_join(ns):
    ctx, block = ns.ctx, ns.block
    tv, ts_, tt, tg = ctx.cnt["v"], ctx.cnt["s"], ctx.cnt["t"], ctx.cnt["g"]
    dkeys = [(nm, val) for nm, val in ctx.dcnt.items() if val > 0]

    @block.vector
    def _(v):
        v = ctx.ch("v", v)
        for e, c in (("s", ts_), ("t", tt), ("g", tg)):
            ctx.wait(v, "v", e, c)
        for nm, val in dkeys:
            ctx.wait(v, "v", nm, val)
        v.engine_nop()
    vj = ctx.cnt["v"]

    @block.scalar
    def _(sc):
        ctx.wait(sc, "s", "v", vj)

    @block.tensor
    def _(t):
        ctx.wait(t, "t", "v", vj)

    @block.gpsimd
    def _(g):
        ctx.wait(g, "g", "v", vj)

    @block.sync
    def _(q):
        ctx.wait(q, "q", "v", vj)
    ns.join_v = vj


def emit_colstats(ns, buf, coff, ncols, mode, eps, scale=None, out=None,
                  bufW=None, outW=None, ooff=0):
    ctx, block, bank = ns.ctx, ns.block, ns.bank
    W = bufW if bufW is not None else ns.postW
    out = out if out is not None else buf
    outW = outW if outW is not None else W
    if out is buf:
        ooff = coff

    for p in range(8):
        @block.scalar
        def _(sc, p=p):
            sc = ctx.ch("s", sc)
            if p == 0:
                ctx.wait(sc, "s", "v", ctx.cnt["v"])
            else:
                ctx.wait(sc, "s", "t", ctx.cnt["t"])
            sc.activation(out=ns.psq[:, 0:ncols],
                          in_=buf[:, p * W + coff:p * W + coff + ncols],
                          func=AF.Square)

        @block.tensor
        def _(t, p=p):
            ctx.wait(t, "t", "s", ctx.cnt["s"])
            bank[6].mm(t, bank[6].h[0:1, 0:ncols], ns.onesr[:, :],
                       ns.psq[:, 0:ncols], start=(p == 0), stop=(p == 7))
            if mode == "ln":
                bank[7].mm(t, bank[7].h[0:1, 0:ncols], ns.onesr[:, :],
                           buf[:, p * W + coff:p * W + coff + ncols],
                           start=(p == 0), stop=(p == 7))
    t_stats = ctx.cnt["t"]

    @block.vector
    def _(v):
        v = ctx.ch("v", v)
        ctx.wait(v, "v", "t", t_stats)
        r1 = ns.prow1[0:1, 0:ncols]
        r2 = ns.prow2[0:1, 0:ncols]
        if mode == "rms":
            v.tensor_scalar(out=r1, in0=bank[6].h[0:1, 0:ncols],
                            scalar1=1.0 / D, scalar2=eps, op0=ALU.mult,
                            op1=ALU.add)
            bank[6].rd("v")
            v.reciprocal(out=r2, in_=r1)
        else:
            v.tensor_scalar(out=ns.prow3[0:1, 0:ncols],
                            in0=bank[7].h[0:1, 0:ncols],
                            scalar1=1.0 / D, scalar2=None, op0=ALU.mult)
            bank[7].rd("v")
            v.tensor_scalar(out=r1, in0=bank[6].h[0:1, 0:ncols],
                            scalar1=1.0 / D, scalar2=None, op0=ALU.mult)
            bank[6].rd("v")
            v.tensor_tensor(out=r2, in0=ns.prow3[0:1, 0:ncols],
                            in1=ns.prow3[0:1, 0:ncols], op=ALU.mult)
            v.tensor_tensor(out=r1, in0=r1, in1=r2, op=ALU.subtract)
            v.tensor_scalar(out=r1, in0=r1, scalar1=eps, scalar2=None,
                            op0=ALU.add)
            v.reciprocal(out=r2, in_=r1)

    @block.scalar
    def _(sc):
        sc = ctx.ch("s", sc)
        ctx.wait(sc, "s", "v", ctx.cnt["v"])
        sc.activation(out=ns.prow2[0:1, 0:ncols], in_=ns.prow2[0:1, 0:ncols],
                      func=AF.Sqrt)
    s_r = ctx.cnt["s"]

    @block.tensor
    def _(t):
        ctx.wait(t, "t", "s", s_r)
        bank[6].mm(t, bank[6].h[:, 0:ncols], ns.onesc[:, :],
                   ns.prow2[0:1, 0:ncols], start=True, stop=True)
        if mode == "ln":
            bank[7].mm(t, bank[7].h[:, 0:ncols], ns.onesc[:, :],
                       ns.prow3[0:1, 0:ncols], start=True, stop=True)
    t_bc = ctx.cnt["t"]

    @block.vector
    def _(v):
        v = ctx.ch("v", v)
        ctx.wait(v, "v", "t", max(t_bc, ctx.cnt["t"]))
        for p in range(8):
            src = buf[:, p * W + coff:p * W + coff + ncols]
            dst = out[:, p * outW + ooff:p * outW + ooff + ncols]
            if mode == "rms":
                if scale is not None:
                    v.scalar_tensor_tensor(out=dst, in0=src,
                                           scalar=scale[:, p:p + 1],
                                           in1=bank[6].h[:, 0:ncols],
                                           op0=ALU.mult, op1=ALU.mult)
                else:
                    v.tensor_tensor(out=dst, in0=src,
                                    in1=bank[6].h[:, 0:ncols], op=ALU.mult)
            else:
                v.tensor_tensor(out=ns.ptt[:, 0:ncols], in0=src,
                                in1=bank[7].h[:, 0:ncols], op=ALU.subtract)
                v.tensor_tensor(out=dst, in0=ns.ptt[:, 0:ncols],
                                in1=bank[6].h[:, 0:ncols], op=ALU.mult)
        bank[6].rd("v")
        if mode == "ln":
            bank[7].rd("v")
    return ctx.cnt["v"]


def emit_l0post(ns):
    ctx, block, bank, nc = ns.ctx, ns.block, ns.bank, ns.nc
    sb = ns.sb
    es2 = ns.es.enter_context(ExitStack())
    sb("xa", (128, 8 * TO), F32, stack=es2)
    sb("psq", (128, 512), F32, stack=es2)
    sb("prow1", (1, 512), F32, stack=es2)
    sb("prow2", (1, 512), F32, stack=es2)
    sb("prow3", (1, 512), F32, stack=es2)
    sb("ptt", (128, 512), F32, stack=es2)
    ns.es2x = ExitStack()
    sb("x0w", (128, 8 * 512), F32, stack=ns.es2x)
    ns.postW = TO
    ns.postWout = TO
    NCK = TO // 512

    @block.gpsimd
    def _(g):
        for key in ns.omix_keys:
            ctx.dwait(g, "g", key)
        ns.cc_orw = ctx.ccg(g, "AllGather", [ns.ORW_B], [ns.ORW_F], ns.groups)

    es3 = ExitStack()
    sb("wob", (128, 8 * D), BF16, stack=es3)
    sb("ogh0", (128, 8 * 512), BF16, stack=es3)
    sb("ogh1", (128, 8 * 512), BF16, stack=es3)
    sb("ogg", (128, 8 * 512), BF16, stack=es3)

    @block.gpsimd
    def _(g):
        ns.wo_ld = ctx.dgroup(g, "g", [
            (ns.wob[:, p * D:(p + 1) * D], ns.I_wo[128 * p:128 * (p + 1), :])
            for p in range(8)], "lg")

    for ck in range(NCK):
        @block.sync
        def _(q, ck=ck):
            ctx.dwait(q, "q", ns.cc_orw)
            ctx.wait(q, "q", "v", getattr(ns, "og_free_v", 0))
            prs = []
            for p in range(8):
                prs.append((ns.ogh0[:, p * 512:(p + 1) * 512],
                            ns.ORW_F[128 * p:128 * (p + 1),
                                     ck * 512:ck * 512 + 512]))
                prs.append((ns.ogh1[:, p * 512:(p + 1) * 512],
                            ns.ORW_F[128 * p:128 * (p + 1),
                                     TO + ck * 512:TO + ck * 512 + 512]))
            ns.og_ld = ctx.dgroup(q, "q", prs, "lq")

        @block.vector
        def _(v, ck=ck):
            v = ctx.ch("v", v)
            ctx.dwait(v, "v", ns.og_ld)
            ctx.wait(v, "v", "t", getattr(ns, "og_free", 0))
            for p in range(8):
                v.tensor_scalar(out=ns.ogg[:, p * 512:(p + 1) * 512],
                                in0=ns.ogh0[:, p * 512:(p + 1) * 512],
                                scalar1=ns.mab[:, 0:1], scalar2=None,
                                op0=ALU.mult)
                v.scalar_tensor_tensor(out=ns.ogg[:, p * 512:(p + 1) * 512],
                                       in0=ns.ogh1[:, p * 512:(p + 1) * 512],
                                       scalar=ns.mab[:, 1:2],
                                       in1=ns.ogg[:, p * 512:(p + 1) * 512],
                                       op0=ALU.mult, op1=ALU.add)
            ns.og_free_v = ctx.cnt["v"]
        v_og = ctx.cnt["v"]

        @block.sync
        def _(q, ck=ck):
            ctx.wait(q, "q", "v", getattr(ns, "x0w_free", 0))
            ns.x0w_ld = ctx.dgroup(q, "q", [
                (ns.x0w[:, p * 512:(p + 1) * 512],
                 ns.I_xo[128 * p:128 * (p + 1), ck * 512:ck * 512 + 512])
                for p in range(8)], "lq")

        for half in range(2):
            @block.tensor
            def _(t, ck=ck, half=half):
                ctx.wait(t, "t", "v", v_og)
                ctx.dwait(t, "t", ns.wo_ld)
                for k in range(8):
                    for mi in range(4):
                        mt = half * 4 + mi
                        bk = bank[mi]
                        bk.mm(t, bk.h[:, 0:512],
                              ns.wob[:, k * D + mt * 128:k * D + (mt + 1) * 128],
                              ns.ogg[:, k * 512:(k + 1) * 512],
                              start=(k == 0), stop=(k == 7))
                ns.og_free = ctx.cnt["t"]

            @block.vector
            def _(v, ck=ck, half=half):
                v = ctx.ch("v", v)
                ctx.wait(v, "v", "t", ctx.cnt["t"])
                ctx.dwait(v, "v", ns.x0w_ld)
                for mi in range(4):
                    mt = half * 4 + mi
                    bk = bank[mi]
                    v.tensor_tensor(
                        out=ns.xa[:, mt * TO + ck * 512:mt * TO + ck * 512 + 512],
                        in0=bk.h[:, 0:512],
                        in1=ns.x0w[:, mt * 512:(mt + 1) * 512],
                        op=ALU.add)
                    bk.rd("v")
                ns.x0w_free = ctx.cnt["v"]
    emit_phase_join(ns)
    es3.close()
    ns.og_ld = None
    if ns.stop_after == "wo":
        ns.x2buf = ns.xa
        emit_phase_join(ns)
        ns.es2x.close()
        return

    es4 = ExitStack()
    sb("h2c", (128, 8 * TO), BF16, stack=es4)
    sb("wf1h", (128, 8 * 2048), BF16, stack=es4)
    sb("wf2h", (128, 16 * D), BF16, stack=es4)
    sb("hallh", (128, 16 * 256), BF16, stack=es4)

    ns.wf_ld = None
    ns.wf_free = 0
    for ck in range(NCK):
        emit_colstats(ns, ns.xa, ck * 512, 512, "rms", EPS_RMS,
                      scale=ns.v_ln2w, out=ns.h2c, bufW=TO, outW=TO,
                      ooff=ck * 512)
    v_h2 = ctx.cnt["v"]
    for hib2 in range(2):
        @block.gpsimd
        def _(g, hib2=hib2):
            g = ctx.ch("g", g)
            ctx.wait(g, "g", "t", ns.wf_free)
            prs = []
            for p in range(8):
                prs.append((ns.wf1h[:, p * 2048:(p + 1) * 2048],
                            ns.I_w1[128 * p:128 * (p + 1),
                                    hib2 * 2048:(hib2 + 1) * 2048]))
            for kk in range(16):
                prs.append((ns.wf2h[:, kk * D:(kk + 1) * D],
                            ns.I_w2[hib2 * 2048 + 128 * kk:
                                    hib2 * 2048 + 128 * (kk + 1), :]))
            ns.wf_ld = ctx.dgroup(g, "g", prs, "lg")

        for ck in range(NCK):
            for tc2 in range(2):
                tco = ck * 512 + tc2 * 256
                for ib4 in range(4):
                    @block.tensor
                    def _(t, ib4=ib4, tco=tco):
                        ctx.wait(t, "t", "v", v_h2)
                        ctx.dwait(t, "t", ns.wf_ld)
                        for k in range(8):
                            for mi in range(4):
                                bk = bank[4 + mi]
                                woff = (ib4 * 4 + mi) * 128
                                bk.mm(t, bk.h[:, 0:256],
                                      ns.wf1h[:, k * 2048 + woff:
                                              k * 2048 + woff + 128],
                                      ns.h2c[:, k * TO + tco:
                                             k * TO + tco + 256],
                                      start=(k == 0), stop=(k == 7))
                    t_up = ctx.cnt["t"]

                    @block.scalar
                    def _(sc, ib4=ib4):
                        sc = ctx.ch("s", sc)
                        ctx.wait(sc, "s", "t", t_up)
                        for mi in range(4):
                            bk = bank[4 + mi]
                            sc.activation(
                                out=ns.hallh[:, (ib4 * 4 + mi) * 256:
                                             (ib4 * 4 + mi + 1) * 256],
                                in_=bk.h[:, 0:256], func=AF.Gelu)
                            bk.rd("s")
                s_hall = ctx.cnt["s"]

                for half in range(2):
                    @block.tensor
                    def _(t, ck=ck, tc2=tc2, half=half, s_hall=s_hall):
                        ctx.wait(t, "t", "s", s_hall)
                        for kk in range(16):
                            for mi in range(4):
                                mt = half * 4 + mi
                                bk = bank[mi]
                                bk.mm(t, bk.h[:, 0:256],
                                      ns.wf2h[:, kk * D + mt * 128:
                                              kk * D + (mt + 1) * 128],
                                      ns.hallh[:, kk * 256:(kk + 1) * 256],
                                      start=(kk == 0), stop=(kk == 15))
                        ns.wf_free = ctx.cnt["t"]

                    @block.vector
                    def _(v, ck=ck, tc2=tc2, half=half):
                        v = ctx.ch("v", v)
                        ctx.wait(v, "v", "t", ctx.cnt["t"])
                        for mi in range(4):
                            mt = half * 4 + mi
                            bk = bank[mi]
                            co = mt * TO + ck * 512 + tc2 * 256
                            v.tensor_tensor(out=ns.xa[:, co:co + 256],
                                            in0=bk.h[:, 0:256],
                                            in1=ns.xa[:, co:co + 256],
                                            op=ALU.add)
                            bk.rd("v")
    emit_phase_join(ns)
    es4.close()
    if ns.stop_after == "ffn":
        ns.x2buf = ns.xa
        emit_phase_join(ns)
        ns.es2x.close()
        return

    es5a = ExitStack()
    ns.x0c = ns.x0w
    for ck in range(NCK):
        @block.sync
        def _(q, ck=ck):
            ctx.wait(q, "q", "v", getattr(ns, "x0c_free", 0))
            ns.x0c_ld = ctx.dgroup(q, "q", [
                (ns.x0c[:, p * 512:(p + 1) * 512],
                 ns.I_xo[128 * p:128 * (p + 1), ck * 512:ck * 512 + 512])
                for p in range(8)], "lq")

        @block.vector
        def _(v, ck=ck):
            v = ctx.ch("v", v)
            ctx.dwait(v, "v", ns.x0c_ld)
            for p in range(8):
                co = p * TO + ck * 512
                v.tensor_tensor(out=ns.xa[:, co:co + 512],
                                in0=ns.x0c[:, p * 512:(p + 1) * 512],
                                in1=ns.xa[:, co:co + 512], op=ALU.add)
            ns.x0c_free = ctx.cnt["v"]
        emit_colstats(ns, ns.xa, ck * 512, 512, "ln", EPS_LN, bufW=TO)
    emit_phase_join(ns)
    es5a.close()
    if ns.stop_after == "x1":
        ns.x2buf = ns.xa
        emit_phase_join(ns)
        ns.es2x.close()
        return

    es5 = ExitStack()
    sb("wqf", (128, 8 * D), F32, stack=es5)
    sb("wqb", (128, 8 * D), BF16, stack=es5)
    sb("qsc", (128, 1), F32, stack=es5)
    sb("x1b", (128, 8 * 512), BF16, stack=es5)

    @block.sync
    def _(q):
        ns.wq_ld = ctx.dgroup(q, "q", [
            (ns.wqf[:, p * D:(p + 1) * D], ns.I_wqt[128 * p:128 * (p + 1), :])
            for p in range(8)], "lq")

    @block.vector
    def _(v):
        v = ctx.ch("v", v)
        ctx.dwait(v, "v", ns.wq_ld)
        v.tensor_reduce(out=ns.psq[:, 0:1], in_=ns.wqf[:, :], axis=AX.X,
                        op=ALU.add, apply_absolute_value=True)
    v_abs = ctx.cnt["v"]

    @block.tensor
    def _(t):
        ctx.wait(t, "t", "v", v_abs)
        bank[6].mm(t, bank[6].h[0:1, 0:1], ns.onesr[:, :], ns.psq[:, 0:1],
                   start=True, stop=True)
    t_abs = ctx.cnt["t"]

    @block.vector
    def _(v):
        v = ctx.ch("v", v)
        ctx.wait(v, "v", "t", t_abs)
        v.tensor_scalar(out=ns.prow1[0:1, 0:1], in0=bank[6].h[0:1, 0:1],
                        scalar1=1.0 / (D * D), scalar2=None, op0=ALU.mult)
        bank[6].rd("v")
        v.tensor_scalar_max(out=ns.prow1[0:1, 0:1], in0=ns.prow1[0:1, 0:1],
                            scalar1=1e-5)
    v_sc = ctx.cnt["v"]

    @block.tensor
    def _(t):
        ctx.wait(t, "t", "v", v_sc)
        bank[6].mm(t, bank[6].h[:, 0:1], ns.onesc[:, :], ns.prow1[0:1, 0:1],
                   start=True, stop=True)
    t_scbc = ctx.cnt["t"]

    @block.vector
    def _(v):
        v = ctx.ch("v", v)
        ctx.wait(v, "v", "t", t_scbc)
        v.tensor_copy(out=ns.qsc[:, :], in_=bank[6].h[:, 0:1])
        bank[6].rd("v")
        v.reciprocal(out=ns.qsc[:, :], in_=ns.qsc[:, :])
        v.tensor_single_scalar(out=ns.wqf[:, :], in_=ns.wqf[:, :],
                               scalar=ns.qsc[:, :], op=ALU.mult)
        v.tensor_single_scalar(out=ns.wqb[:, :], in_=ns.wqf[:, :], scalar=0.5,
                               op=ALU.is_gt)
        v.tensor_single_scalar(out=ns.wqf[:, :], in_=ns.wqf[:, :], scalar=-0.5,
                               op=ALU.is_lt)
        v.tensor_tensor(out=ns.wqf[:, :], in0=ns.wqb[:, :], in1=ns.wqf[:, :],
                        op=ALU.subtract)
        v.reciprocal(out=ns.qsc[:, :], in_=ns.qsc[:, :])
        v.tensor_single_scalar(out=ns.wqb[:, :], in_=ns.wqf[:, :],
                               scalar=ns.qsc[:, :], op=ALU.mult)
    v_wq = ctx.cnt["v"]

    for ck in range(NCK):
        @block.vector
        def _(v, ck=ck):
            v = ctx.ch("v", v)
            ctx.wait(v, "v", "t", ctx.cnt["t"])
            for p in range(8):
                v.tensor_copy(out=ns.x1b[:, p * 512:(p + 1) * 512],
                              in_=ns.xa[:, p * TO + ck * 512:p * TO + ck * 512 + 512])
        v_x1b = ctx.cnt["v"]
        for half in range(2):
            @block.tensor
            def _(t, ck=ck, half=half, v_x1b=v_x1b):
                ctx.wait(t, "t", "v", max(v_wq, v_x1b))
                for k in range(8):
                    for mi in range(4):
                        mt = half * 4 + mi
                        bk = bank[mi]
                        bk.mm(t, bk.h[:, 0:512],
                              ns.wqb[:, k * D + mt * 128:k * D + (mt + 1) * 128],
                              ns.x1b[:, k * 512:(k + 1) * 512],
                              start=(k == 0), stop=(k == 7))

            @block.vector
            def _(v, ck=ck, half=half):
                v = ctx.ch("v", v)
                ctx.wait(v, "v", "t", ctx.cnt["t"])
                for mi in range(4):
                    mt = half * 4 + mi
                    bk = bank[mi]
                    co = mt * TO + ck * 512
                    v.tensor_tensor(out=ns.xa[:, co:co + 512],
                                    in0=bk.h[:, 0:512],
                                    in1=ns.xa[:, co:co + 512], op=ALU.add)
                    bk.rd("v")
    emit_phase_join(ns)
    es5.close()

    for ck in range(NCK):
        emit_colstats(ns, ns.xa, ck * 512, 512, "ln", EPS_LN, bufW=TO)
    ns.x2buf = ns.xa
    ns.x2_done = ctx.cnt["v"]
    emit_phase_join(ns)
    ns.es2x.close()


def emit_l1attn(ns):
    ctx, block, bank, nc, sb = ns.ctx, ns.block, ns.bank, ns.nc, ns.sb
    NCK = TO // 512

    esx = ExitStack()
    sb("x2c", (128, 8 * 512), BF16, stack=esx)
    for ck in range(NCK):
        @block.vector
        def _(v, ck=ck):
            v = ctx.ch("v", v)
            ctx.dwait(v, "v", getattr(ns, "x2c_st", None))
            for p in range(8):
                v.tensor_copy(out=ns.x2c[:, p * 512:(p + 1) * 512],
                              in_=ns.xa[:, p * TO + ck * 512:p * TO + ck * 512 + 512])
        vv = ctx.cnt["v"]

        @block.sync
        def _(q, ck=ck, vv=vv):
            ctx.wait(q, "q", "v", vv)
            ns.x2c_st = ctx.dgroup(q, "q", [
                (ns.X2B[128 * p:128 * (p + 1), ck * 512:ck * 512 + 512],
                 ns.x2c[:, p * 512:(p + 1) * 512]) for p in range(8)], "sq")
            ns.x2_sts = getattr(ns, "x2_sts", [])
            ns.x2_sts.append(ns.x2c_st)
    emit_phase_join(ns)
    esx.close()

    @block.gpsimd
    def _(g):
        for key in ns.x2_sts:
            ctx.dwait(g, "g", key)
        ns.cc_x2 = ctx.ccg(g, "AllGather", [ns.X2B], [ns.X2F], ns.groups)

    es6 = ExitStack()
    sb("qb2", (128, 4 * TF), BF16, stack=es6)
    sb("kb2", (128, 4 * TF), BF16, stack=es6)
    sb("xst", (128, 8 * 512), BF16, stack=es6)

    es6q = ExitStack()
    sb("sinc", (128, 512), F32, stack=es6q)
    sb("cosc", (128, 512), F32, stack=es6q)
    sb("wl1", (128, 8 * FH), BF16, stack=es6q)
    sb("wl2", (128, 8 * FH), BF16, stack=es6q)
    sb("qraw", (128, 4 * 512), F32, stack=es6q)
    sb("rope_perm", (128, 128), F32, stack=es6q)
    sb("v_l1bq", (128, 4), F32, stack=es6q)
    sb("v_l1bk", (128, 4), F32, stack=es6q)

    @block.gpsimd
    def _(g):
        g = ctx.ch("g", g)
        prs = []
        for nm, buf in (("l1wq", ns.wl1), ("l1wk", ns.wl2)):
            srcd = getattr(ns, "I_" + nm)
            for p in range(8):
                prs.append((buf[:, p * FH:(p + 1) * FH],
                            srcd[128 * p:128 * (p + 1), :]))
        prs.append((ns.rope_perm[:, :], ns.I_rope_perm))
        prs.append((ns.v_l1bq[:, :], ns.I_l1bq))
        prs.append((ns.v_l1bk[:, :], ns.I_l1bk))
        ns.l1w_ld = ctx.dgroup(g, "g", prs, "lg")

    for ck8 in range(TF // 512):
        blk = ck8 // 4
        co = (ck8 % 4) * 512

        @block.sync
        def _(q, ck8=ck8, blk=blk, co=co):
            ctx.dwait(q, "q", ns.cc_x2)
            ctx.wait(q, "q", "t", getattr(ns, "xst_free", 0))
            ns.xst_ld = ctx.dgroup(q, "q", [
                (ns.xst[:, p * 512:(p + 1) * 512],
                 ns.X2F[blk * D + 128 * p:blk * D + 128 * (p + 1),
                        co:co + 512]) for p in range(8)], "lq")

        @block.sync
        def _(q, ck8=ck8):
            ctx.wait(q, "q", "v", getattr(ns, "tab_free", 0))
            ns.tab_ld = ctx.dgroup(q, "q", [
                (ns.sinc[:, :], ns.I_ropesin[:, ck8 * 512:(ck8 + 1) * 512]),
                (ns.cosc[:, :], ns.I_ropecos[:, ck8 * 512:(ck8 + 1) * 512])],
                "lq")
        if ns.dbg_rope and ck8 == 7:
            @block.sync
            def _(q):
                ctx.dwait(q, "q", ns.tab_ld)
                ns.dbg1 = ctx.dgroup(q, "q", [
                    (ns.DBG_SC[:, 0:512], ns.sinc[:, :]),
                    (ns.DBG_SC[:, 512:1024], ns.cosc[:, :])], "sq")

        for which, wbuf, bvec in (("q", ns.wl1, ns.v_l1bq),
                                  ("k", ns.wl2, ns.v_l1bk)):
            @block.tensor
            def _(t, ck8=ck8, wbuf=wbuf):
                ctx.dwait(t, "t", ns.xst_ld)
                ctx.dwait(t, "t", ns.l1w_ld)
                ctx.wait(t, "t", "v", ctx.cnt["v"])
                for k in range(8):
                    for mt in range(4):
                        bk = bank[mt]
                        bk.mm(t, bk.h[:, 0:512],
                              wbuf[:, k * FH + mt * 128:k * FH + (mt + 1) * 128],
                              ns.xst[:, k * 512:(k + 1) * 512],
                              start=(k == 0), stop=(k == 7))
                ns.xst_free = ctx.cnt["t"]
            t_p = ctx.cnt["t"]

            @block.vector
            def _(v, ck8=ck8, which=which, bvec=bvec, t_p=t_p):
                v = ctx.ch("v", v)
                ctx.wait(v, "v", "t", t_p)
                for mt in range(4):
                    bk = bank[mt]
                    v.tensor_scalar(out=ns.qraw[:, mt * 512:(mt + 1) * 512],
                                    in0=bk.h[:, 0:512],
                                    scalar1=bvec[:, mt:mt + 1], scalar2=None,
                                    op0=ALU.add)
                    bk.rd("v")
            v_raw = ctx.cnt["v"]

            @block.tensor
            def _(t, ck8=ck8, which=which, v_raw=v_raw):
                ctx.wait(t, "t", "v", v_raw)
                for mt in range(4):
                    bk = bank[4 + mt % 2]
                    bk.mm(t, bk.h[:, 0:512], ns.rope_perm[:, :],
                          ns.qraw[:, mt * 512:(mt + 1) * 512],
                          start=True, stop=True)

                    @block.vector
                    def _(v, bk=bk, mt=mt, ck8=ck8, which=which):
                        v = ctx.ch("v", v)
                        ctx.wait(v, "v", "t", ctx.cnt["t"])
                        dst = (ns.qb2 if which == "q" else ns.kb2)[
                            :, mt * TF + ck8 * 512:mt * TF + ck8 * 512 + 512]
                        ctx.dwait(v, "v", ns.tab_ld)
                        v.tensor_tensor(out=ns.ptt[:, 0:512],
                                        in0=bk.h[:, 0:512], in1=ns.sinc[:, :],
                                        op=ALU.mult)
                        bk.rd("v")
                        v.tensor_tensor(out=ns.psq[:, 0:512],
                                        in0=ns.qraw[:, mt * 512:(mt + 1) * 512],
                                        in1=ns.cosc[:, :], op=ALU.mult)
                        v.tensor_tensor(out=dst, in0=ns.ptt[:, 0:512],
                                        in1=ns.psq[:, 0:512], op=ALU.add)
        ns.tab_free = ctx.cnt["v"]
    v_qkv = ctx.cnt["v"]
    if ns.dbg_rope:
        @block.sync
        def _(q):
            ctx.wait(q, "q", "v", ctx.cnt["v"])
            ns.dbg2 = ctx.dgroup(q, "q", [
                (ns.DBG_QK[:, 0:4 * TF], ns.qb2[:, :]),
                (ns.DBG_QK[:, 4 * TF:8 * TF], ns.kb2[:, :])], "sq")
    emit_phase_join(ns)
    es6q.close()


    es6z = ExitStack()
    sb("vtm2", (128, (TF // 128) * 520), BF16, stack=es6z)

    es6v = ExitStack()
    sb("wl3", (128, 8 * FH), BF16, stack=es6v)
    sb("onesbf", (1, 128), BF16, stack=es6v)
    sb("bvrow", (1, FH), BF16, stack=es6v)

    @block.gpsimd
    def _(g):
        prs = [(ns.wl3[:, p * FH:(p + 1) * FH],
                ns.I_l1wv[128 * p:128 * (p + 1), :]) for p in range(8)]
        prs.append((ns.bvrow[:, :], ns.I_l1bv_row))
        ns.l1v_ld = ctx.dgroup(g, "g", prs, "lg")

    @block.vector
    def _(v):
        v = ctx.ch("v", v)
        v.memset(ns.onesbf[:, :], 1.0)
        full = ns.vtm2[:, :]
        pitch = full.ap[0][0]
        for h in range(NHL):
            onescols = bass.AP(full.tensor, full.offset + 65 * h + 64,
                               [[pitch, 128], [520, TF // 128], [1, 1]])
            v.memset(onescols, 1.0)
    v_ones = ctx.cnt["v"]

    ns.xst_free = 0
    for ck8 in range(TF // 512):
        blk = ck8 // 4
        co = (ck8 % 4) * 512

        @block.sync
        def _(q, ck8=ck8, blk=blk, co=co):
            ctx.dwait(q, "q", ns.cc_x2)
            ctx.wait(q, "q", "t", ns.xst_free)
            ns.xst_ld = ctx.dgroup(q, "q", [
                (ns.xst[:, p * 512:(p + 1) * 512],
                 ns.X2F[blk * D + 128 * p:blk * D + 128 * (p + 1),
                        co:co + 512]) for p in range(8)], "lq")

        for sub in range(4):
            @block.tensor
            def _(t, ck8=ck8, sub=sub):
                ctx.dwait(t, "t", ns.xst_ld)
                ctx.dwait(t, "t", ns.l1v_ld)
                ctx.wait(t, "t", "v", max(v_ones, ctx.cnt["v"]))
                bk = bank[6 + sub % 2]
                for k in range(8):
                    bk.mm(t, bk.h[:, 0:512],
                          ns.xst[:, k * 512 + sub * 128:k * 512 + sub * 128 + 128],
                          ns.wl3[:, k * FH:(k + 1) * FH],
                          start=(k == 0), stop=False)
                bk.mm(t, bk.h[:, 0:512], ns.onesbf[:, 0:128],
                      ns.bvrow[:, :], start=False, stop=True)
                if sub == 3:
                    ns.xst_free = ctx.cnt["t"]

                @block.vector
                def _(v, bk=bk, ck8=ck8, sub=sub):
                    v = ctx.ch("v", v)
                    ctx.wait(v, "v", "t", ctx.cnt["t"])
                    tokb = ck8 * 4 + sub
                    full = ns.vtm2[:, :]
                    pitch = full.ap[0][0]
                    dst = bass.AP(full.tensor, full.offset + tokb * 520,
                                  [[pitch, 128], [65, 8], [1, 64]])
                    v.tensor_copy(out=dst,
                                  in_=bk.h[:, 0:512].rearrange(
                                      "p (h c) -> p h c", c=64))
                    bk.rd("v")
    emit_phase_join(ns)
    es6v.close()

    es6d = ExitStack()
    sb("pt", (128, 512), BF16, stack=es6d)
    sb("pt2", (128, 512), BF16, stack=es6d)
    sb("pt3", (128, 512), BF16, stack=es6d)
    sb("pt4", (128, 512), BF16, stack=es6d)
    sb("orow", (1, 512), F32, stack=es6d)
    sb("ost", (128, 512), BF16, stack=es6d)
    sb("expb", (128, 1), F32, stack=es6d)
    sb("cmask", (128, 4 * 512), BF16, stack=es6d)

    @block.vector
    def _(v):
        v = ctx.ch("v", v)
        v.memset(ns.expb[:, :], -30.0)
    v_expb = ctx.cnt["v"]

    @block.sync
    def _(q):
        ns.cm_ld = ctx.dgroup(q, "q", [(ns.cmask[:, :], ns.I_cmask)], "lq")
    ns.ptfree = {0: 0, 1: 0, 2: 0, 3: 0}

    PTS = (None,)
    for h in range(NHL):
        mt, hr = h // 2, 64 * (h % 2)
        for qt in range(TF // 512):
            nkc = 4 * (qt + 1)
            ob = bank[0] if (h * 8 + qt) % 2 == 0 else bank[4]
            gate = {}
            for kc in range(nkc + 3):
                if kc < nkc:
                    sbnk = bank[1 + kc % 3]
                    ptb = (ns.pt, ns.pt2, ns.pt3, ns.pt4)[kc % 4]

                    @block.tensor
                    def _(t, h=h, qt=qt, kc=kc, sbnk=sbnk, mt=mt, hr=hr):
                        ctx.wait(t, "t", "v", v_qkv)
                        sbnk.mm(t, sbnk.h[:, 0:512],
                                ns.kb2[hr:hr + 64,
                                       mt * TF + kc * 128:
                                       mt * TF + (kc + 1) * 128],
                                ns.qb2[hr:hr + 64,
                                       mt * TF + qt * 512:
                                       mt * TF + (qt + 1) * 512],
                                start=True, stop=True)
                    t_sc = ctx.cnt["t"]

                    @block.scalar
                    def _(sc, kc=kc, sbnk=sbnk, t_sc=t_sc, ptb=ptb):
                        sc = ctx.ch("s", sc)
                        ctx.wait(sc, "s", "t", t_sc)
                        ctx.wait(sc, "s", "t", ns.ptfree[kc % 4])
                        ctx.wait(sc, "s", "v", v_expb)
                        sc.activation(out=ptb[:, :], in_=sbnk.h[:, 0:512],
                                      func=AF.Exp, scale=0.125,
                                      bias=ns.expb[:, 0:1])
                        sbnk.rd("s")
                    s_exp = ctx.cnt["s"]

                    if kc >= 4 * qt:
                        @block.vector
                        def _(v, qt=qt, kc=kc, s_exp=s_exp, ptb=ptb):
                            v = ctx.ch("v", v)
                            ctx.wait(v, "v", "s", s_exp)
                            ctx.dwait(v, "v", ns.cm_ld)
                            j = kc - 4 * qt
                            v.tensor_tensor(
                                out=ptb[:, :], in0=ptb[:, :],
                                in1=ns.cmask[:, j * 512:(j + 1) * 512],
                                op=ALU.mult)
                        gate[kc] = ("v", ctx.cnt["v"])
                    else:
                        gate[kc] = ("s", s_exp)

                pv = kc - 3
                if pv >= 0:
                    ptb_pv = (ns.pt, ns.pt2, ns.pt3, ns.pt4)[pv % 4]

                    @block.tensor
                    def _(t, h=h, pv=pv, nkc=nkc, ob=ob, ptb_pv=ptb_pv,
                          g8=gate[pv]):
                        ctx.wait(t, "t", g8[0], g8[1])
                        ob.mm(t, ob.h[0:65, 0:512],
                              ns.vtm2[:, pv * 520 + 65 * h:
                                      pv * 520 + 65 * h + 65],
                              ptb_pv[:, :], start=(pv == 0),
                              stop=(pv == nkc - 1))
                        ns.ptfree[pv % 4] = ctx.cnt["t"]

            @block.vector
            def _(v, h=h, qt=qt, ob=ob):
                v = ctx.ch("v", v)
                ctx.wait(v, "v", "t", ctx.cnt["t"])
                v.reciprocal(out=ns.orow[:, :], in_=ob.h[64:65, 0:512])
            v_r = ctx.cnt["v"]

            @block.tensor
            def _(t, h=h, qt=qt, v_r=v_r):
                ctx.wait(t, "t", "v", v_r)
                bank[5].mm(t, bank[5].h[0:64, 0:512], ns.onesc[:, 0:64],
                           ns.orow[:, :], start=True, stop=True)
            t_bc = ctx.cnt["t"]

            @block.scalar
            def _(sc, h=h, qt=qt, t_bc=t_bc):
                sc = ctx.ch("s", sc)
                ctx.wait(sc, "s", "t", t_bc)
                sc.copy(out=ns.ptt[0:64, 0:512], in_=bank[5].h[0:64, 0:512])
                bank[5].rd("s")
            s_bc = ctx.cnt["s"]

            @block.vector
            def _(v, h=h, qt=qt, ob=ob, s_bc=s_bc):
                v = ctx.ch("v", v)
                ctx.wait(v, "v", "s", s_bc)
                ctx.dwait(v, "v", getattr(ns, "oat_st", None))
                v.tensor_tensor(out=ns.ost[0:64, :], in0=ob.h[0:64, 0:512],
                                in1=ns.ptt[0:64, 0:512], op=ALU.mult)
                ob.rd("v")
            v_o = ctx.cnt["v"]

            @block.sync
            def _(q, h=h, qt=qt, v_o=v_o):
                ctx.wait(q, "q", "v", v_o)
                ns.oat_st = ctx.dgroup(q, "q", [
                    (ns.OAT_B[64 * h:64 * h + 64, qt * 512:(qt + 1) * 512],
                     ns.ost[0:64, :])], "sq")

    emit_phase_join(ns)
    es6d.close()
    es6z.close()
    es6.close()

    @block.gpsimd
    def _(g):
        for nm, val in ctx.dcnt.items():
            ctx.wait(g, "g", nm, val)
        ns.cc_oat = ctx.ccg(g, "AllGather", [ns.OAT_B], [ns.OAT_F], ns.groups)

    es7 = ExitStack()
    sb("wob2", (128, 8 * D), BF16, stack=es7)
    sb("v_l1bo", (128, 8), F32, stack=es7)
    sb("ogh0b", (128, 8 * 512), BF16, stack=es7)
    sb("ogh1b", (128, 8 * 512), BF16, stack=es7)
    sb("oggb", (128, 8 * 512), BF16, stack=es7)

    @block.gpsimd
    def _(g):
        ns.wo2_ld = ctx.dgroup(g, "g", [
            (ns.wob2[:, p * D:(p + 1) * D],
             ns.I_l1wo[128 * p:128 * (p + 1), :]) for p in range(8)] +
            [(ns.v_l1bo[:, :], ns.I_l1bo)], "lg")

    for ck in range(NCK):
        @block.sync
        def _(q, ck=ck):
            ctx.dwait(q, "q", ns.cc_oat)
            ctx.wait(q, "q", "v", getattr(ns, "og2_free_v", 0))
            prs = []
            for p in range(8):
                prs.append((ns.ogh0b[:, p * 512:(p + 1) * 512],
                            ns.OAT_F[128 * p:128 * (p + 1),
                                     ck * 512:ck * 512 + 512]))
                prs.append((ns.ogh1b[:, p * 512:(p + 1) * 512],
                            ns.OAT_F[128 * p:128 * (p + 1),
                                     TO + ck * 512:TO + ck * 512 + 512]))
            ns.og2_ld = ctx.dgroup(q, "q", prs, "lq")

        @block.vector
        def _(v, ck=ck):
            v = ctx.ch("v", v)
            ctx.dwait(v, "v", ns.og2_ld)
            ctx.wait(v, "v", "t", getattr(ns, "og2_free", 0))
            for p in range(8):
                v.tensor_scalar(out=ns.oggb[:, p * 512:(p + 1) * 512],
                                in0=ns.ogh0b[:, p * 512:(p + 1) * 512],
                                scalar1=ns.mab[:, 0:1], scalar2=None,
                                op0=ALU.mult)
                v.scalar_tensor_tensor(out=ns.oggb[:, p * 512:(p + 1) * 512],
                                       in0=ns.ogh1b[:, p * 512:(p + 1) * 512],
                                       scalar=ns.mab[:, 1:2],
                                       in1=ns.oggb[:, p * 512:(p + 1) * 512],
                                       op0=ALU.mult, op1=ALU.add)
            ns.og2_free_v = ctx.cnt["v"]
        v_og2 = ctx.cnt["v"]

        for half in range(2):
            @block.tensor
            def _(t, ck=ck, half=half, v_og2=v_og2):
                ctx.wait(t, "t", "v", v_og2)
                ctx.dwait(t, "t", ns.wo2_ld)
                for k in range(8):
                    for mi in range(4):
                        mt2 = half * 4 + mi
                        bk = bank[mi]
                        bk.mm(t, bk.h[:, 0:512],
                              ns.wob2[:, k * D + mt2 * 128:k * D + (mt2 + 1) * 128],
                              ns.oggb[:, k * 512:(k + 1) * 512],
                              start=(k == 0), stop=(k == 7))
                ns.og2_free = ctx.cnt["t"]

            @block.vector
            def _(v, ck=ck, half=half):
                v = ctx.ch("v", v)
                ctx.wait(v, "v", "t", ctx.cnt["t"])
                for mi in range(4):
                    mt2 = half * 4 + mi
                    bk = bank[mi]
                    co = mt2 * TO + ck * 512
                    v.tensor_scalar(out=ns.ptt[:, 0:512], in0=bk.h[:, 0:512],
                                    scalar1=ns.v_l1bo[:, mt2:mt2 + 1],
                                    scalar2=None, op0=ALU.add)
                    bk.rd("v")
                    v.tensor_tensor(out=ns.xa[:, co:co + 512],
                                    in0=ns.ptt[:, 0:512],
                                    in1=ns.xa[:, co:co + 512], op=ALU.add)
        emit_colstats(ns, ns.xa, ck * 512, 512, "ln", EPS_LN, bufW=TO)
    emit_phase_join(ns)
    es7.close()
    ns.x2buf = ns.xa



def emit_moe(ns):
    ctx, block, bank, nc, sb = ns.ctx, ns.block, ns.bank, ns.nc, ns.sb
    NCK = TO // 512
    es8 = ns.es.enter_context(ExitStack())
    sb("acc", (128, 8 * TO), F32, stack=es8)
    sb("x3b", (128, 8 * TO), BF16, stack=es8)
    sb("cfs", (16, TO), BF16, stack=es8)
    sb("sqeps", (16, 1), F32, stack=es8)
    sb("inde16", (16, 16 * 128), BF16, stack=es8)
    es8r = ExitStack()
    sb("gwt", (128, 8 * 16), F32, stack=es8r)
    sb("cmb", (128, 16 * 16), F32, stack=es8r)
    sb("bebc", (128, 16), F32, stack=es8r)
    sb("mrow", (128, 2), F32, stack=es8r)
    sb("rsc1", (128, 16), F32, stack=es8r)
    sb("rsc2", (128, 16), F32, stack=es8r)
    sb("identf", (128, 128), F32, stack=es8r)
    sb("berow", (1, 16), F32, stack=es8r)

    @block.vector
    def _(v):
        v = ctx.ch("v", v)
        v.memset(ns.sqeps[:, :], 1e-30)
        for p in range(8):
            v.tensor_copy(out=ns.x3b[:, p * TO:(p + 1) * TO],
                          in_=ns.xa[:, p * TO:(p + 1) * TO])
    v_x3b = ctx.cnt["v"]

    @block.sync
    def _(q):
        ns.gw_ld = ctx.dgroup(q, "q", [
            (ns.gwt[:, p * 16:(p + 1) * 16],
             ns.I_gate_wt[128 * p:128 * (p + 1), :]) for p in range(8)] +
            [(ns.identf[:, :], ns.I_identf), (ns.inde16[:, :], ns.I_inde16),
             (ns.berow[:, :], ns.I_bias_e)], "lq")

    @block.tensor
    def _(t):
        ctx.dwait(t, "t", ns.gw_ld)
        bank[6].mm(t, bank[6].h[:, 0:16], ns.onesc[:, :], ns.berow[:, :],
                   start=True, stop=True)
    t_be = ctx.cnt["t"]

    @block.vector
    def _(v):
        v = ctx.ch("v", v)
        ctx.wait(v, "v", "t", t_be)
        v.tensor_copy(out=ns.bebc[:, :], in_=bank[6].h[:, 0:16])
        bank[6].rd("v")
    v_be = ctx.cnt["v"]

    for ck in range(NCK):
        @block.tensor
        def _(t, ck=ck):
            ctx.dwait(t, "t", ns.gw_ld)
            ctx.wait(t, "t", "v", ctx.cnt["v"])
            for k in range(8):
                bank[7].mm(t, bank[7].h[0:16, 0:512],
                           ns.gwt[:, k * 16:(k + 1) * 16],
                           ns.xa[:, k * TO + ck * 512:k * TO + ck * 512 + 512],
                           start=(k == 0), stop=(k == 7))
        t_lg = ctx.cnt["t"]

        @block.vector
        def _(v, ck=ck):
            v = ctx.ch("v", v)
            ctx.wait(v, "v", "t", t_lg)
            v.tensor_copy(out=ns.psq[0:16, 0:512], in_=bank[7].h[0:16, 0:512])
            bank[7].rd("v")
        v_lg = ctx.cnt["v"]

        for sub in range(4):
            ti = ck * 4 + sub

            @block.tensor
            def _(t, ck=ck, sub=sub):
                ctx.wait(t, "t", "v", v_lg)
                ctx.dwait(t, "t", ns.gw_ld)
                bk = bank[4 + sub % 2]
                bk.tr(t, bk.h[:, 0:16], ns.psq[0:16, sub * 128:(sub + 1) * 128],
                      ns.identf[0:16, 0:16])
            t_tr = ctx.cnt["t"]

            @block.scalar
            def _(sc, ti=ti):
                sc = ctx.ch("s", sc)
                ctx.wait(sc, "s", "t", t_tr)
                bk = bank[4 + ti % 2]
                sc.activation(out=ns.cmb[:, ti * 16:(ti + 1) * 16],
                              in_=bk.h[:, 0:16], func=AF.Sigmoid)
                bk.rd("s")
            s_sig = ctx.cnt["s"]

            @block.vector
            def _(v, ti=ti):
                v = ctx.ch("v", v)
                ctx.wait(v, "v", "s", s_sig)
                ctx.wait(v, "v", "v", v_be)
                sco = ns.cmb[:, ti * 16:(ti + 1) * 16]
                sc_ = ns.rsc1[:, :]
                v.tensor_tensor(out=sc_, in0=sco, in1=ns.bebc[:, :], op=ALU.add)
                g4 = lambda a, i: bass.AP(a[:, :].tensor, a[:, :].offset + i,
                                          [[a[:, :].ap[0][0], 128], [4, 4]])
                pairs = [(0, 1), (0, 2), (0, 3), (1, 2), (1, 3), (2, 3)]
                first = True
                for (i, j) in pairs:
                    v.tensor_tensor(out=ns.rsc2[:, 0:4], in0=g4(ns.rsc1, i),
                                    in1=g4(ns.rsc1, j), op=ALU.min)
                    v.tensor_tensor(out=ns.rsc2[:, 4:8], in0=g4(ns.rsc1, i),
                                    in1=g4(ns.rsc1, j), op=ALU.add)
                    if first:
                        v.tensor_copy(out=ns.rsc2[:, 8:12], in_=ns.rsc2[:, 0:4])
                        v.tensor_copy(out=ns.rsc2[:, 12:16], in_=ns.rsc2[:, 4:8])
                        first = False
                    else:
                        v.tensor_tensor(out=ns.rsc2[:, 8:12],
                                        in0=ns.rsc2[:, 8:12],
                                        in1=ns.rsc2[:, 0:4], op=ALU.max)
                        v.tensor_tensor(out=ns.rsc2[:, 12:16],
                                        in0=ns.rsc2[:, 12:16],
                                        in1=ns.rsc2[:, 4:8], op=ALU.max)
                v.tensor_reduce(out=ns.rsc2[:, 0:4],
                                in_=ns.rsc1[:, :].rearrange("p (g i) -> p g i", i=4),
                                axis=AX.X, op=ALU.max)
                gs = ns.rsc2[:, 12:16]
                first = True
                for (i, j) in pairs:
                    v.tensor_tensor(out=ns.rsc2[:, 0:1], in0=gs[:, i:i + 1],
                                    in1=gs[:, j:j + 1], op=ALU.min)
                    if first:
                        v.tensor_copy(out=ns.rsc2[:, 4:5], in_=ns.rsc2[:, 0:1])
                        first = False
                    else:
                        v.tensor_tensor(out=ns.rsc2[:, 4:5],
                                        in0=ns.rsc2[:, 4:5],
                                        in1=ns.rsc2[:, 0:1], op=ALU.max)
                v.tensor_scalar(out=ns.rsc2[:, 6:7], in0=ns.rsc2[:, 4:5],
                                scalar1=-1.0, scalar2=None, op0=ALU.mult)
                v.tensor_single_scalar(out=ns.rsc2[:, 8:12], in_=gs,
                                       scalar=ns.rsc2[:, 6:7], op=ALU.add)
                v.tensor_single_scalar(out=ns.rsc2[:, 8:12],
                                       in_=ns.rsc2[:, 8:12], scalar=0.0,
                                       op=ALU.is_ge)
                gselb = bass.AP(ns.rsc2[:, :].tensor,
                                ns.rsc2[:, :].offset + 8,
                                [[ns.rsc2[:, :].ap[0][0], 128], [1, 4], [0, 4]])
                v.tensor_tensor(out=ns.rsc1[:, :].rearrange("p (g i) -> p g i", i=4),
                                in0=ns.rsc1[:, :].rearrange("p (g i) -> p g i", i=4),
                                in1=gselb, op=ALU.mult)
                for _r in range(4):
                    v.tensor_reduce(out=ns.mrow[:, 0:1], in_=ns.rsc1[:, :],
                                    axis=AX.X, op=ALU.max)
                    v.tensor_scalar(out=ns.mrow[:, 1:2], in0=ns.mrow[:, 0:1],
                                    scalar1=-1.0, scalar2=None, op0=ALU.mult)
                    v.tensor_single_scalar(out=ns.rsc2[:, 0:16],
                                           in_=ns.rsc1[:, :],
                                           scalar=ns.mrow[:, 1:2],
                                           op=ALU.add)
                    v.tensor_single_scalar(out=ns.rsc2[:, 0:16],
                                           in_=ns.rsc2[:, 0:16], scalar=0.0,
                                           op=ALU.is_equal)
                    v.scalar_tensor_tensor(out=ns.rsc1[:, :],
                                           in0=ns.rsc2[:, 0:16],
                                           scalar=-1e30, in1=ns.rsc1[:, :],
                                           op0=ALU.mult, op1=ALU.add)
                v.tensor_single_scalar(out=ns.rsc2[:, 0:16], in_=ns.rsc1[:, :],
                                       scalar=-1e29, op=ALU.is_lt)
                v.tensor_tensor(out=sco, in0=sco, in1=ns.rsc2[:, 0:16],
                                op=ALU.mult)
                v.tensor_reduce(out=ns.mrow[:, 0:1], in_=sco, axis=AX.X,
                                op=ALU.add)
                v.tensor_scalar(out=ns.mrow[:, 0:1], in0=ns.mrow[:, 0:1],
                                scalar1=1e-20, scalar2=None, op0=ALU.add)
                v.reciprocal(out=ns.mrow[:, 0:1], in_=ns.mrow[:, 0:1])
                v.tensor_scalar(out=ns.mrow[:, 0:1], in0=ns.mrow[:, 0:1],
                                scalar1=2.5, scalar2=None, op0=ALU.mult)
                v.tensor_single_scalar(out=sco, in_=sco,
                                       scalar=ns.mrow[:, 0:1], op=ALU.mult)
    v_cmb = ctx.cnt["v"]

    for ti in range(16):
        @block.tensor
        def _(t, ti=ti):
            ctx.wait(t, "t", "v", v_cmb)
            ctx.dwait(t, "t", ns.gw_ld)
            bk = bank[4 + ti % 2]
            bk.tr(t, bk.h[0:16, 0:128], ns.cmb[:, ti * 16:(ti + 1) * 16],
                  ns.identf[:, :])

            @block.scalar
            def _(sc, bk=bk, ti=ti):
                sc = ctx.ch("s", sc)
                ctx.wait(sc, "s", "t", ctx.cnt["t"])
                sc.activation(out=ns.cfs[:, ti * 128:(ti + 1) * 128],
                              in_=bk.h[0:16, 0:128], func=AF.Sqrt,
                              bias=ns.sqeps[:, 0:1])
                bk.rd("s")
    s_cfs = ctx.cnt["s"]

    emit_phase_join(ns)
    es8r.close()

    es9 = ExitStack()
    sb("wsh1", (128, 8 * 512), BF16, stack=es9)
    sb("wsh2", (128, 4 * 1024), BF16, stack=es9)
    sb("hsh", (128, 4 * 512), BF16, stack=es9)
    ns.wsh1_free = 0
    ns.wsh2_free = 0
    for qsh in range(4):
        @block.gpsimd
        def _(g, qsh=qsh):
            g = ctx.ch("g", g)
            ctx.wait(g, "g", "t", ns.wsh1_free)
            ns.wsh1_ld = ctx.dgroup(g, "g", [
                (ns.wsh1[:, p * 512:(p + 1) * 512],
                 ns.I_sh_upt[128 * p:128 * (p + 1),
                             qsh * 512:(qsh + 1) * 512])
                for p in range(8)], "lg")
            ctx.wait(g, "g", "t", ns.wsh2_free)
            ns.wsh2_ld = ctx.dgroup(g, "g", [
                (ns.wsh2[:, kk * 1024:(kk + 1) * 1024],
                 ns.I_sh_dnt[qsh * 512 + 128 * kk:
                             qsh * 512 + 128 * (kk + 1), :])
                for kk in range(4)], "lg")

        for ck in range(NCK):
            tco = ck * 512

            @block.tensor
            def _(t, qsh=qsh, tco=tco, ck=ck):
                ctx.wait(t, "t", "v", v_x3b)
                ctx.dwait(t, "t", ns.wsh1_ld)
                for k in range(8):
                    for mi in range(4):
                        bk = bank[4 + mi]
                        bk.mm(t, bk.h[:, 0:512],
                              ns.wsh1[:, k * 512 + mi * 128:
                                      k * 512 + (mi + 1) * 128],
                              ns.x3b[:, k * TO + tco:k * TO + tco + 512],
                              start=(k == 0), stop=(k == 7))
                if ck == NCK - 1:
                    ns.wsh1_free = ctx.cnt["t"]
            t_up = ctx.cnt["t"]

            v_hsh_mi = []

            @block.vector
            def _(v, t_up=t_up):
                v = ctx.ch("v", v)
                ctx.wait(v, "v", "t", t_up)
                for mi in range(4):
                    bk = bank[4 + mi]
                    v.tensor_scalar_max(out=ns.psq[:, 0:512],
                                        in0=bk.h[:, 0:512], scalar1=0.0)
                    bk.rd("v")
                    v.tensor_tensor(out=ns.hsh[:, mi * 512:(mi + 1) * 512],
                                    in0=ns.psq[:, 0:512], in1=ns.psq[:, 0:512],
                                    op=ALU.mult)
                    v_hsh_mi.append(ctx.cnt["v"])

            for half in range(2):
                @block.tensor
                def _(t, qsh=qsh, tco=tco, half=half, v_hsh_mi=v_hsh_mi):
                    ctx.dwait(t, "t", ns.wsh2_ld)
                    for kk in range(4):
                        ctx.wait(t, "t", "v", v_hsh_mi[kk])
                        for mi in range(4):
                            mt = half * 4 + mi
                            bk = bank[mi]
                            bk.mm(t, bk.h[:, 0:512],
                                  ns.wsh2[:, kk * 1024 + mt * 128:
                                          kk * 1024 + (mt + 1) * 128],
                                  ns.hsh[:, kk * 512:(kk + 1) * 512],
                                  start=(kk == 0), stop=(kk == 3))
                    ns.wsh2_free = ctx.cnt["t"]

                @block.vector
                def _(v, qsh=qsh, tco=tco, half=half):
                    v = ctx.ch("v", v)
                    ctx.wait(v, "v", "t", ctx.cnt["t"])
                    for mi in range(4):
                        mt = half * 4 + mi
                        bk = bank[mi]
                        co = mt * TO + tco
                        if qsh == 0:
                            v.tensor_copy(out=ns.acc[:, co:co + 512],
                                          in_=bk.h[:, 0:512])
                        else:
                            v.tensor_tensor(out=ns.acc[:, co:co + 512],
                                            in0=bk.h[:, 0:512],
                                            in1=ns.acc[:, co:co + 512],
                                            op=ALU.add)
                        bk.rd("v")
    emit_phase_join(ns)
    es9.close()

    es10 = ExitStack()
    sb("wme1", (128, 8 * 512), BF16, stack=es10)
    sb("wme2", (128, 4 * 1024), BF16, stack=es10)
    sb("hme", (128, 4 * 512), BF16, stack=es10)
    ns.wme1_free = 0
    ns.wme2_free = 0
    for e in range(MOE_NE):
        @block.gpsimd
        def _(g, e=e):
            g = ctx.ch("g", g)
            ctx.wait(g, "g", "t", ns.wme1_free)
            ns.wme1_ld = ctx.dgroup(g, "g", [
                (ns.wme1[:, p * 512:(p + 1) * 512],
                 ns.I_moe_up[e * D + 128 * p:e * D + 128 * (p + 1), :])
                for p in range(8)], "lg")
            ctx.wait(g, "g", "t", ns.wme2_free)
            ns.wme2_ld = ctx.dgroup(g, "g", [
                (ns.wme2[:, kk * D:(kk + 1) * D],
                 ns.I_moe_dn[e * 512 + 128 * kk:
                             e * 512 + 128 * (kk + 1), :])
                for kk in range(4)], "lg")

        for ck in range(NCK):
            tco = ck * 512

            @block.tensor
            def _(t, e=e, tco=tco):
                ctx.wait(t, "t", "s", s_cfs)
                ctx.wait(t, "t", "v", ctx.cnt["v"])
                ctx.wait(t, "t", "s", ctx.cnt["s"])
                bank[0].mm(t, bank[0].h[:, 0:512],
                           ns.inde16[:, e * 128:(e + 1) * 128],
                           ns.cfs[0:16, tco:tco + 512],
                           start=True, stop=True)

            @block.scalar
            def _(sc, e=e):
                sc = ctx.ch("s", sc)
                ctx.wait(sc, "s", "t", ctx.cnt["t"])
                sc.copy(out=ns.ptt[:, 0:512], in_=bank[0].h[:, 0:512])
                bank[0].rd("s")
            s_cbc = ctx.cnt["s"]

            @block.tensor
            def _(t, tco=tco, ck=ck):
                ctx.wait(t, "t", "v", v_x3b)
                ctx.dwait(t, "t", ns.wme1_ld)
                for k in range(8):
                    for mi in range(4):
                        bk = bank[4 + mi]
                        bk.mm(t, bk.h[:, 0:512],
                              ns.wme1[:, k * 512 + mi * 128:
                                      k * 512 + (mi + 1) * 128],
                              ns.x3b[:, k * TO + tco:k * TO + tco + 512],
                              start=(k == 0), stop=(k == 7))
                if ck == NCK - 1:
                    ns.wme1_free = ctx.cnt["t"]
            t_up = ctx.cnt["t"]

            v_hme_mi = []

            @block.vector
            def _(v, s_cbc=s_cbc, t_up=t_up):
                v = ctx.ch("v", v)
                ctx.wait(v, "v", "t", t_up)
                ctx.wait(v, "v", "s", s_cbc)
                for mi in range(4):
                    bk = bank[4 + mi]
                    v.scalar_tensor_tensor(out=ns.psq[:, 0:512],
                                           in0=bk.h[:, 0:512],
                                           scalar=0.0,
                                           in1=ns.ptt[:, 0:512],
                                           op0=ALU.max, op1=ALU.mult)
                    bk.rd("v")
                    v.tensor_tensor(out=ns.hme[:, mi * 512:(mi + 1) * 512],
                                    in0=ns.psq[:, 0:512],
                                    in1=ns.psq[:, 0:512], op=ALU.mult)
                    v_hme_mi.append(ctx.cnt["v"])

            for half in range(2):
                @block.tensor
                def _(t, half=half, v_hme_mi=v_hme_mi):
                    ctx.dwait(t, "t", ns.wme2_ld)
                    for kk in range(4):
                        ctx.wait(t, "t", "v", v_hme_mi[kk])
                        for mi in range(4):
                            mt = half * 4 + mi
                            bk = bank[mi]
                            bk.mm(t, bk.h[:, 0:512],
                                  ns.wme2[:, kk * D + mt * 128:
                                          kk * D + (mt + 1) * 128],
                                  ns.hme[:, kk * 512:(kk + 1) * 512],
                                  start=(kk == 0), stop=(kk == 3))
                    ns.wme2_free = ctx.cnt["t"]

                @block.vector
                def _(v, half=half, tco=tco):
                    v = ctx.ch("v", v)
                    ctx.wait(v, "v", "t", ctx.cnt["t"])
                    for mi in range(4):
                        mt = half * 4 + mi
                        bk = bank[mi]
                        co = mt * TO + tco
                        v.tensor_tensor(out=ns.acc[:, co:co + 512],
                                        in0=bk.h[:, 0:512],
                                        in1=ns.acc[:, co:co + 512],
                                        op=ALU.add)
                        bk.rd("v")
    emit_phase_join(ns)
    es10.close()

    for ck in range(NCK):
        @block.vector
        def _(v, ck=ck):
            v = ctx.ch("v", v)
            for mt in range(8):
                co = mt * TO + ck * 512
                v.tensor_tensor(out=ns.acc[:, co:co + 512],
                                in0=ns.xa[:, co:co + 512],
                                in1=ns.acc[:, co:co + 512], op=ALU.add)
        emit_colstats(ns, ns.acc, ck * 512, 512, "ln", EPS_LN, bufW=TO)
    ns.x2buf = ns.acc
